# version 15
# baseline (speedup 1.0000x reference)
import numpy as np
import concourse.bass as bass
import concourse.mybir as mybir
from concourse.bass_utils import run_bass_kernel_spmd
from contextlib import ExitStack


F32 = mybir.dt.float32
AF = mybir.ActivationFunctionType
ALU = mybir.AluOpType
AX = mybir.AxisListType

NCORES = 8
B_CORE = 2
N = 2048
C = 1024
H = 16
DH = 64
IF = 2816
NTC = N // 128
KC = C // 128
FFN_PASSES = [(i * 256, 256) for i in range(11)]

ENGS = ("sync", "gpsimd", "scalar", "vector", "tensor")


class Sched:
    def __init__(self):
        self.items = {e: [] for e in ENGS}
        self.cnt = {s: 0 for s in
                    ("pl0", "pl1", "xl0", "xl1", "lw", "lu", "lb", "ld2",
                     "st0", "st1", "st2", "st3", "so0", "so1",
                     "gx", "ga", "gb", "pe", "vv", "ss", "gg")}

    def add(self, eng, fn, waits=(), inc=None, inc_by=1, drain_after=False):
        self.items[eng].append((tuple(waits), fn, inc, inc_by, drain_after))
        if inc is not None:
            self.cnt[inc] += inc_by
            return self.cnt[inc]
        return None

    def val(self, sem):
        return self.cnt[sem]


def build_core_program(stage=7, nbatch=B_CORE, skip=(), debug=False):
    nc = bass.Bass("TRN2", target_bir_lowering=False,
                   detect_race_conditions=False)

    x_in = nc.dram_tensor("x", [B_CORE, N, C], F32, kind="ExternalInput").ap()
    wqkv = nc.dram_tensor("wqkv", [3 * C, C], F32, kind="ExternalInput").ap()
    wproj = nc.dram_tensor("wproj", [C, C], F32, kind="ExternalInput").ap()
    bproj = nc.dram_tensor("bproj", [C], F32, kind="ExternalInput").ap()
    wg_in = nc.dram_tensor("wg", [IF, C], F32, kind="ExternalInput").ap()
    wu_in = nc.dram_tensor("wu", [IF, C], F32, kind="ExternalInput").ap()
    wd_in = nc.dram_tensor("wd", [C, IF], F32, kind="ExternalInput").ap()
    out = nc.dram_tensor("out", [B_CORE, N, C], F32, kind="ExternalOutput").ap()
    s_qkvT = nc.dram_tensor("s_qkvT", [C, 3 * C], F32, kind="Internal").ap()
    s_projT = nc.dram_tensor("s_projT", [C, C], F32, kind="Internal").ap()
    s_gateT = nc.dram_tensor("s_gateT", [C, IF], F32, kind="Internal").ap()
    s_upT = nc.dram_tensor("s_upT", [C, IF], F32, kind="Internal").ap()
    s_downT = nc.dram_tensor("s_downT", [IF, C], F32, kind="Internal").ap()
    if debug:
        d_ctx = nc.dram_tensor("d_ctx", [128, 512], F32, kind="ExternalOutput").ap()
        d_rs = nc.dram_tensor("d_rs", [128, 8], F32, kind="ExternalOutput").ap()
        d_G = nc.dram_tensor("d_G", [128, 8192], F32, kind="ExternalOutput").ap()
        d_kfm = nc.dram_tensor("d_kfm", [128, 2048], F32, kind="ExternalOutput").ap()
        d_xnf = nc.dram_tensor("d_xnf", [128, 2048], F32, kind="ExternalOutput").ap()
        d_xq = nc.dram_tensor("d_xq", [128, 1024], F32, kind="ExternalOutput").ap()
        d_stat = nc.dram_tensor("d_stat", [128, 32], F32, kind="ExternalOutput").ap()
        d_qu = nc.dram_tensor("d_qu", [128, 512], F32, kind="ExternalOutput").ap()
        d_qn = nc.dram_tensor("d_qn", [128, 512], F32, kind="ExternalOutput").ap()
        d_qsg = nc.dram_tensor("d_qsg", [128, 512], F32, kind="ExternalOutput").ap()
        d_qsc = nc.dram_tensor("d_qsc", [128, 96], F32, kind="ExternalOutput").ap()
        d_xn0 = nc.dram_tensor("d_xn0", [128, 1024], F32, kind="ExternalOutput").ap()
        d_rn = [nc.dram_tensor(f"d_rn{i}", [128, 1], F32, kind="ExternalOutput").ap()
                for i in range(4)]

    S = Sched()

    with ExitStack() as stk:
        def sbt(name, shape, dt=F32):
            return stk.enter_context(nc.sbuf_tensor(name, shape, dt))

        XNF = sbt("XNF", [128, 16384])
        XQF = sbt("XQF", [128, 16384])
        WBIG = sbt("WBIG", [128, 8192])
        xin = sbt("xin", [128, 2048])
        tA = sbt("tA", [128, 1024])
        tB = sbt("tB", [128, 1024])
        qu = sbt("qu", [128, 384])
        qn = sbt("qn", [128, 384])
        qm = sbt("qm", [128, 384])
        qsg = sbt("qsg", [128, 384])
        hg = sbt("hg", [128, 384])
        hh = sbt("hh", [128, 384])
        hqt = sbt("hqt", [128, 384])
        hqT = sbt("hqT", [128, 384])
        wstr = sbt("wstr", [128, 1024])
        outst = sbt("outst", [128, 2048])
        ctxsb = outst
        stat = sbt("stat", [128, 64])
        rsdP = [sbt("rsd0", [128, 1]), sbt("rsd1", [128, 1])]
        nbP = [sbt("nb0", [128, 1]), sbt("nb1", [128, 1])]
        qsc = sbt("qsc", [128, 128])
        rs_all = sbt("rs_all", [128, 8])
        zeros1 = sbt("zeros1", [128, 1])
        zrow = sbt("zrow", [1, 128])
        ident = sbt("ident", [128, 128])
        ones1 = sbt("ones1", [1, 128])
        brow = sbt("brow", [1, 1024])
        P = [stk.enter_context(nc.psum_tensor(f"P{i}", [128, 512], F32))
             for i in range(8)]
        sem = {s: stk.enter_context(nc.semaphore(s))
               for s in ("pl0", "pl1", "xl0", "xl1", "lw", "lu", "lb", "ld2",
                         "st0", "st1", "st2", "st3", "so0", "so1",
                         "gx", "ga", "gb", "pe", "vv", "ss", "gg")}
        block = stk.enter_context(nc.Block())

        x3 = lambda ap: ap.rearrange("p (b k) -> p b k", k=16)

        def bc16(ap, nb):
            return ap.unsqueeze(2).broadcast_to([128, nb, 16])

        def ld_dma(semn, dst, src, waits=()):
            return S.add("sync", lambda e, d=dst, s=src: e.dma_start(out=d, in_=s),
                         waits=waits, inc=semn, inc_by=16)

        def st_dma(semn, dst, src, waits=()):
            return S.add("sync", lambda e, d=dst, s=src: e.dma_start(out=d, in_=s),
                         waits=waits, inc=semn, inc_by=16)

        def ld_dma_g(semn, dst, src, waits=()):
            return S.add("gpsimd",
                         lambda e, d=dst, s=src: e.dma_start(out=d, in_=s),
                         waits=waits, inc=semn, inc_by=16)

        def gd_accum(semn, dst, src, waits=()):
            return S.add("gpsimd",
                         lambda e, d=dst, s=src: e.dma_start(out=d, in_=s,
                                                             accum_op=ALU.add),
                         waits=waits, inc=semn, inc_by=16)

        def v_op(fn, waits=(), inc=True, drain=False):
            return S.add("vector", fn, waits=waits, inc="vv" if inc else None,
                         drain_after=drain)

        def s_op(fn, waits=(), inc=True):
            return S.add("scalar", fn, waits=waits, inc="ss" if inc else None)

        qsg_busy = [0]

        def emit_quant(src, dst, w, src_ready, dst_free=(), bufs=None):
            nb = w // 16
            bm = qsc[:, 0:nb]
            bm64, rc64, s264 = qsc[:, 0:64], qsc[:, 64:128], qsc[:, 0:64]
            bu, bg, bm2, bsg = bufs if bufs is not None else (qu, qn, qm, qsg)
            sg = bsg[:, 0:w]
            u = bu[:, 0:w]
            gg_ = bg[:, 0:w]
            xv = lambda ap: ap.rearrange("p (b k) -> p b k", k=16)
            ss_sign = s_op(lambda e, s=src, sg=sg: e.sign(out=sg, in_=s),
                           waits=list(src_ready) + [("vv", qsg_busy[0])])
            v_op(lambda e, s=src, bm=bm: e.reduce_max(
                out=bm, in_=xv(s), axis=AX.X, apply_absolute_value=True),
                waits=src_ready, inc=False, drain=True)
            v_op(lambda e: e.reciprocal(out=rc64, in_=bm64), inc=False, drain=True)
            v_op(lambda e: e.tensor_scalar(out=rc64, in0=rc64, scalar1=12.0,
                                           scalar2=2e12, op0=ALU.mult,
                                           op1=ALU.min), inc=False, drain=True)
            v_op(lambda e: e.reciprocal(out=s264, in_=rc64), inc=False, drain=True)
            v_op(lambda e, s=src, u=u: e.scalar_tensor_tensor(
                out=u, in0=s, scalar=-1.0, in1=s, op0=ALU.mult, op1=ALU.max),
                inc=False)
            v_op(lambda e, u=u, nb=nb: e.tensor_tensor(
                out=xv(u), in0=xv(u), in1=bc16(rc64[:, 0:nb], nb), op=ALU.mult),
                inc=False)
            m2 = bm2[:, 0:w]
            v_op(lambda e, u=u, gg_=gg_: e.tensor_scalar(
                out=gg_, in0=u, scalar1=0.5, scalar2=None, op0=ALU.is_gt),
                inc=False)
            for th in (1.5, 2.5, 3.5):
                v_op(lambda e, th=th, u=u, gg_=gg_: e.scalar_tensor_tensor(
                    out=gg_, in0=u, scalar=th, in1=gg_, op0=ALU.is_gt,
                    op1=ALU.add), inc=False)
            v_op(lambda e, u=u, m2=m2: e.tensor_scalar(
                out=m2, in0=u, scalar1=5.0, scalar2=None, op0=ALU.is_gt),
                inc=False)
            v_op(lambda e, u=u, m2=m2: e.scalar_tensor_tensor(
                out=m2, in0=u, scalar=7.0, in1=m2, op0=ALU.is_gt, op1=ALU.add),
                inc=False)
            v_op(lambda e, m2=m2, gg_=gg_: e.scalar_tensor_tensor(
                out=gg_, in0=m2, scalar=2.0, in1=gg_, op0=ALU.mult, op1=ALU.add),
                inc=False)
            v_op(lambda e, u=u, m2=m2: e.tensor_scalar(
                out=m2, in0=u, scalar1=10.0, scalar2=4.0, op0=ALU.is_gt,
                op1=ALU.mult), inc=False)
            v_op(lambda e, m2=m2, gg_=gg_: e.tensor_tensor(
                out=gg_, in0=gg_, in1=m2, op=ALU.add), inc=False)
            v_op(lambda e, sg=sg, nb=nb: e.scalar_tensor_tensor(
                out=xv(sg), in0=xv(sg), scalar=0.0, in1=bc16(s264[:, 0:nb], nb),
                op0=ALU.add, op1=ALU.mult),
                waits=[("ss", ss_sign)], inc=False)
            vv_fin = v_op(lambda e, d=dst, gg_=gg_, sg=sg: e.tensor_tensor(
                out=d, in0=gg_, in1=sg, op=ALU.mult), waits=dst_free)
            qsg_busy[0] = vv_fin
            return vv_fin, ss_sign

        def emit_quant_multi(src, dst, cols, src_ready, dst_free=(),
                             bufs=None, maxw=512):
            off, last = 0, (0, 0)
            while off < cols:
                w = min(maxw, cols - off)
                last = emit_quant(src[:, off:off + w], dst[:, off:off + w], w,
                                  src_ready, dst_free, bufs=bufs)
                off += w
            return last

        gg_ms = S.add("gpsimd", lambda e: e.memset(ident[:], 0.0), inc="gg")
        gg_id = S.add("gpsimd", lambda e: e.affine_select(
            out=ident[:], in_=ident[:], compare_op=ALU.not_equal, fill=1.0,
            base=0, pattern=[[-1, 128]], channel_multiplier=1),
            waits=[("gg", gg_ms)], inc="gg")
        gg_ones = S.add("gpsimd", lambda e: e.memset(ones1[:], 1.0), inc="gg")
        gg_zrow = S.add("gpsimd", lambda e: e.memset(zrow[:], 0.0), inc="gg")
        ld_brow = ld_dma("lb", brow[0:1, :], bproj.rearrange("(a c) -> a c", a=1))
        v_zero = v_op(lambda e: e.memset(zeros1[:], 0.0))
        v_op(lambda e: e.memset(qsc[:], 1.0), drain=True)

        rawsl = [XNF[:, 0:2816], XNF[:, 2816:5632]]
        wqsl = [XNF[:, 5632:8448], XNF[:, 8448:11264]]
        evsl = [XNF[:, 11264:11776], XNF[:, 11776:12288],
                XNF[:, 12288:12800], XNF[:, 12800:13312]]
        prep_bufs = (XNF[:, 13312:14336], XNF[:, 14336:15360],
                     XNF[:, 15360:16384], XQF[:, 0:1024])
        ev_free = [("st0", 0), ("st1", 0), ("st2", 0), ("st3", 0)]
        bank_ev = [0, 0]
        slot_free = [[], []]

        def prep_slab(src_hbm, rows, cols, dst_hbm, quant):
            nonlocal ev_free, bank_ev, slot_free
            nchunk = rows // 128
            ncc = cols // 128
            ei = 0
            for ch in range(nchunk):
                sl = ch % 2
                raw = rawsl[sl][:, 0:cols]
                hc = cols // 2
                ld_r = 0
                for qh in range(2):
                    ld_r = ld_dma_g(
                        f"pl{sl}", raw[:, qh * hc:(qh + 1) * hc],
                        src_hbm[ch * 128:(ch + 1) * 128, qh * hc:(qh + 1) * hc],
                        waits=slot_free[sl] if qh == 0 else [])
                if quant:
                    wqt = wqsl[sl][:, 0:cols]
                    (qdone, _) = emit_quant_multi(
                        raw, wqt, cols, src_ready=[(f"pl{sl}", ld_r)],
                        dst_free=[("pe", S.val("pe"))],
                        bufs=prep_bufs, maxw=1024)
                    src_t, src_rdy = wqt, ("vv", qdone)
                    slot_free[sl] = [("vv", qdone)]
                else:
                    src_t, src_rdy = raw, (f"pl{sl}", ld_r)
                    slot_free[sl] = []
                for g in range((ncc + 3) // 4):
                    bank = P[g % 2]
                    n_in_g = min(4, ncc - g * 4)
                    last_pe = None
                    for j in range(n_in_g):
                        cc = g * 4 + j
                        w_ = [src_rdy]
                        if j == 0:
                            w_.append(("ss", bank_ev[g % 2]))
                            w_.append(("gg", gg_id))
                        last_pe = S.add(
                            "tensor",
                            lambda e, b=bank, j=j, s=src_t, cc=cc:
                            e.transpose(b[:, j * 128:(j + 1) * 128],
                                        s[:, cc * 128:(cc + 1) * 128], ident[:]),
                            waits=w_, inc="pe" if j == n_in_g - 1 else None)
                    es = evsl[ei % 4]
                    ss_ev = s_op(lambda e, b=bank, es=es, n=n_in_g:
                                 e.copy(out=es[:, 0:n * 128], in_=b[:, 0:n * 128]),
                                 waits=[("pe", last_pe), ev_free[ei % 4]])
                    bank_ev[g % 2] = ss_ev
                    last_st = 0
                    for j in range(n_in_g):
                        cc = g * 4 + j
                        last_st = st_dma(
                            f"st{ei % 4}",
                            dst_hbm[cc * 128:(cc + 1) * 128,
                                    ch * 128:(ch + 1) * 128],
                            es[:, j * 128:(j + 1) * 128], waits=[("ss", ss_ev)])
                    ev_free[ei % 4] = (f"st{ei % 4}", last_st)
                    ei += 1
                if not quant:
                    slot_free[sl] = [("pe", S.val("pe"))]
                else:
                    slot_free[sl].append(("pe", S.val("pe")))

        prep_slab(wqkv, 3 * C, C, s_qkvT, quant=False)
        prep_slab(wproj, C, C, s_projT, quant=False)
        prep_slab(wg_in, IF, C, s_gateT, quant=True)
        prep_slab(wd_in, C, IF, s_downT, quant=True)
        prep_pe = S.val("pe")
        prep_sts = [(f"st{i}", S.val(f"st{i}")) for i in range(4)]
        prep_ss, prep_vv = S.val("ss"), S.val("vv")

        LP_PIECES = [(ch, s) for ch in range(IF // 128) for s in range(3)]
        lp_state = {"i": 0,
                    "slot_free": [[("vv", prep_vv)], [("vv", prep_vv)]],
                    "ev_free": [], "p5_free": []}
        lp_slots = [hg, hh]
        lp_ev = outst[:, 512:896]

        def pump_up(n):
            for _ in range(n):
                if lp_state["i"] >= len(LP_PIECES):
                    return
                p = lp_state["i"]
                lp_state["i"] += 1
                ch, s = LP_PIECES[p]
                w = 256 if s == 2 else 384
                off = s * 384
                sl = p % 2
                slot = lp_slots[sl][:, 0:w]
                ld_p = ld_dma_g(f"pl{sl}", slot,
                                wu_in[ch * 128:(ch + 1) * 128, off:off + w],
                                waits=lp_state["slot_free"][sl])
                (qd, _) = emit_quant(slot, slot, w,
                                     src_ready=[(f"pl{sl}", ld_p)])
                nt = w // 128
                lastt = None
                for j in range(nt):
                    w_ = [("vv", qd)] if j == 0 else []
                    if j == 0:
                        w_ += lp_state["p5_free"]
                    lastt = S.add("tensor", lambda e, j=j, sl=sl:
                                  e.transpose(P[5][:, j * 128:(j + 1) * 128],
                                              lp_slots[sl][:, j * 128:(j + 1) * 128],
                                              ident[:]),
                                  waits=w_, inc="pe" if j == nt - 1 else None)
                lp_state["slot_free"][sl] = [("pe", lastt)]
                ev = s_op(lambda e, nt=nt: e.copy(
                    out=lp_ev[:, 0:nt * 128], in_=P[5][:, 0:nt * 128]),
                    waits=[("pe", lastt)] + lp_state["ev_free"])
                lp_state["p5_free"] = [("ss", ev)]
                last_st = 0
                for j in range(nt):
                    cc = s * 3 + j
                    last_st = st_dma(
                        "st0",
                        s_upT[cc * 128:(cc + 1) * 128, ch * 128:(ch + 1) * 128],
                        lp_ev[:, j * 128:(j + 1) * 128], waits=[("ss", ev)])
                lp_state["ev_free"] = [("st0", last_st)]

        xnf = lambda kc: XNF[:, kc * 2048:(kc + 1) * 2048]
        xqtm = lambda tc: XQF[:, tc * 1024:(tc + 1) * 1024]
        xqf = xnf
        v_tm = WBIG[:, 0:4096]
        k_fm = WBIG[:, 4096:6144]
        kT = WBIG[:, 6144:8192]
        G_t = lambda rc: WBIG[:, rc * 1024:(rc + 1) * 1024]
        fW_g = lambda kc: WBIG[:, kc * 384:(kc + 1) * 384]
        fW_u = lambda kc: WBIG[:, 3072 + kc * 384:3072 + (kc + 1) * 384]
        fW_d = lambda sl: WBIG[:, 6144 + sl * 1024:6144 + (sl + 1) * 1024]
        wv_t = lambda kc: WBIG[:, 6144 + kc * 256:6144 + (kc + 1) * 256]
        wk_t = lambda kc: WBIG[:, 6144 + kc * 128:6144 + (kc + 1) * 128]
        FFN_P = [(0, 384), (384, 384), (768, 384), (1152, 384), (1536, 384),
                 (1920, 384), (2304, 384), (2688, 128)]
        NFP = len(FFN_P)

        pro_bufs = (wstr[:, 0:1024], tA[:, 0:1024], tB[:, 0:1024],
                    outst[:, 1024:2048])
        xin_t = [xin[:, 0:1024], xin[:, 1024:2048]]
        out_sl = [outst[:, 0:1024], outst[:, 1024:2048]]
        out_free = [[], []]

        for b in range(nbatch):
            if stage < 2:
                break
            if b == 0:
                xnf_guard = [("pe", prep_pe), ("ss", prep_ss),
                             ("vv", prep_vv)] + prep_sts
            else:
                xnf_guard = [("pe", S.val("pe")), ("ss", S.val("ss")),
                             ("vv", S.val("vv")), ("ga", S.val("ga")),
                             ("gb", S.val("gb"))]
            xin_free = [[], []]
            b_start_pe = S.val("pe")
            ctx_b_guard = [("vv", S.val("vv")), ("ss", S.val("ss"))]
            ev_tc = [0] * NTC
            wv_ready = 0
            for kcw in range(KC):
                w_ = []
                if kcw == 0:
                    w_ = [("pe", b_start_pe)]
                    if b == 0:
                        w_ += prep_sts
                wv_ready = ld_dma(
                    "lw", wv_t(kcw),
                    s_qkvT[kcw * 128:(kcw + 1) * 128, 2048:2048 + 256],
                    waits=w_)
            vps_ev = [("ss", 0), ("ss", 0)]
            for tc in range(NTC):
                sl = tc % 2
                xt = xin_t[sl]
                w0 = list(xnf_guard) if tc < 2 else xin_free[sl]
                ld_x = 0
                for qh in range(2):
                    ld_x = ld_dma(
                        f"xl{sl}", xt[:, qh * 512:(qh + 1) * 512],
                        x_in[b, tc * 128:(tc + 1) * 128, qh * 512:(qh + 1) * 512],
                        waits=w0 if qh == 0 else [])
                p8 = (tc % 2) * 8
                su, mn, vr, t1, ve = (
                    stat[:, p8:p8 + 1], stat[:, p8 + 1:p8 + 2],
                    stat[:, p8 + 2:p8 + 3], stat[:, p8 + 5:p8 + 6],
                    stat[:, p8 + 6:p8 + 7])
                rsd = rsdP[tc % 2][:]
                nb_ = nbP[tc % 2][:]
                s_op(lambda e, xt=xt, su=su, tc=tc: e.activation(
                    out=xqtm(tc), in_=xt, func=AF.Identity, accum_out=su),
                    waits=[(f"xl{sl}", ld_x)])
                ss_sq = s_op(lambda e, xt=xt, t1=t1, tc=tc: e.activation(
                    out=xqtm(tc), in_=xt, func=AF.Square, accum_out=t1),
                    waits=[(f"xl{sl}", ld_x)])
                v_op(lambda e, su=su, mn=mn: e.tensor_scalar(
                    out=mn, in0=su, scalar1=1.0 / C, scalar2=None, op0=ALU.mult),
                    waits=[("ss", ss_sq)], inc=False, drain=True)
                v_op(lambda e, vr=vr, t1=t1: e.tensor_scalar(
                    out=vr, in0=t1, scalar1=1.0 / C, scalar2=None, op0=ALU.mult),
                    waits=[("ss", ss_sq)], inc=False, drain=True)
                v_op(lambda e, mn=mn, t1=t1: e.tensor_scalar(
                    out=t1, in0=mn, scalar1=mn, scalar2=-1.0, op0=ALU.mult,
                    op1=ALU.mult), inc=False, drain=True)
                v_op(lambda e, vr=vr, t1=t1: e.tensor_tensor(
                    out=vr, in0=vr, in1=t1, op=ALU.add), inc=False, drain=True)
                vv_ve = v_op(lambda e, vr=vr, ve=ve: e.tensor_scalar(
                    out=ve, in0=vr, scalar1=1e-5, scalar2=None, op0=ALU.add),
                    drain=True)
                ss_sv = s_op(lambda e, ve=ve, t1=t1: e.activation(
                    out=t1, in_=ve, func=AF.Sqrt),
                    waits=[("vv", vv_ve)])
                v_op(lambda e, rsd=rsd, t1=t1: e.reciprocal(out=rsd, in_=t1),
                     waits=[("ss", ss_sv)], inc=False, drain=True)
                v_op(lambda e, rsd=rsd, t1=t1: e.tensor_tensor(
                    out=t1, in0=rsd, in1=rsd, op=ALU.mult), inc=False, drain=True)
                v_op(lambda e, ve=ve, t1=t1: e.tensor_tensor(
                    out=t1, in0=t1, in1=ve, op=ALU.mult), inc=False, drain=True)
                v_op(lambda e, t1=t1: e.tensor_scalar(
                    out=t1, in0=t1, scalar1=-0.5, scalar2=1.5, op0=ALU.mult,
                    op1=ALU.add), inc=False, drain=True)
                v_op(lambda e, rsd=rsd, t1=t1: e.tensor_tensor(
                    out=rsd, in0=rsd, in1=t1, op=ALU.mult), inc=False, drain=True)
                vv_nb = v_op(lambda e, mn=mn, rsd=rsd, nb_=nb_: e.tensor_scalar(
                    out=nb_, in0=mn, scalar1=rsd, scalar2=-1.0, op0=ALU.mult,
                    op1=ALU.mult), drain=True)
                ss_xn = s_op(lambda e, xt=xt, rsd=rsd, nb_=nb_: e.activation(
                    out=xt, in_=xt, func=AF.Identity, scale=rsd, bias=nb_),
                    waits=[("vv", vv_nb)])
                if debug and b == 0 and tc == 0:
                    (qd0, _) = emit_quant(xt[:, 0:512], xqtm(tc)[:, 0:512], 512,
                                          src_ready=[("ss", ss_xn)],
                                          bufs=pro_bufs)
                    dbg_st = 0
                    for dap, src_ in [(d_qu, qu[:]), (d_qn, qn[:]),
                                      (d_qsg, qsg[:]), (d_qsc, qsc[:, 0:96]),
                                      (d_xn0, xt)]:
                        dbg_st = st_dma("st1", dap, src_, waits=[("vv", qd0)])
                    (qd, qss) = emit_quant(xt[:, 512:1024], xqtm(tc)[:, 512:1024],
                                           512, src_ready=[("ss", ss_xn),
                                                           ("st1", dbg_st)],
                                           bufs=pro_bufs)
                else:
                    (qd, qss) = emit_quant_multi(xt, xqtm(tc), 1024,
                                                 src_ready=[("ss", ss_xn)],
                                                 bufs=pro_bufs, maxw=1024)
                last_tr = None
                for kc in range(KC):
                    w_ = [("ss", ss_xn)]
                    if kc == 0 and tc > 0:
                        w_.append(("ss", ev_tc[tc - 1]))
                    bankt = P[6 + kc // 4]
                    last_tr = S.add("tensor",
                                    lambda e, kc=kc, xt=xt, bankt=bankt:
                                    e.transpose(bankt[:, (kc % 4) * 128:
                                                      (kc % 4 + 1) * 128],
                                                xt[:, kc * 128:(kc + 1) * 128],
                                                ident[:]),
                                    waits=w_, inc="pe" if kc == KC - 1 else None)
                xin_free[sl] = [("pe", last_tr), ("vv", qd), ("ss", qss)]
                ev = 0
                for kc in range(KC):
                    ev = s_op(lambda e, kc=kc, tc=tc: e.copy(
                        out=xnf(kc)[:, tc * 128:(tc + 1) * 128],
                        in_=P[6 + kc // 4][:, (kc % 4) * 128:(kc % 4 + 1) * 128]),
                        waits=[("pe", last_tr)] if kc == 0 else [])
                ev_tc[tc] = ev
                bank = P[tc % 2]
                last = None
                for kc in range(KC):
                    w_ = [("lw", wv_ready)]
                    if kc == 0:
                        w_.append(("ss", ev_tc[tc]))
                        w_.append(vps_ev[tc % 2])
                    last = S.add("tensor", lambda e, bank=bank, kc=kc, tc=tc:
                                 e.matmul(bank[:, 0:256],
                                          xnf(kc)[:, tc * 128:(tc + 1) * 128],
                                          wv_t(kc),
                                          start=(kc == 0), stop=(kc == KC - 1)),
                                 waits=w_, inc="pe" if kc == KC - 1 else None)
                ev_v = s_op(lambda e, bank=bank, tc=tc: e.copy(
                    out=v_tm[:, tc * 256:(tc + 1) * 256], in_=bank[:, 0:256]),
                    waits=[("pe", last)])
                vps_ev[tc % 2] = ("ss", ev_v)
            prologue_ss = S.val("ss")
            prologue_vv = S.val("vv")

            if stage < 3:
                continue
            ctx_ev_prev = list(ctx_b_guard)
            for vg in range(4):
                vz = v_op(lambda e: e.memset(P[4][0:64, 128:512], 0.0),
                          waits=ctx_ev_prev)
                S.add("tensor", lambda e: e.matmul(
                    P[4][0:64, 0:128], ones1[0:1, 0:64], zrow[0:1, :],
                    start=True, stop=False),
                    waits=[("gg", gg_zrow), ("vv", vz)] + ctx_ev_prev,
                    inc="pe")
                if vg > 0:
                    wv_ready = 0
                    for kcw in range(KC):
                        wv_ready = ld_dma(
                            "lw", wv_t(kcw),
                            s_qkvT[kcw * 128:(kcw + 1) * 128,
                                   2048 + vg * 256:2048 + (vg + 1) * 256],
                            waits=[("pe", S.val("pe"))] if kcw == 0 else [])
                    vps_ev = [("ss", S.val("ss")), ("ss", S.val("ss"))]
                    for tc in range(NTC):
                        bank = P[tc % 2]
                        last = None
                        for kc in range(KC):
                            w_ = [("lw", wv_ready)]
                            if kc == 0:
                                w_.append(("ss", ev_tc[tc]))
                                w_.append(vps_ev[tc % 2])
                            last = S.add("tensor",
                                         lambda e, bank=bank, kc=kc, tc=tc:
                                         e.matmul(bank[:, 0:256],
                                                  xnf(kc)[:, tc * 128:(tc + 1) * 128],
                                                  wv_t(kc),
                                                  start=(kc == 0),
                                                  stop=(kc == KC - 1)),
                                         waits=w_,
                                         inc="pe" if kc == KC - 1 else None)
                        ev = s_op(lambda e, bank=bank, tc=tc: e.copy(
                            out=v_tm[:, tc * 256:(tc + 1) * 256],
                            in_=bank[:, 0:256]),
                            waits=[("pe", last)])
                        vps_ev[tc % 2] = ("ss", ev)
                    v_done_ss = S.val("ss")
                else:
                    v_done_ss = prologue_ss
                k_free_vv = 0
                for ks in range(2):
                    sc_idx = vg * 2 + ks
                    wk_ready = 0
                    for kcw in range(KC):
                        wk_ready = ld_dma(
                            "lw", wk_t(kcw),
                            s_qkvT[kcw * 128:(kcw + 1) * 128,
                                   1024 + sc_idx * 128:1024 + (sc_idx + 1) * 128],
                            waits=[("pe", S.val("pe"))] if kcw == 0 else [])
                    kps_ev = [("ss", S.val("ss")), ("ss", S.val("ss"))]
                    kexp_last = 0
                    for tq in range(4):
                        bank = P[2 + tq % 2]
                        last = None
                        for kc in range(KC):
                            w_ = [("lw", wk_ready)]
                            if kc == 0:
                                w_.append(kps_ev[tq % 2])
                            last = S.add("tensor", lambda e, bank=bank, kc=kc, tq=tq:
                                         e.matmul(bank[:],
                                                  wk_t(kc),
                                                  xnf(kc)[:, tq * 512:(tq + 1) * 512],
                                                  start=(kc == 0),
                                                  stop=(kc == KC - 1)),
                                         waits=w_, inc="pe" if kc == KC - 1 else None)
                        w_e = [("pe", last)]
                        kexp_last = s_op(
                            lambda e, bank=bank, tq=tq, sc_idx=sc_idx:
                            e.activation(
                                out=k_fm[:, tq * 512:(tq + 1) * 512], in_=bank[:],
                                func=AF.Exp,
                                accum_out=stat[:, 32 + sc_idx * 4 + tq:
                                               33 + sc_idx * 4 + tq]),
                            waits=w_e)
                        kps_ev[tq % 2] = ("ss", kexp_last)
                    kexp_ss = kexp_last
                    v_op(lambda e, sc_idx=sc_idx: e.reduce_sum(
                        out=rs_all[:, sc_idx:sc_idx + 1],
                        in_=stat[:, 32 + sc_idx * 4:36 + sc_idx * 4], axis=AX.X),
                        waits=[("ss", kexp_ss)], inc=False, drain=True)
                    k_free_vv = v_op(lambda e, sc_idx=sc_idx: e.reciprocal(
                        out=rs_all[:, sc_idx:sc_idx + 1],
                        in_=rs_all[:, sc_idx:sc_idx + 1]), drain=True)
                    tr_ev = ("ss", 0)
                    lastt = None
                    for g in range(4):
                        for j in range(4):
                            tcc = g * 4 + j
                            w_ = [("ss", kexp_ss)]
                            if j == 0:
                                w_.append(tr_ev)
                            lastt = S.add("tensor", lambda e, j=j, tcc=tcc:
                                          e.transpose(P[2][:, j * 128:(j + 1) * 128],
                                                      k_fm[:, tcc * 128:(tcc + 1) * 128],
                                                      ident[:]),
                                          waits=w_, inc="pe" if j == 3 else None)
                        evt = s_op(lambda e, g=g: e.copy(
                            out=kT[:, g * 512:(g + 1) * 512], in_=P[2][:]),
                            waits=[("pe", lastt)])
                        tr_ev = ("ss", evt)
                    kT_ss = S.val("ss")
                    for hl in range(2):
                        head_local = ks * 2 + hl
                        last = None
                        for tc in range(NTC):
                            w_ = []
                            if tc == 0:
                                w_ = [("ss", kT_ss), ("ss", v_done_ss),
                                      ("vv", vz)]
                            fin = (head_local == 3 and tc == NTC - 1)
                            last = S.add(
                                "tensor",
                                lambda e, tc=tc, head_local=head_local, fin=fin:
                                e.matmul(P[4][0:64, head_local * 128:
                                              head_local * 128 + 128],
                                         v_tm[:, tc * 256 + head_local * 64:
                                              tc * 256 + (head_local + 1) * 64],
                                         kT[:, tc * 128:(tc + 1) * 128],
                                         start=False, stop=fin),
                                waits=w_, inc="pe" if tc == NTC - 1 else None)
                        ctx_last_pe = last
                ev_c = 0
                for head_local in range(4):
                    hgl = vg * 4 + head_local
                    ev_c = s_op(lambda e, hgl=hgl, head_local=head_local: e.copy(
                        out=ctxsb[(hgl % 2) * 64:(hgl % 2) * 64 + 64,
                                  (hgl // 2) * 64:(hgl // 2) * 64 + 64],
                        in_=P[4][0:64, head_local * 128 + (head_local % 2) * 64:
                                 head_local * 128 + (head_local % 2) * 64 + 64]),
                        waits=[("pe", ctx_last_pe)] if head_local == 0 else [])
                ctx_ev_prev = [("ss", ev_c)]
                if b == 0:
                    pump_up(17)
            if b == 0:
                pump_up(len(LP_PIECES))
                out_free[0] = [("st0", S.val("st0"))]
            ctx_ss = S.val("ss")
            phase1_vv = S.val("vv")

            if stage < 4:
                continue
            gf_ev = ("ss", 0)
            g_last_pe = S.val("pe")
            for rc in range(8):
                for hh_ in range(2):
                    h_gl = rc * 2 + hh_
                    wp_ready = ld_dma(
                        "lw", wstr[hh_ * 64:hh_ * 64 + 64, 0:1024],
                        s_projT[h_gl * 64:(h_gl + 1) * 64, :],
                        waits=[("pe", g_last_pe)])
                    for half in range(2):
                        bank = P[4 + half]
                        w_ = [("lw", wp_ready), ("ss", ctx_ss)]
                        if hh_ == 0 and half == 0:
                            w_.append(gf_ev)
                        g_last_pe = S.add(
                            "tensor",
                            lambda e, bank=bank, hh_=hh_, h_gl=h_gl, half=half:
                            e.matmul(bank[hh_ * 64:(hh_ + 1) * 64, :],
                                     ctxsb[(h_gl % 2) * 64:(h_gl % 2) * 64 + 64,
                                           (h_gl // 2) * 64:(h_gl // 2) * 64 + 64],
                                     wstr[hh_ * 64:hh_ * 64 + 64,
                                          half * 512:half * 512 + 512],
                                     start=True, stop=True),
                            waits=w_,
                            inc="pe" if half == 1 else None)
                s_op(lambda e, rc=rc: e.activation(
                    out=G_t(rc)[:, 0:512], in_=P[4][:], func=AF.Identity,
                    scale=rs_all[:, rc:rc + 1], bias=zeros1[:]),
                    waits=[("pe", g_last_pe), ("vv", v_zero),
                           ("vv", phase1_vv)])
                ev2 = s_op(lambda e, rc=rc: e.activation(
                    out=G_t(rc)[:, 512:1024], in_=P[5][:], func=AF.Identity,
                    scale=rs_all[:, rc:rc + 1], bias=zeros1[:]))
                gf_ev = ("ss", ev2)
            g_ready_ss = S.val("ss")
            g_ready_pe = S.val("pe")
            if debug and b == 0:
                for name_ap, src in [(d_ctx, ctxsb[:]), (d_rs, rs_all[:]),
                                     (d_G, WBIG[:]), (d_kfm, k_fm),
                                     (d_xnf, xnf(0)), (d_xq, xqtm(0)),
                                     (d_stat, stat[:]),
                                     (d_rn[0], rsdP[0][:]),
                                     (d_rn[1], rsdP[1][:]),
                                     (d_rn[2], nbP[0][:]),
                                     (d_rn[3], nbP[1][:])]:
                    st_dma("st0", name_ap, src,
                           waits=[("ss", g_ready_ss), ("vv", S.val("vv"))])

            if stage < 5:
                continue
            wq_sl = [wstr[:, 0:512], wstr[:, 512:1024]]
            wq_free = [[("pe", g_ready_pe)], [("pe", g_ready_pe)]]
            xp2_free = [list(xin_free[0]), list(xin_free[1])]
            qpsum_ev = ("ss", g_ready_ss)
            ap_ev = ("ss", g_ready_ss)
            tr_ev2 = ("ss", g_ready_ss)
            qtr_prev = 0
            for pair in range(8):
                qmm_last = None
                for kc in range(KC):
                    for half in range(2):
                        wsl = (kc * 2 + half) % 2
                        ldq = 0
                        for qh in range(2):
                            ldq = ld_dma(
                                "lw", wq_sl[wsl][:, qh * 256:(qh + 1) * 256],
                                s_qkvT[kc * 128:(kc + 1) * 128,
                                       half * 512 + qh * 256:
                                       half * 512 + (qh + 1) * 256],
                                waits=wq_free[wsl] if qh == 0 else [])
                        for ti in range(2):
                            tc = pair * 2 + ti
                            bank = P[ti * 2 + half]
                            w_ = [("lw", ldq)]
                            if kc == 0:
                                w_.append(qpsum_ev)
                            qmm_last = S.add(
                                "tensor",
                                lambda e, bank=bank, kc=kc, tc=tc, half=half,
                                wsl=wsl:
                                e.matmul(bank[:],
                                         xnf(kc)[:, tc * 128:(tc + 1) * 128],
                                         wq_sl[wsl][:],
                                         start=(kc == 0), stop=(kc == KC - 1)),
                                waits=w_,
                                inc="pe" if ti == 1 else None)
                        wq_free[wsl] = [("pe", qmm_last)]
                for ti in range(2):
                    tc = pair * 2 + ti
                    s_op(lambda e, ti=ti: e.activation(
                        out=tA[:, 0:512], in_=P[ti * 2][:], func=AF.Exp),
                        waits=[("pe", qmm_last), ("vv", S.val("vv"))])
                    ss_exp = s_op(lambda e, ti=ti: e.activation(
                        out=tA[:, 512:1024], in_=P[ti * 2 + 1][:], func=AF.Exp))
                    if ti == 1:
                        qpsum_ev = ("ss", ss_exp)
                    v_op(lambda e: e.reduce_sum(
                        out=stat[:, 16:32],
                        in_=tA[:].rearrange("p (h d) -> p h d", d=64), axis=AX.X),
                        waits=[("ss", ss_exp)], inc=False, drain=True)
                    v_op(lambda e: e.reciprocal(out=stat[:, 16:32],
                                                in_=stat[:, 16:32]), inc=False,
                         drain=True)
                    vv_qs = v_op(lambda e: e.tensor_tensor(
                        out=tB[:].rearrange("p (h d) -> p h d", d=64),
                        in0=tA[:].rearrange("p (h d) -> p h d", d=64),
                        in1=stat[:, 16:32].unsqueeze(2).broadcast_to([128, 16, 64]),
                        op=ALU.mult), waits=[("pe", qtr_prev)])
                    for g in range(2):
                        bankt = P[6 + g]
                        lastq = None
                        for j in range(4):
                            kc = g * 4 + j
                            w_ = [("vv", vv_qs)]
                            if j == 0:
                                w_.append(tr_ev2)
                            lastq = S.add("tensor",
                                          lambda e, bankt=bankt, j=j, kc=kc:
                                          e.transpose(bankt[:, j * 128:(j + 1) * 128],
                                                      tB[:, kc * 128:(kc + 1) * 128],
                                                      ident[:]),
                                          waits=w_, inc="pe" if j == 3 else None)
                        ss_t = s_op(lambda e, g=g: e.copy(
                            out=tA[:, g * 512:(g + 1) * 512], in_=P[6 + g][:]),
                            waits=[("pe", lastq)])
                        tr_ev2 = ("ss", ss_t)
                        qtr_prev = lastq
                    qsf_ss = S.val("ss")
                    last_ap = None
                    for half in range(2):
                        bank = P[4 + half]
                        for kc in range(KC):
                            w_ = [("ss", qsf_ss)]
                            if kc == 0 and half == 0:
                                w_.append(ap_ev)
                            last_ap = S.add(
                                "tensor", lambda e, bank=bank, kc=kc, half=half:
                                e.matmul(bank[:], tA[:, kc * 128:(kc + 1) * 128],
                                         G_t(kc)[:, half * 512:(half + 1) * 512],
                                         start=(kc == 0), stop=False),
                                waits=w_, inc=None)
                        if "bias" in skip:
                            last_ap = S.add(
                                "tensor", lambda e, bank=bank, kc=KC - 1, half=half:
                                e.matmul(bank[:], tA[:, (KC - 1) * 128:KC * 128],
                                         G_t(KC - 1)[:, half * 512:(half + 1) * 512],
                                         start=False, stop=True),
                                waits=[], inc="pe")
                        else:
                            last_ap = S.add(
                                "tensor", lambda e, bank=bank, half=half:
                                e.matmul(bank[:], ones1[:],
                                         brow[0:1, half * 512:(half + 1) * 512],
                                         start=False, stop=True),
                                waits=[("lb", ld_brow), ("gg", gg_ones)], inc="pe")
                    osl = ti
                    ld_x2 = 0
                    for qh in range(2):
                        ld_x2 = ld_dma(
                            f"xl{ti}", xin_t[ti][:, qh * 512:(qh + 1) * 512],
                            x_in[b, tc * 128:(tc + 1) * 128,
                                 qh * 512:(qh + 1) * 512],
                            waits=xp2_free[ti] if qh == 0 else [])
                    v_op(lambda e, osl=osl, ti=ti: e.tensor_tensor(
                        out=out_sl[osl][:, 0:512], in0=P[4][:],
                        in1=xin_t[ti][:, 0:512], op=ALU.add),
                        waits=[("pe", last_ap), (f"xl{ti}", ld_x2)] + out_free[osl],
                        inc=False)
                    vv_o = v_op(lambda e, osl=osl, ti=ti: e.tensor_tensor(
                        out=out_sl[osl][:, 512:1024], in0=P[5][:],
                        in1=xin_t[ti][:, 512:1024], op=ALU.add))
                    ap_ev = ("vv", vv_o)
                    xp2_free[ti] = [("vv", vv_o)]
                    st_w = st_dma(f"so{osl}", out[b, tc * 128:(tc + 1) * 128, :],
                                  out_sl[osl], waits=[("vv", vv_o)])
                    out_free[osl] = [(f"so{osl}", st_w)]
            resid_gx = S.val("gx")

            if stage < 6:
                continue
            xq_guard = [("pe", S.val("pe")), ("ss", S.val("ss")),
                        ("vv", S.val("vv"))]
            prev_ev = 0
            last_tr = None
            for tc in range(NTC):
                for kc in range(KC):
                    w_ = [("ss", prologue_ss), ("vv", prologue_vv)]
                    if tc == 0 and kc == 0:
                        w_ += xq_guard
                    if tc % 4 == 0 and kc == 0:
                        w_.append(("ss", prev_ev))
                    last_tr = S.add("tensor", lambda e, kc=kc, tc=tc:
                                    e.transpose(P[kc][:, (tc % 4) * 128:(tc % 4 + 1) * 128],
                                                xqtm(tc)[:, kc * 128:(kc + 1) * 128],
                                                ident[:]),
                                    waits=w_, inc="pe" if kc == KC - 1 else None)
                if tc % 4 == 3:
                    ev = 0
                    for kc in range(KC):
                        ev = s_op(lambda e, kc=kc, tc=tc: e.copy(
                            out=xqf(kc)[:, (tc - 3) * 128:(tc + 1) * 128],
                            in_=P[kc][:, 0:512]),
                            waits=[("pe", last_tr)])
                    prev_ev = ev
            xq_fm_ss = S.val("ss")

            g_free = [("pe", S.val("pe")), ("ss", xq_fm_ss)]
            u_free = [("pe", S.val("pe")), ("st0", S.val("st0"))]
            dpsum_ev = ("vv", S.val("vv"))
            gate_ev = ("ss", S.val("ss"))
            up_ev = ("vv", S.val("vv"))
            hq_free = [("pe", S.val("pe")), ("pe", S.val("pe"))]
            hqT_busy = [[], []]
            ht_ev = [S.val("ss"), S.val("ss")]
            fwd_free = [[("pe", S.val("pe"))], [("pe", S.val("pe"))]]
            hqt2 = [hqt[:, 0:384], tA[:, 0:384]]
            hqT_sl = [hqT[:, 0:384], tB[:, 0:384]]
            for pair in range(8):
                qd_p = [None, None]
                tr_p = [None, None]

                def emit_downstage(dp, wdt_d, i0_d):
                    nonlocal fwd_free, hqT_busy
                    nj = wdt_d // 128
                    ht_last = 0
                    for ti in range(2):
                        ht_ev[ti] = s_op(lambda e, ti=ti, wdt_d=wdt_d: e.copy(
                            out=hqT_sl[ti][:, 0:wdt_d], in_=P[6 + ti][:, 0:wdt_d]),
                            waits=[("pe", tr_p[ti])] + hqT_busy[ti])
                        ht_last = ht_ev[ti]
                    dld = {}
                    for j in range(min(nj, 2)):
                        dld[j] = ld_dma_g("ld2", fW_d(j % 2),
                                          s_downT[i0_d + j * 128:i0_d + (j + 1) * 128, :],
                                          waits=fwd_free[j % 2])
                    if nj >= 2:
                        dld[0] = dld[1]
                    dmm = {}
                    last_d = None
                    for j in range(nj):
                        dsl = j % 2
                        if j >= 2:
                            dld[j] = ld_dma_g("ld2", fW_d(dsl),
                                              s_downT[i0_d + j * 128:i0_d + (j + 1) * 128, :],
                                              waits=[("pe", dmm[j - 2])])
                        for ti in range(2):
                            for half in range(2):
                                bank = P[2 + ti * 2 + half]
                                last_d = S.add(
                                    "tensor",
                                    lambda e, j=j, bank=bank, half=half, dp=dp,
                                    dsl=dsl, nj=nj, ti=ti, wdt_d=wdt_d:
                                    e.matmul(bank[:],
                                             hqT_sl[ti][:, j * 128:(j + 1) * 128],
                                             fW_d(dsl)[:, half * 512:(half + 1) * 512],
                                             start=(dp == 0 and j == 0),
                                             stop=(dp == NFP - 1 and j == nj - 1)),
                                    waits=[("ss", ht_last), ("ld2", dld[j])],
                                    inc="pe" if (ti == 1 and half == 1) else None)
                        dmm[j] = last_d
                        fwd_free[dsl] = [("pe", last_d)]
                    hqT_busy = [[("pe", last_d)], [("pe", last_d)]]
                    return last_d

                for pi, (i0, wdt) in enumerate(FFN_P):
                    wld_g = 0
                    for kc in range(KC):
                        wld_g = ld_dma("lw", fW_g(kc)[:, 0:wdt],
                                       s_gateT[kc * 128:(kc + 1) * 128, i0:i0 + wdt],
                                       waits=g_free if kc == 0 else [])
                    wld_u = 0
                    for kc in range(KC):
                        wld_u = ld_dma("lu", fW_u(kc)[:, 0:wdt],
                                       s_upT[kc * 128:(kc + 1) * 128, i0:i0 + wdt],
                                       waits=u_free if kc == 0 else [])
                    lastu_all = None
                    lastg_all = None
                    for ti in range(2):
                        tc = pair * 2 + ti
                        lastg = lastu = None
                        for kc in range(KC):
                            w_ = [("lw", wld_g), ("ss", xq_fm_ss)]
                            if kc == 0:
                                w_.append(gate_ev)
                                if pi == 0 and ti == 0:
                                    w_.append(dpsum_ev)
                            lastg = S.add("tensor", lambda e, kc=kc, tc=tc, wdt=wdt:
                                          e.matmul(P[0][:, 0:wdt],
                                                   xqf(kc)[:, tc * 128:(tc + 1) * 128],
                                                   fW_g(kc)[:, 0:wdt],
                                                   start=(kc == 0), stop=(kc == KC - 1)),
                                          waits=w_, inc="pe" if kc == KC - 1 else None)
                        ss_g = s_op(lambda e, wdt=wdt: e.activation(
                            out=hg[:, 0:wdt], in_=P[0][:, 0:wdt], func=AF.Silu),
                            waits=[("pe", lastg)])
                        gate_ev = ("ss", ss_g)
                        if pi > 0 and qd_p[ti] is not None:
                            wdt_prev = FFN_P[pi - 1][1]
                            lastt = None
                            for j in range(wdt_prev // 128):
                                w_ = [("vv", qd_p[ti])]
                                if j == 0:
                                    w_.append(("ss", ht_ev[ti]))
                                lastt = S.add(
                                    "tensor", lambda e, j=j, ti=ti:
                                    e.transpose(P[6 + ti][:, j * 128:(j + 1) * 128],
                                                hqt2[ti][:, j * 128:(j + 1) * 128],
                                                ident[:]),
                                    waits=w_,
                                    inc="pe" if j == wdt_prev // 128 - 1 else None)
                            tr_p[ti] = lastt
                            hq_free[ti] = ("pe", lastt)
                        for kc in range(KC):
                            w_ = [("lu", wld_u), up_ev] if kc == 0 else []
                            lastu = S.add("tensor", lambda e, kc=kc, tc=tc, wdt=wdt:
                                          e.matmul(P[1][:, 0:wdt],
                                                   xqf(kc)[:, tc * 128:(tc + 1) * 128],
                                                   fW_u(kc)[:, 0:wdt],
                                                   start=(kc == 0), stop=(kc == KC - 1)),
                                          waits=w_, inc="pe" if kc == KC - 1 else None)
                        lastu_all = lastu
                        lastg_all = lastg
                        vv_h = v_op(lambda e, wdt=wdt: e.tensor_tensor(
                            out=hh[:, 0:wdt], in0=hg[:, 0:wdt], in1=P[1][:, 0:wdt],
                            op=ALU.mult), waits=[("ss", ss_g), ("pe", lastu)])
                        up_ev = ("vv", vv_h)
                        (qd, _) = emit_quant(hh[:, 0:wdt], hqt2[ti][:, 0:wdt], wdt,
                                             src_ready=[("vv", vv_h)],
                                             dst_free=[hq_free[ti]])
                        qd_prev_ti = qd_p[ti]
                        qd_p[ti] = qd
                    g_free = [("pe", lastg_all)]
                    u_free = [("pe", lastu_all)]
                    if pi > 0:
                        emit_downstage(pi - 1, FFN_P[pi - 1][1], FFN_P[pi - 1][0])
                wdt_l, i0_l = FFN_P[NFP - 1][1], FFN_P[NFP - 1][0]
                for ti in range(2):
                    lastt = None
                    for j in range(wdt_l // 128):
                        w_ = [("vv", qd_p[ti])]
                        lastt = S.add("tensor", lambda e, j=j, ti=ti:
                                      e.transpose(P[6 + ti][:, j * 128:(j + 1) * 128],
                                                  hqt2[ti][:, j * 128:(j + 1) * 128],
                                                  ident[:]),
                                      waits=w_,
                                      inc="pe" if j == wdt_l // 128 - 1 else None)
                    tr_p[ti] = lastt
                    hq_free[ti] = ("pe", lastt)
                emit_downstage(NFP - 1, wdt_l, i0_l)
                dd = S.val("vv")
                for ti in range(2):
                    tc = pair * 2 + ti
                    osl = ti
                    v_op(lambda e, osl=osl, ti=ti: e.tensor_copy(
                        out=out_sl[osl][:, 0:512], in_=P[2 + ti * 2][:]),
                        waits=[("pe", S.val("pe"))] + out_free[osl], inc=False)
                    vv_e = v_op(lambda e, osl=osl, ti=ti: e.tensor_copy(
                        out=out_sl[osl][:, 512:1024], in_=P[2 + ti * 2 + 1][:]))
                    gsem = "ga" if osl == 0 else "gb"
                    gda = gd_accum(gsem, out[b, tc * 128:(tc + 1) * 128, :],
                                   out_sl[osl],
                                   waits=[("vv", vv_e), ("gx", resid_gx),
                                          (f"so{osl}", S.val(f"so{osl}"))])
                    out_free[osl] = [(gsem, gda)]
                    dd = vv_e
                dpsum_ev = ("vv", dd)

        for sname in ("gx", "ga", "gb"):
            S.add("gpsimd", lambda e, sname=sname: e.wait_ge(sem[sname], S.cnt[sname]))
        for sname in ("so0", "so1", "gx", "ga", "gb", "st0", "st1", "st2", "st3"):
            S.add("sync", lambda e, sname=sname: e.wait_ge(sem[sname], S.cnt[sname]))

        def run_items(eng_obj, name):
            for waits, fn, inc, inc_by, drain_after in S.items[name]:
                for (s_name, val) in waits:
                    if val:
                        eng_obj.wait_ge(sem[s_name], val)
                inst = fn(eng_obj) if fn is not None else None
                if inc is not None and inst is not None:
                    inst.then_inc(sem[inc], inc_by)
                if drain_after:
                    eng_obj.drain()

        @block.sync
        def _(e):
            run_items(e, "sync")

        @block.gpsimd
        def _(e):
            run_items(e, "gpsimd")

        @block.scalar
        def _(e):
            run_items(e, "scalar")

        @block.vector
        def _(e):
            run_items(e, "vector")

        @block.tensor
        def _(e):
            run_items(e, "tensor")

    return nc


def kernel(L0_attn_x, L0_attn_w_qkv, L0_attn_w_proj, L0_attn_b_proj,
           L0_ffn_w_gate, L0_ffn_w_up, L0_ffn_w_down):
    x = np.ascontiguousarray(np.asarray(L0_attn_x, dtype=np.float32))
    wqkv = np.ascontiguousarray(np.asarray(L0_attn_w_qkv, dtype=np.float32))
    wproj = np.ascontiguousarray(np.asarray(L0_attn_w_proj, dtype=np.float32))
    bproj = np.ascontiguousarray(np.asarray(L0_attn_b_proj, dtype=np.float32))
    wg = np.ascontiguousarray(np.asarray(L0_ffn_w_gate, dtype=np.float32))
    wu = np.ascontiguousarray(np.asarray(L0_ffn_w_up, dtype=np.float32))
    wd = np.ascontiguousarray(np.asarray(L0_ffn_w_down, dtype=np.float32))

    nc = build_core_program()
    in_maps = [{"x": x[2 * c:2 * c + 2], "wqkv": wqkv, "wproj": wproj,
                "bproj": bproj, "wg": wg, "wu": wu, "wd": wd}
               for c in range(NCORES)]
    res = run_bass_kernel_spmd(nc, in_maps, core_ids=list(range(NCORES)))
    return np.concatenate([r["out"] for r in res.results], axis=0)


# version 16
# speedup vs baseline: 1.1464x; 1.1464x over previous
import numpy as np
import concourse.bass as bass
import concourse.mybir as mybir
from concourse.bass_utils import run_bass_kernel_spmd
from contextlib import ExitStack


F32 = mybir.dt.float32
AF = mybir.ActivationFunctionType
ALU = mybir.AluOpType
AX = mybir.AxisListType

NCORES = 8
B_CORE = 2
N = 2048
C = 1024
H = 16
DH = 64
IF = 2816
NTC = N // 128
KC = C // 128
FFN_PASSES = [(i * 256, 256) for i in range(11)]

ENGS = ("sync", "gpsimd", "scalar", "vector", "tensor")


class Sched:
    def __init__(self):
        self.items = {e: [] for e in ENGS}
        self.cnt = {s: 0 for s in
                    ("pl0", "pl1", "xl0", "xl1", "lw", "lu", "lb", "ld2",
                     "st0", "st1", "st2", "st3", "so0", "so1",
                     "gx", "ga", "gb", "pe", "vv", "ss", "gg")}

    def add(self, eng, fn, waits=(), inc=None, inc_by=1, drain_after=False):
        self.items[eng].append((tuple(waits), fn, inc, inc_by, drain_after))
        if inc is not None:
            self.cnt[inc] += inc_by
            return self.cnt[inc]
        return None

    def val(self, sem):
        return self.cnt[sem]


def build_core_program(stage=7, nbatch=B_CORE, skip=(), debug=False):
    nc = bass.Bass("TRN2", target_bir_lowering=False,
                   detect_race_conditions=False)

    x_in = nc.dram_tensor("x", [B_CORE, N, C], F32, kind="ExternalInput").ap()
    wqkv = nc.dram_tensor("wqkv", [3 * C, C], F32, kind="ExternalInput").ap()
    wproj = nc.dram_tensor("wproj", [C, C], F32, kind="ExternalInput").ap()
    bproj = nc.dram_tensor("bproj", [C], F32, kind="ExternalInput").ap()
    wg_in = nc.dram_tensor("wg", [IF, C], F32, kind="ExternalInput").ap()
    wu_in = nc.dram_tensor("wu", [IF, C], F32, kind="ExternalInput").ap()
    wd_in = nc.dram_tensor("wd", [C, IF], F32, kind="ExternalInput").ap()
    out = nc.dram_tensor("out", [B_CORE, N, C], F32, kind="ExternalOutput").ap()
    s_qkvT = nc.dram_tensor("s_qkvT", [C, 3 * C], F32, kind="Internal").ap()
    s_projT = nc.dram_tensor("s_projT", [C, C], F32, kind="Internal").ap()
    s_gateT = nc.dram_tensor("s_gateT", [C, IF], F32, kind="Internal").ap()
    s_upT = nc.dram_tensor("s_upT", [C, IF], F32, kind="Internal").ap()
    s_downT = nc.dram_tensor("s_downT", [IF, C], F32, kind="Internal").ap()
    if debug:
        d_ctx = nc.dram_tensor("d_ctx", [128, 512], F32, kind="ExternalOutput").ap()
        d_rs = nc.dram_tensor("d_rs", [128, 8], F32, kind="ExternalOutput").ap()
        d_G = nc.dram_tensor("d_G", [128, 8192], F32, kind="ExternalOutput").ap()
        d_kfm = nc.dram_tensor("d_kfm", [128, 2048], F32, kind="ExternalOutput").ap()
        d_xnf = nc.dram_tensor("d_xnf", [128, 2048], F32, kind="ExternalOutput").ap()
        d_xq = nc.dram_tensor("d_xq", [128, 1024], F32, kind="ExternalOutput").ap()
        d_stat = nc.dram_tensor("d_stat", [128, 32], F32, kind="ExternalOutput").ap()
        d_qu = nc.dram_tensor("d_qu", [128, 512], F32, kind="ExternalOutput").ap()
        d_qn = nc.dram_tensor("d_qn", [128, 512], F32, kind="ExternalOutput").ap()
        d_qsg = nc.dram_tensor("d_qsg", [128, 512], F32, kind="ExternalOutput").ap()
        d_qsc = nc.dram_tensor("d_qsc", [128, 96], F32, kind="ExternalOutput").ap()
        d_xn0 = nc.dram_tensor("d_xn0", [128, 1024], F32, kind="ExternalOutput").ap()
        d_rn = [nc.dram_tensor(f"d_rn{i}", [128, 1], F32, kind="ExternalOutput").ap()
                for i in range(4)]

    S = Sched()

    with ExitStack() as stk:
        def sbt(name, shape, dt=F32):
            return stk.enter_context(nc.sbuf_tensor(name, shape, dt))

        XNF = sbt("XNF", [128, 16384])
        XQF = sbt("XQF", [128, 16384])
        WBIG = sbt("WBIG", [128, 8192])
        xin = sbt("xin", [128, 2048])
        tA = sbt("tA", [128, 1024])
        tB = sbt("tB", [128, 1024])
        qu = sbt("qu", [128, 384])
        qn = sbt("qn", [128, 384])
        qm = sbt("qm", [128, 384])
        qsg = sbt("qsg", [128, 384])
        hg = sbt("hg", [128, 384])
        hh = sbt("hh", [128, 384])
        hqt = sbt("hqt", [128, 384])
        hqT = sbt("hqT", [128, 384])
        wstr = sbt("wstr", [128, 1024])
        outst = sbt("outst", [128, 2048])
        ctxsb = outst
        stat = sbt("stat", [128, 64])
        rsdP = [sbt("rsd0", [128, 1]), sbt("rsd1", [128, 1])]
        nbP = [sbt("nb0", [128, 1]), sbt("nb1", [128, 1])]
        qsc = sbt("qsc", [128, 128])
        rs_all = sbt("rs_all", [128, 8])
        zeros1 = sbt("zeros1", [128, 1])
        zrow = sbt("zrow", [1, 128])
        ident = sbt("ident", [128, 128])
        ones1 = sbt("ones1", [1, 128])
        brow = sbt("brow", [1, 1024])
        P = [stk.enter_context(nc.psum_tensor(f"P{i}", [128, 512], F32))
             for i in range(8)]
        sem = {s: stk.enter_context(nc.semaphore(s))
               for s in ("pl0", "pl1", "xl0", "xl1", "lw", "lu", "lb", "ld2",
                         "st0", "st1", "st2", "st3", "so0", "so1",
                         "gx", "ga", "gb", "pe", "vv", "ss", "gg")}
        block = stk.enter_context(nc.Block())

        x3 = lambda ap: ap.rearrange("p (b k) -> p b k", k=16)

        def bc16(ap, nb):
            return ap.unsqueeze(2).broadcast_to([128, nb, 16])

        def ld_dma(semn, dst, src, waits=()):
            return S.add("sync", lambda e, d=dst, s=src: e.dma_start(out=d, in_=s),
                         waits=waits, inc=semn, inc_by=16)

        def st_dma(semn, dst, src, waits=()):
            return S.add("sync", lambda e, d=dst, s=src: e.dma_start(out=d, in_=s),
                         waits=waits, inc=semn, inc_by=16)

        def ld_dma_g(semn, dst, src, waits=()):
            return S.add("gpsimd",
                         lambda e, d=dst, s=src: e.dma_start(out=d, in_=s),
                         waits=waits, inc=semn, inc_by=16)

        def gd_accum(semn, dst, src, waits=()):
            return S.add("gpsimd",
                         lambda e, d=dst, s=src: e.dma_start(out=d, in_=s,
                                                             accum_op=ALU.add),
                         waits=waits, inc=semn, inc_by=16)

        def v_op(fn, waits=(), inc=True, drain=False):
            return S.add("vector", fn, waits=waits, inc="vv" if inc else None,
                         drain_after=drain)

        def s_op(fn, waits=(), inc=True):
            return S.add("scalar", fn, waits=waits, inc="ss" if inc else None)

        qsg_busy = [0]

        def emit_quant(src, dst, w, src_ready, dst_free=(), bufs=None):
            nb = w // 16
            bm = qsc[:, 0:nb]
            bm64, rc64, s264 = qsc[:, 0:64], qsc[:, 64:128], qsc[:, 0:64]
            bu, bg, bm2, bsg = bufs if bufs is not None else (qu, qn, qm, qsg)
            sg = bsg[:, 0:w]
            u = bu[:, 0:w]
            gg_ = bg[:, 0:w]
            xv = lambda ap: ap.rearrange("p (b k) -> p b k", k=16)
            ss_sign = s_op(lambda e, s=src, sg=sg: e.sign(out=sg, in_=s),
                           waits=list(src_ready) + [("vv", qsg_busy[0])])
            v_op(lambda e, s=src, bm=bm: e.reduce_max(
                out=bm, in_=xv(s), axis=AX.X, apply_absolute_value=True),
                waits=src_ready, inc=False, drain=True)
            v_op(lambda e: e.reciprocal(out=rc64, in_=bm64), inc=False, drain=True)
            v_op(lambda e: e.tensor_scalar(out=rc64, in0=rc64, scalar1=12.0,
                                           scalar2=2e12, op0=ALU.mult,
                                           op1=ALU.min), inc=False, drain=True)
            v_op(lambda e: e.reciprocal(out=s264, in_=rc64), inc=False, drain=True)
            v_op(lambda e, s=src, u=u: e.scalar_tensor_tensor(
                out=u, in0=s, scalar=-1.0, in1=s, op0=ALU.mult, op1=ALU.max),
                inc=False)
            v_op(lambda e, u=u, nb=nb: e.tensor_tensor(
                out=xv(u), in0=xv(u), in1=bc16(rc64[:, 0:nb], nb), op=ALU.mult),
                inc=False)
            m2 = bm2[:, 0:w]
            v_op(lambda e, u=u, gg_=gg_: e.tensor_scalar(
                out=gg_, in0=u, scalar1=0.5, scalar2=None, op0=ALU.is_gt),
                inc=False)
            for th in (1.5, 2.5, 3.5):
                v_op(lambda e, th=th, u=u, gg_=gg_: e.scalar_tensor_tensor(
                    out=gg_, in0=u, scalar=th, in1=gg_, op0=ALU.is_gt,
                    op1=ALU.add), inc=False)
            v_op(lambda e, u=u, m2=m2: e.tensor_scalar(
                out=m2, in0=u, scalar1=5.0, scalar2=None, op0=ALU.is_gt),
                inc=False)
            v_op(lambda e, u=u, m2=m2: e.scalar_tensor_tensor(
                out=m2, in0=u, scalar=7.0, in1=m2, op0=ALU.is_gt, op1=ALU.add),
                inc=False)
            v_op(lambda e, m2=m2, gg_=gg_: e.scalar_tensor_tensor(
                out=gg_, in0=m2, scalar=2.0, in1=gg_, op0=ALU.mult, op1=ALU.add),
                inc=False)
            v_op(lambda e, u=u, m2=m2: e.tensor_scalar(
                out=m2, in0=u, scalar1=10.0, scalar2=4.0, op0=ALU.is_gt,
                op1=ALU.mult), inc=False)
            v_op(lambda e, m2=m2, gg_=gg_: e.tensor_tensor(
                out=gg_, in0=gg_, in1=m2, op=ALU.add), inc=False)
            v_op(lambda e, sg=sg, nb=nb: e.scalar_tensor_tensor(
                out=xv(sg), in0=xv(sg), scalar=0.0, in1=bc16(s264[:, 0:nb], nb),
                op0=ALU.add, op1=ALU.mult),
                waits=[("ss", ss_sign)], inc=False)
            vv_fin = v_op(lambda e, d=dst, gg_=gg_, sg=sg: e.tensor_tensor(
                out=d, in0=gg_, in1=sg, op=ALU.mult), waits=dst_free)
            qsg_busy[0] = vv_fin
            return vv_fin, ss_sign

        def emit_quant_multi(src, dst, cols, src_ready, dst_free=(),
                             bufs=None, maxw=512):
            off, last = 0, (0, 0)
            while off < cols:
                w = min(maxw, cols - off)
                last = emit_quant(src[:, off:off + w], dst[:, off:off + w], w,
                                  src_ready, dst_free, bufs=bufs)
                off += w
            return last

        gg_ms = S.add("gpsimd", lambda e: e.memset(ident[:], 0.0), inc="gg")
        gg_id = S.add("gpsimd", lambda e: e.affine_select(
            out=ident[:], in_=ident[:], compare_op=ALU.not_equal, fill=1.0,
            base=0, pattern=[[-1, 128]], channel_multiplier=1),
            waits=[("gg", gg_ms)], inc="gg")
        gg_ones = S.add("gpsimd", lambda e: e.memset(ones1[:], 1.0), inc="gg")
        gg_zrow = S.add("gpsimd", lambda e: e.memset(zrow[:], 0.0), inc="gg")
        ld_brow = ld_dma("lb", brow[0:1, :], bproj.rearrange("(a c) -> a c", a=1))
        v_zero = v_op(lambda e: e.memset(zeros1[:], 0.0))
        v_op(lambda e: e.memset(qsc[:], 1.0), drain=True)

        rawsl = [XNF[:, 0:2816], XNF[:, 2816:5632]]
        wqsl = [XNF[:, 5632:8448], XNF[:, 8448:11264]]
        evsl = [XNF[:, 11264:11776], XNF[:, 11776:12288],
                XNF[:, 12288:12800], XNF[:, 12800:13312]]
        prep_bufs = (XNF[:, 13312:14336], XNF[:, 14336:15360],
                     XNF[:, 15360:16384], XQF[:, 0:1024])
        ev_free = [("st0", 0), ("st1", 0), ("st2", 0), ("st3", 0)]
        bank_ev = [0, 0]
        slot_free = [[], []]

        def prep_slab(src_hbm, rows, cols, dst_hbm, quant):
            nonlocal ev_free, bank_ev, slot_free
            nchunk = rows // 128
            ncc = cols // 128
            ei = 0
            for ch in range(nchunk):
                sl = ch % 2
                raw = rawsl[sl][:, 0:cols]
                hc = cols // 2
                ld_r = 0
                for qh in range(2):
                    ld_r = ld_dma_g(
                        f"pl{sl}", raw[:, qh * hc:(qh + 1) * hc],
                        src_hbm[ch * 128:(ch + 1) * 128, qh * hc:(qh + 1) * hc],
                        waits=slot_free[sl] if qh == 0 else [])
                if quant:
                    wqt = wqsl[sl][:, 0:cols]
                    (qdone, _) = emit_quant_multi(
                        raw, wqt, cols, src_ready=[(f"pl{sl}", ld_r)],
                        dst_free=[("pe", S.val("pe"))],
                        bufs=prep_bufs, maxw=1024)
                    src_t, src_rdy = wqt, ("vv", qdone)
                    slot_free[sl] = [("vv", qdone)]
                else:
                    src_t, src_rdy = raw, (f"pl{sl}", ld_r)
                    slot_free[sl] = []
                for g in range((ncc + 3) // 4):
                    bank = P[g % 2]
                    n_in_g = min(4, ncc - g * 4)
                    last_pe = None
                    for j in range(n_in_g):
                        cc = g * 4 + j
                        w_ = [src_rdy]
                        if j == 0:
                            w_.append(("ss", bank_ev[g % 2]))
                            w_.append(("gg", gg_id))
                        last_pe = S.add(
                            "tensor",
                            lambda e, b=bank, j=j, s=src_t, cc=cc:
                            e.transpose(b[:, j * 128:(j + 1) * 128],
                                        s[:, cc * 128:(cc + 1) * 128], ident[:]),
                            waits=w_, inc="pe" if j == n_in_g - 1 else None)
                    es = evsl[ei % 4]
                    ss_ev = s_op(lambda e, b=bank, es=es, n=n_in_g:
                                 e.copy(out=es[:, 0:n * 128], in_=b[:, 0:n * 128]),
                                 waits=[("pe", last_pe), ev_free[ei % 4]])
                    bank_ev[g % 2] = ss_ev
                    last_st = 0
                    for j in range(n_in_g):
                        cc = g * 4 + j
                        last_st = st_dma(
                            f"st{ei % 4}",
                            dst_hbm[cc * 128:(cc + 1) * 128,
                                    ch * 128:(ch + 1) * 128],
                            es[:, j * 128:(j + 1) * 128], waits=[("ss", ss_ev)])
                    ev_free[ei % 4] = (f"st{ei % 4}", last_st)
                    ei += 1
                if not quant:
                    slot_free[sl] = [("pe", S.val("pe"))]
                else:
                    slot_free[sl].append(("pe", S.val("pe")))

        prep_slab(wqkv, 3 * C, C, s_qkvT, quant=False)
        prep_slab(wproj, C, C, s_projT, quant=False)
        prep_slab(wg_in, IF, C, s_gateT, quant=True)
        prep_slab(wd_in, C, IF, s_downT, quant=True)
        prep_pe = S.val("pe")
        prep_sts = [(f"st{i}", S.val(f"st{i}")) for i in range(4)]
        prep_ss, prep_vv = S.val("ss"), S.val("vv")

        LP_PIECES = [(ch, s) for ch in range(IF // 128) for s in range(3)]
        lp_state = {"i": 0,
                    "slot_free": [[("vv", prep_vv)], [("vv", prep_vv)]],
                    "ev_free": [], "p5_free": []}
        lp_slots = [hg, hh]
        lp_ev = outst[:, 512:896]

        lp_pend = {}

        def lp_back(p):
            qd, sl, ch, s, w = lp_pend.pop(p)
            nt = w // 128
            lastt = None
            for j in range(nt):
                w_ = [("vv", qd)] if j == 0 else []
                if j == 0:
                    w_ += lp_state["p5_free"]
                lastt = S.add("tensor", lambda e, j=j, sl=sl:
                              e.transpose(P[5][:, j * 128:(j + 1) * 128],
                                          lp_slots[sl][:, j * 128:(j + 1) * 128],
                                          ident[:]),
                              waits=w_, inc="pe" if j == nt - 1 else None)
            lp_state["slot_free"][sl] = [("pe", lastt)]
            ev = s_op(lambda e, nt=nt: e.copy(
                out=lp_ev[:, 0:nt * 128], in_=P[5][:, 0:nt * 128]),
                waits=[("pe", lastt)] + lp_state["ev_free"])
            lp_state["p5_free"] = [("ss", ev)]
            last_st = 0
            for j in range(nt):
                cc = s * 3 + j
                last_st = st_dma(
                    "st0",
                    s_upT[cc * 128:(cc + 1) * 128, ch * 128:(ch + 1) * 128],
                    lp_ev[:, j * 128:(j + 1) * 128], waits=[("ss", ev)])
            lp_state["ev_free"] = [("st0", last_st)]

        def pump_up(n):
            for _ in range(n):
                if lp_state["i"] >= len(LP_PIECES):
                    return
                p = lp_state["i"]
                lp_state["i"] += 1
                if p - 2 in lp_pend:
                    lp_back(p - 2)
                ch, s = LP_PIECES[p]
                w = 256 if s == 2 else 384
                off = s * 384
                sl = p % 2
                slot = lp_slots[sl][:, 0:w]
                ld_p = ld_dma_g(f"pl{sl}", slot,
                                wu_in[ch * 128:(ch + 1) * 128, off:off + w],
                                waits=lp_state["slot_free"][sl])
                (qd, _) = emit_quant(slot, slot, w,
                                     src_ready=[(f"pl{sl}", ld_p)])
                lp_pend[p] = (qd, sl, ch, s, w)

        def pump_flush():
            for p in sorted(lp_pend):
                lp_back(p)

        xnf = lambda kc: XNF[:, kc * 2048:(kc + 1) * 2048]
        xqtm = lambda tc: XQF[:, tc * 1024:(tc + 1) * 1024]
        xqf = xnf
        v_tm = WBIG[:, 0:4096]
        k_fm = WBIG[:, 4096:6144]
        kT = WBIG[:, 6144:8192]
        G_t = lambda rc: WBIG[:, rc * 1024:(rc + 1) * 1024]
        fW_g = lambda kc: WBIG[:, kc * 384:(kc + 1) * 384]
        fW_u = lambda kc: WBIG[:, 3072 + kc * 384:3072 + (kc + 1) * 384]
        fW_d = lambda sl: WBIG[:, 6144 + sl * 1024:6144 + (sl + 1) * 1024]
        wv_t = lambda kc: WBIG[:, 6144 + kc * 256:6144 + (kc + 1) * 256]
        wk_t = lambda kc: WBIG[:, 6144 + kc * 128:6144 + (kc + 1) * 128]
        FFN_P = [(0, 384), (384, 384), (768, 384), (1152, 384), (1536, 384),
                 (1920, 384), (2304, 384), (2688, 128)]
        NFP = len(FFN_P)

        pro_bufs = (wstr[:, 0:1024], tA[:, 0:1024], tB[:, 0:1024],
                    outst[:, 1024:2048])
        xin_t = [xin[:, 0:1024], xin[:, 1024:2048]]
        out_sl = [outst[:, 0:1024], outst[:, 1024:2048]]
        out_free = [[], []]

        for b in range(nbatch):
            if stage < 2:
                break
            if b == 0:
                xnf_guard = [("pe", prep_pe), ("ss", prep_ss),
                             ("vv", prep_vv)] + prep_sts
            else:
                xnf_guard = [("pe", S.val("pe")), ("ss", S.val("ss")),
                             ("vv", S.val("vv")), ("ga", S.val("ga")),
                             ("gb", S.val("gb"))]
            xin_free = [[], []]
            b_start_pe = S.val("pe")
            ctx_b_guard = [("vv", S.val("vv")), ("ss", S.val("ss"))]
            ev_tc = [0] * NTC
            wv_ready = 0
            for kcw in range(KC):
                w_ = []
                if kcw == 0:
                    w_ = [("pe", b_start_pe)]
                    if b == 0:
                        w_ += prep_sts
                wv_ready = ld_dma(
                    "lw", wv_t(kcw),
                    s_qkvT[kcw * 128:(kcw + 1) * 128, 2048:2048 + 256],
                    waits=w_)
            vps_ev = [("ss", 0), ("ss", 0)]
            for tc in range(NTC):
                sl = tc % 2
                xt = xin_t[sl]
                w0 = list(xnf_guard) if tc < 2 else xin_free[sl]
                ld_x = 0
                for qh in range(2):
                    ld_x = ld_dma(
                        f"xl{sl}", xt[:, qh * 512:(qh + 1) * 512],
                        x_in[b, tc * 128:(tc + 1) * 128, qh * 512:(qh + 1) * 512],
                        waits=w0 if qh == 0 else [])
                p8 = (tc % 2) * 8
                su, mn, vr, t1, ve = (
                    stat[:, p8:p8 + 1], stat[:, p8 + 1:p8 + 2],
                    stat[:, p8 + 2:p8 + 3], stat[:, p8 + 5:p8 + 6],
                    stat[:, p8 + 6:p8 + 7])
                rsd = rsdP[tc % 2][:]
                nb_ = nbP[tc % 2][:]
                s_op(lambda e, xt=xt, su=su, tc=tc: e.activation(
                    out=xqtm(tc), in_=xt, func=AF.Identity, accum_out=su),
                    waits=[(f"xl{sl}", ld_x)])
                ss_sq = s_op(lambda e, xt=xt, t1=t1, tc=tc: e.activation(
                    out=xqtm(tc), in_=xt, func=AF.Square, accum_out=t1),
                    waits=[(f"xl{sl}", ld_x)])
                v_op(lambda e, su=su, mn=mn: e.tensor_scalar(
                    out=mn, in0=su, scalar1=1.0 / C, scalar2=None, op0=ALU.mult),
                    waits=[("ss", ss_sq)], inc=False, drain=True)
                v_op(lambda e, vr=vr, t1=t1: e.tensor_scalar(
                    out=vr, in0=t1, scalar1=1.0 / C, scalar2=None, op0=ALU.mult),
                    waits=[("ss", ss_sq)], inc=False, drain=True)
                v_op(lambda e, mn=mn, t1=t1: e.tensor_scalar(
                    out=t1, in0=mn, scalar1=mn, scalar2=-1.0, op0=ALU.mult,
                    op1=ALU.mult), inc=False, drain=True)
                v_op(lambda e, vr=vr, t1=t1: e.tensor_tensor(
                    out=vr, in0=vr, in1=t1, op=ALU.add), inc=False, drain=True)
                vv_ve = v_op(lambda e, vr=vr, ve=ve: e.tensor_scalar(
                    out=ve, in0=vr, scalar1=1e-5, scalar2=None, op0=ALU.add),
                    drain=True)
                ss_sv = s_op(lambda e, ve=ve, t1=t1: e.activation(
                    out=t1, in_=ve, func=AF.Sqrt),
                    waits=[("vv", vv_ve)])
                v_op(lambda e, rsd=rsd, t1=t1: e.reciprocal(out=rsd, in_=t1),
                     waits=[("ss", ss_sv)], inc=False, drain=True)
                v_op(lambda e, rsd=rsd, t1=t1: e.tensor_tensor(
                    out=t1, in0=rsd, in1=rsd, op=ALU.mult), inc=False, drain=True)
                v_op(lambda e, ve=ve, t1=t1: e.tensor_tensor(
                    out=t1, in0=t1, in1=ve, op=ALU.mult), inc=False, drain=True)
                v_op(lambda e, t1=t1: e.tensor_scalar(
                    out=t1, in0=t1, scalar1=-0.5, scalar2=1.5, op0=ALU.mult,
                    op1=ALU.add), inc=False, drain=True)
                v_op(lambda e, rsd=rsd, t1=t1: e.tensor_tensor(
                    out=rsd, in0=rsd, in1=t1, op=ALU.mult), inc=False, drain=True)
                vv_nb = v_op(lambda e, mn=mn, rsd=rsd, nb_=nb_: e.tensor_scalar(
                    out=nb_, in0=mn, scalar1=rsd, scalar2=-1.0, op0=ALU.mult,
                    op1=ALU.mult), drain=True)
                ss_xn = s_op(lambda e, xt=xt, rsd=rsd, nb_=nb_: e.activation(
                    out=xt, in_=xt, func=AF.Identity, scale=rsd, bias=nb_),
                    waits=[("vv", vv_nb)])
                if debug and b == 0 and tc == 0:
                    (qd0, _) = emit_quant(xt[:, 0:512], xqtm(tc)[:, 0:512], 512,
                                          src_ready=[("ss", ss_xn)],
                                          bufs=pro_bufs)
                    dbg_st = 0
                    for dap, src_ in [(d_qu, qu[:]), (d_qn, qn[:]),
                                      (d_qsg, qsg[:]), (d_qsc, qsc[:, 0:96]),
                                      (d_xn0, xt)]:
                        dbg_st = st_dma("st1", dap, src_, waits=[("vv", qd0)])
                    (qd, qss) = emit_quant(xt[:, 512:1024], xqtm(tc)[:, 512:1024],
                                           512, src_ready=[("ss", ss_xn),
                                                           ("st1", dbg_st)],
                                           bufs=pro_bufs)
                else:
                    (qd, qss) = emit_quant_multi(xt, xqtm(tc), 1024,
                                                 src_ready=[("ss", ss_xn)],
                                                 bufs=pro_bufs, maxw=1024)
                last_tr = None
                for kc in range(KC):
                    w_ = [("ss", ss_xn)]
                    if kc == 0 and tc > 0:
                        w_.append(("ss", ev_tc[tc - 1]))
                    bankt = P[6 + kc // 4]
                    last_tr = S.add("tensor",
                                    lambda e, kc=kc, xt=xt, bankt=bankt:
                                    e.transpose(bankt[:, (kc % 4) * 128:
                                                      (kc % 4 + 1) * 128],
                                                xt[:, kc * 128:(kc + 1) * 128],
                                                ident[:]),
                                    waits=w_, inc="pe" if kc == KC - 1 else None)
                xin_free[sl] = [("pe", last_tr), ("vv", qd), ("ss", qss)]
                ev = 0
                for kc in range(KC):
                    ev = s_op(lambda e, kc=kc, tc=tc: e.copy(
                        out=xnf(kc)[:, tc * 128:(tc + 1) * 128],
                        in_=P[6 + kc // 4][:, (kc % 4) * 128:(kc % 4 + 1) * 128]),
                        waits=[("pe", last_tr)] if kc == 0 else [])
                ev_tc[tc] = ev
                bank = P[tc % 2]
                last = None
                for kc in range(KC):
                    w_ = [("lw", wv_ready)]
                    if kc == 0:
                        w_.append(("ss", ev_tc[tc]))
                        w_.append(vps_ev[tc % 2])
                    last = S.add("tensor", lambda e, bank=bank, kc=kc, tc=tc:
                                 e.matmul(bank[:, 0:256],
                                          xnf(kc)[:, tc * 128:(tc + 1) * 128],
                                          wv_t(kc),
                                          start=(kc == 0), stop=(kc == KC - 1)),
                                 waits=w_, inc="pe" if kc == KC - 1 else None)
                ev_v = s_op(lambda e, bank=bank, tc=tc: e.copy(
                    out=v_tm[:, tc * 256:(tc + 1) * 256], in_=bank[:, 0:256]),
                    waits=[("pe", last)])
                vps_ev[tc % 2] = ("ss", ev_v)
            prologue_ss = S.val("ss")
            prologue_vv = S.val("vv")

            if stage < 3:
                continue
            ctx_ev_prev = list(ctx_b_guard)
            for vg in range(4):
                vz = v_op(lambda e: e.memset(P[4][0:64, 128:512], 0.0),
                          waits=ctx_ev_prev)
                S.add("tensor", lambda e: e.matmul(
                    P[4][0:64, 0:128], ones1[0:1, 0:64], zrow[0:1, :],
                    start=True, stop=False),
                    waits=[("gg", gg_zrow), ("vv", vz)] + ctx_ev_prev,
                    inc="pe")
                if vg > 0:
                    wv_ready = 0
                    for kcw in range(KC):
                        wv_ready = ld_dma(
                            "lw", wv_t(kcw),
                            s_qkvT[kcw * 128:(kcw + 1) * 128,
                                   2048 + vg * 256:2048 + (vg + 1) * 256],
                            waits=[("pe", S.val("pe"))] if kcw == 0 else [])
                    vps_ev = [("ss", S.val("ss")), ("ss", S.val("ss"))]
                    for tc in range(NTC):
                        bank = P[tc % 2]
                        last = None
                        for kc in range(KC):
                            w_ = [("lw", wv_ready)]
                            if kc == 0:
                                w_.append(("ss", ev_tc[tc]))
                                w_.append(vps_ev[tc % 2])
                            last = S.add("tensor",
                                         lambda e, bank=bank, kc=kc, tc=tc:
                                         e.matmul(bank[:, 0:256],
                                                  xnf(kc)[:, tc * 128:(tc + 1) * 128],
                                                  wv_t(kc),
                                                  start=(kc == 0),
                                                  stop=(kc == KC - 1)),
                                         waits=w_,
                                         inc="pe" if kc == KC - 1 else None)
                        ev = s_op(lambda e, bank=bank, tc=tc: e.copy(
                            out=v_tm[:, tc * 256:(tc + 1) * 256],
                            in_=bank[:, 0:256]),
                            waits=[("pe", last)])
                        vps_ev[tc % 2] = ("ss", ev)
                    v_done_ss = S.val("ss")
                else:
                    v_done_ss = prologue_ss
                if b == 0:
                    pump_up(17)
                k_free_vv = 0
                for ks in range(2):
                    sc_idx = vg * 2 + ks
                    wk_ready = 0
                    for kcw in range(KC):
                        wk_ready = ld_dma(
                            "lw", wk_t(kcw),
                            s_qkvT[kcw * 128:(kcw + 1) * 128,
                                   1024 + sc_idx * 128:1024 + (sc_idx + 1) * 128],
                            waits=[("pe", S.val("pe"))] if kcw == 0 else [])
                    kps_ev = [("ss", S.val("ss")), ("ss", S.val("ss"))]
                    kexp_last = 0
                    for tq in range(4):
                        bank = P[2 + tq % 2]
                        last = None
                        for kc in range(KC):
                            w_ = [("lw", wk_ready)]
                            if kc == 0:
                                w_.append(kps_ev[tq % 2])
                            last = S.add("tensor", lambda e, bank=bank, kc=kc, tq=tq:
                                         e.matmul(bank[:],
                                                  wk_t(kc),
                                                  xnf(kc)[:, tq * 512:(tq + 1) * 512],
                                                  start=(kc == 0),
                                                  stop=(kc == KC - 1)),
                                         waits=w_, inc="pe" if kc == KC - 1 else None)
                        w_e = [("pe", last)]
                        kexp_last = s_op(
                            lambda e, bank=bank, tq=tq, sc_idx=sc_idx:
                            e.activation(
                                out=k_fm[:, tq * 512:(tq + 1) * 512], in_=bank[:],
                                func=AF.Exp,
                                accum_out=stat[:, 32 + sc_idx * 4 + tq:
                                               33 + sc_idx * 4 + tq]),
                            waits=w_e)
                        kps_ev[tq % 2] = ("ss", kexp_last)
                    kexp_ss = kexp_last
                    v_op(lambda e, sc_idx=sc_idx: e.reduce_sum(
                        out=rs_all[:, sc_idx:sc_idx + 1],
                        in_=stat[:, 32 + sc_idx * 4:36 + sc_idx * 4], axis=AX.X),
                        waits=[("ss", kexp_ss)], inc=False, drain=True)
                    k_free_vv = v_op(lambda e, sc_idx=sc_idx: e.reciprocal(
                        out=rs_all[:, sc_idx:sc_idx + 1],
                        in_=rs_all[:, sc_idx:sc_idx + 1]), drain=True)
                    tr_ev = ("ss", 0)
                    lastt = None
                    for g in range(4):
                        for j in range(4):
                            tcc = g * 4 + j
                            w_ = [("ss", kexp_ss)]
                            if j == 0:
                                w_.append(tr_ev)
                            lastt = S.add("tensor", lambda e, j=j, tcc=tcc:
                                          e.transpose(P[2][:, j * 128:(j + 1) * 128],
                                                      k_fm[:, tcc * 128:(tcc + 1) * 128],
                                                      ident[:]),
                                          waits=w_, inc="pe" if j == 3 else None)
                        evt = s_op(lambda e, g=g: e.copy(
                            out=kT[:, g * 512:(g + 1) * 512], in_=P[2][:]),
                            waits=[("pe", lastt)])
                        tr_ev = ("ss", evt)
                    kT_ss = S.val("ss")
                    for hl in range(2):
                        head_local = ks * 2 + hl
                        last = None
                        for tc in range(NTC):
                            w_ = []
                            if tc == 0:
                                w_ = [("ss", kT_ss), ("ss", v_done_ss),
                                      ("vv", vz)]
                            fin = (head_local == 3 and tc == NTC - 1)
                            last = S.add(
                                "tensor",
                                lambda e, tc=tc, head_local=head_local, fin=fin:
                                e.matmul(P[4][0:64, head_local * 128:
                                              head_local * 128 + 128],
                                         v_tm[:, tc * 256 + head_local * 64:
                                              tc * 256 + (head_local + 1) * 64],
                                         kT[:, tc * 128:(tc + 1) * 128],
                                         start=False, stop=fin),
                                waits=w_, inc="pe" if tc == NTC - 1 else None)
                        ctx_last_pe = last
                ev_c = 0
                for head_local in range(4):
                    hgl = vg * 4 + head_local
                    ev_c = s_op(lambda e, hgl=hgl, head_local=head_local: e.copy(
                        out=ctxsb[(hgl % 2) * 64:(hgl % 2) * 64 + 64,
                                  (hgl // 2) * 64:(hgl // 2) * 64 + 64],
                        in_=P[4][0:64, head_local * 128 + (head_local % 2) * 64:
                                 head_local * 128 + (head_local % 2) * 64 + 64]),
                        waits=[("pe", ctx_last_pe)] if head_local == 0 else [])
                ctx_ev_prev = [("ss", ev_c)]
            if b == 0:
                pump_up(len(LP_PIECES))
                pump_flush()
                out_free[0] = [("st0", S.val("st0"))]
            ctx_ss = S.val("ss")
            phase1_vv = S.val("vv")

            if stage < 4:
                continue
            gf_ev = ("ss", 0)
            g_last_pe = S.val("pe")
            for rc in range(8):
                for hh_ in range(2):
                    h_gl = rc * 2 + hh_
                    wp_ready = ld_dma(
                        "lw", wstr[hh_ * 64:hh_ * 64 + 64, 0:1024],
                        s_projT[h_gl * 64:(h_gl + 1) * 64, :],
                        waits=[("pe", g_last_pe)])
                    for half in range(2):
                        bank = P[4 + half]
                        w_ = [("lw", wp_ready), ("ss", ctx_ss)]
                        if hh_ == 0 and half == 0:
                            w_.append(gf_ev)
                        g_last_pe = S.add(
                            "tensor",
                            lambda e, bank=bank, hh_=hh_, h_gl=h_gl, half=half:
                            e.matmul(bank[hh_ * 64:(hh_ + 1) * 64, :],
                                     ctxsb[(h_gl % 2) * 64:(h_gl % 2) * 64 + 64,
                                           (h_gl // 2) * 64:(h_gl // 2) * 64 + 64],
                                     wstr[hh_ * 64:hh_ * 64 + 64,
                                          half * 512:half * 512 + 512],
                                     start=True, stop=True),
                            waits=w_,
                            inc="pe" if half == 1 else None)
                s_op(lambda e, rc=rc: e.activation(
                    out=G_t(rc)[:, 0:512], in_=P[4][:], func=AF.Identity,
                    scale=rs_all[:, rc:rc + 1], bias=zeros1[:]),
                    waits=[("pe", g_last_pe), ("vv", v_zero),
                           ("vv", phase1_vv)])
                ev2 = s_op(lambda e, rc=rc: e.activation(
                    out=G_t(rc)[:, 512:1024], in_=P[5][:], func=AF.Identity,
                    scale=rs_all[:, rc:rc + 1], bias=zeros1[:]))
                gf_ev = ("ss", ev2)
            g_ready_ss = S.val("ss")
            g_ready_pe = S.val("pe")
            if debug and b == 0:
                for name_ap, src in [(d_ctx, ctxsb[:]), (d_rs, rs_all[:]),
                                     (d_G, WBIG[:]), (d_kfm, k_fm),
                                     (d_xnf, xnf(0)), (d_xq, xqtm(0)),
                                     (d_stat, stat[:]),
                                     (d_rn[0], rsdP[0][:]),
                                     (d_rn[1], rsdP[1][:]),
                                     (d_rn[2], nbP[0][:]),
                                     (d_rn[3], nbP[1][:])]:
                    st_dma("st0", name_ap, src,
                           waits=[("ss", g_ready_ss), ("vv", S.val("vv"))])

            if stage < 5:
                continue
            wq_sl = [wstr[:, 0:512], wstr[:, 512:1024]]
            wq_free = [[("pe", g_ready_pe)], [("pe", g_ready_pe)]]
            xp2_free = [list(xin_free[0]), list(xin_free[1])]
            qpsum_ev = ("ss", g_ready_ss)
            ap_ev = ("ss", g_ready_ss)
            tr_ev2 = ("ss", g_ready_ss)
            qtr_prev = 0
            for pair in range(8):
                qmm_last = None
                for kc in range(KC):
                    for half in range(2):
                        wsl = (kc * 2 + half) % 2
                        ldq = 0
                        for qh in range(2):
                            ldq = ld_dma(
                                "lw", wq_sl[wsl][:, qh * 256:(qh + 1) * 256],
                                s_qkvT[kc * 128:(kc + 1) * 128,
                                       half * 512 + qh * 256:
                                       half * 512 + (qh + 1) * 256],
                                waits=wq_free[wsl] if qh == 0 else [])
                        for ti in range(2):
                            tc = pair * 2 + ti
                            bank = P[ti * 2 + half]
                            w_ = [("lw", ldq)]
                            if kc == 0:
                                w_.append(qpsum_ev)
                            qmm_last = S.add(
                                "tensor",
                                lambda e, bank=bank, kc=kc, tc=tc, half=half,
                                wsl=wsl:
                                e.matmul(bank[:],
                                         xnf(kc)[:, tc * 128:(tc + 1) * 128],
                                         wq_sl[wsl][:],
                                         start=(kc == 0), stop=(kc == KC - 1)),
                                waits=w_,
                                inc="pe" if ti == 1 else None)
                        wq_free[wsl] = [("pe", qmm_last)]
                for ti in range(2):
                    tc = pair * 2 + ti
                    s_op(lambda e, ti=ti: e.activation(
                        out=tA[:, 0:512], in_=P[ti * 2][:], func=AF.Exp),
                        waits=[("pe", qmm_last), ("vv", S.val("vv"))])
                    ss_exp = s_op(lambda e, ti=ti: e.activation(
                        out=tA[:, 512:1024], in_=P[ti * 2 + 1][:], func=AF.Exp))
                    if ti == 1:
                        qpsum_ev = ("ss", ss_exp)
                    v_op(lambda e: e.reduce_sum(
                        out=stat[:, 16:32],
                        in_=tA[:].rearrange("p (h d) -> p h d", d=64), axis=AX.X),
                        waits=[("ss", ss_exp)], inc=False, drain=True)
                    v_op(lambda e: e.reciprocal(out=stat[:, 16:32],
                                                in_=stat[:, 16:32]), inc=False,
                         drain=True)
                    vv_qs = v_op(lambda e: e.tensor_tensor(
                        out=tB[:].rearrange("p (h d) -> p h d", d=64),
                        in0=tA[:].rearrange("p (h d) -> p h d", d=64),
                        in1=stat[:, 16:32].unsqueeze(2).broadcast_to([128, 16, 64]),
                        op=ALU.mult), waits=[("pe", qtr_prev)])
                    for g in range(2):
                        bankt = P[6 + g]
                        lastq = None
                        for j in range(4):
                            kc = g * 4 + j
                            w_ = [("vv", vv_qs)]
                            if j == 0:
                                w_.append(tr_ev2)
                            lastq = S.add("tensor",
                                          lambda e, bankt=bankt, j=j, kc=kc:
                                          e.transpose(bankt[:, j * 128:(j + 1) * 128],
                                                      tB[:, kc * 128:(kc + 1) * 128],
                                                      ident[:]),
                                          waits=w_, inc="pe" if j == 3 else None)
                        ss_t = s_op(lambda e, g=g: e.copy(
                            out=tA[:, g * 512:(g + 1) * 512], in_=P[6 + g][:]),
                            waits=[("pe", lastq)])
                        tr_ev2 = ("ss", ss_t)
                        qtr_prev = lastq
                    qsf_ss = S.val("ss")
                    last_ap = None
                    for half in range(2):
                        bank = P[4 + half]
                        for kc in range(KC):
                            w_ = [("ss", qsf_ss)]
                            if kc == 0 and half == 0:
                                w_.append(ap_ev)
                            last_ap = S.add(
                                "tensor", lambda e, bank=bank, kc=kc, half=half:
                                e.matmul(bank[:], tA[:, kc * 128:(kc + 1) * 128],
                                         G_t(kc)[:, half * 512:(half + 1) * 512],
                                         start=(kc == 0), stop=False),
                                waits=w_, inc=None)
                        if "bias" in skip:
                            last_ap = S.add(
                                "tensor", lambda e, bank=bank, kc=KC - 1, half=half:
                                e.matmul(bank[:], tA[:, (KC - 1) * 128:KC * 128],
                                         G_t(KC - 1)[:, half * 512:(half + 1) * 512],
                                         start=False, stop=True),
                                waits=[], inc="pe")
                        else:
                            last_ap = S.add(
                                "tensor", lambda e, bank=bank, half=half:
                                e.matmul(bank[:], ones1[:],
                                         brow[0:1, half * 512:(half + 1) * 512],
                                         start=False, stop=True),
                                waits=[("lb", ld_brow), ("gg", gg_ones)], inc="pe")
                    osl = ti
                    ld_x2 = 0
                    for qh in range(2):
                        ld_x2 = ld_dma(
                            f"xl{ti}", xin_t[ti][:, qh * 512:(qh + 1) * 512],
                            x_in[b, tc * 128:(tc + 1) * 128,
                                 qh * 512:(qh + 1) * 512],
                            waits=xp2_free[ti] if qh == 0 else [])
                    v_op(lambda e, osl=osl, ti=ti: e.tensor_tensor(
                        out=out_sl[osl][:, 0:512], in0=P[4][:],
                        in1=xin_t[ti][:, 0:512], op=ALU.add),
                        waits=[("pe", last_ap), (f"xl{ti}", ld_x2)] + out_free[osl],
                        inc=False)
                    vv_o = v_op(lambda e, osl=osl, ti=ti: e.tensor_tensor(
                        out=out_sl[osl][:, 512:1024], in0=P[5][:],
                        in1=xin_t[ti][:, 512:1024], op=ALU.add))
                    ap_ev = ("vv", vv_o)
                    xp2_free[ti] = [("vv", vv_o)]
                    st_w = st_dma(f"so{osl}", out[b, tc * 128:(tc + 1) * 128, :],
                                  out_sl[osl], waits=[("vv", vv_o)])
                    out_free[osl] = [(f"so{osl}", st_w)]
            resid_gx = S.val("gx")

            if stage < 6:
                continue
            xq_guard = [("pe", S.val("pe")), ("ss", S.val("ss")),
                        ("vv", S.val("vv"))]
            prev_ev = 0
            last_tr = None
            for tc in range(NTC):
                for kc in range(KC):
                    w_ = [("ss", prologue_ss), ("vv", prologue_vv)]
                    if tc == 0 and kc == 0:
                        w_ += xq_guard
                    if tc % 4 == 0 and kc == 0:
                        w_.append(("ss", prev_ev))
                    last_tr = S.add("tensor", lambda e, kc=kc, tc=tc:
                                    e.transpose(P[kc][:, (tc % 4) * 128:(tc % 4 + 1) * 128],
                                                xqtm(tc)[:, kc * 128:(kc + 1) * 128],
                                                ident[:]),
                                    waits=w_, inc="pe" if kc == KC - 1 else None)
                if tc % 4 == 3:
                    ev = 0
                    for kc in range(KC):
                        ev = s_op(lambda e, kc=kc, tc=tc: e.copy(
                            out=xqf(kc)[:, (tc - 3) * 128:(tc + 1) * 128],
                            in_=P[kc][:, 0:512]),
                            waits=[("pe", last_tr)])
                    prev_ev = ev
            xq_fm_ss = S.val("ss")

            g_free = [("pe", S.val("pe")), ("ss", xq_fm_ss)]
            u_free = [("pe", S.val("pe")), ("st0", S.val("st0"))]
            dpsum_ev = ("vv", S.val("vv"))
            gate_ev = ("ss", S.val("ss"))
            up_ev = ("vv", S.val("vv"))
            hq_free = [("pe", S.val("pe")), ("pe", S.val("pe"))]
            hqT_busy = [[], []]
            ht_ev = [S.val("ss"), S.val("ss")]
            fwd_free = [[("pe", S.val("pe"))], [("pe", S.val("pe"))]]
            hqt2 = [hqt[:, 0:384], tA[:, 0:384]]
            hqT_sl = [hqT[:, 0:384], tB[:, 0:384]]
            for pair in range(8):
                qd_p = [None, None]
                tr_p = [None, None]

                def emit_downstage(dp, wdt_d, i0_d):
                    nonlocal fwd_free, hqT_busy
                    nj = wdt_d // 128
                    ht_last = 0
                    for ti in range(2):
                        ht_ev[ti] = s_op(lambda e, ti=ti, wdt_d=wdt_d: e.copy(
                            out=hqT_sl[ti][:, 0:wdt_d], in_=P[6 + ti][:, 0:wdt_d]),
                            waits=[("pe", tr_p[ti])] + hqT_busy[ti])
                        ht_last = ht_ev[ti]
                    dld = {}
                    for j in range(min(nj, 2)):
                        dld[j] = ld_dma_g("ld2", fW_d(j % 2),
                                          s_downT[i0_d + j * 128:i0_d + (j + 1) * 128, :],
                                          waits=fwd_free[j % 2])
                    if nj >= 2:
                        dld[0] = dld[1]
                    dmm = {}
                    last_d = None
                    for j in range(nj):
                        dsl = j % 2
                        if j >= 2:
                            dld[j] = ld_dma_g("ld2", fW_d(dsl),
                                              s_downT[i0_d + j * 128:i0_d + (j + 1) * 128, :],
                                              waits=[("pe", dmm[j - 2])])
                        for ti in range(2):
                            for half in range(2):
                                bank = P[2 + ti * 2 + half]
                                last_d = S.add(
                                    "tensor",
                                    lambda e, j=j, bank=bank, half=half, dp=dp,
                                    dsl=dsl, nj=nj, ti=ti, wdt_d=wdt_d:
                                    e.matmul(bank[:],
                                             hqT_sl[ti][:, j * 128:(j + 1) * 128],
                                             fW_d(dsl)[:, half * 512:(half + 1) * 512],
                                             start=(dp == 0 and j == 0),
                                             stop=(dp == NFP - 1 and j == nj - 1)),
                                    waits=[("ss", ht_last), ("ld2", dld[j])],
                                    inc="pe" if (ti == 1 and half == 1) else None)
                        dmm[j] = last_d
                        fwd_free[dsl] = [("pe", last_d)]
                    hqT_busy = [[("pe", last_d)], [("pe", last_d)]]
                    return last_d

                for pi, (i0, wdt) in enumerate(FFN_P):
                    wld_g = 0
                    for kc in range(KC):
                        wld_g = ld_dma("lw", fW_g(kc)[:, 0:wdt],
                                       s_gateT[kc * 128:(kc + 1) * 128, i0:i0 + wdt],
                                       waits=g_free if kc == 0 else [])
                    wld_u = 0
                    for kc in range(KC):
                        wld_u = ld_dma("lu", fW_u(kc)[:, 0:wdt],
                                       s_upT[kc * 128:(kc + 1) * 128, i0:i0 + wdt],
                                       waits=u_free if kc == 0 else [])
                    lastu_all = None
                    lastg_all = None
                    for ti in range(2):
                        tc = pair * 2 + ti
                        lastg = lastu = None
                        for kc in range(KC):
                            w_ = [("lw", wld_g), ("ss", xq_fm_ss)]
                            if kc == 0:
                                w_.append(gate_ev)
                                if pi == 0 and ti == 0:
                                    w_.append(dpsum_ev)
                            lastg = S.add("tensor", lambda e, kc=kc, tc=tc, wdt=wdt:
                                          e.matmul(P[0][:, 0:wdt],
                                                   xqf(kc)[:, tc * 128:(tc + 1) * 128],
                                                   fW_g(kc)[:, 0:wdt],
                                                   start=(kc == 0), stop=(kc == KC - 1)),
                                          waits=w_, inc="pe" if kc == KC - 1 else None)
                        ss_g = s_op(lambda e, wdt=wdt: e.activation(
                            out=hg[:, 0:wdt], in_=P[0][:, 0:wdt], func=AF.Silu),
                            waits=[("pe", lastg)])
                        gate_ev = ("ss", ss_g)
                        if pi > 0 and qd_p[ti] is not None:
                            wdt_prev = FFN_P[pi - 1][1]
                            lastt = None
                            for j in range(wdt_prev // 128):
                                w_ = [("vv", qd_p[ti])]
                                if j == 0:
                                    w_.append(("ss", ht_ev[ti]))
                                lastt = S.add(
                                    "tensor", lambda e, j=j, ti=ti:
                                    e.transpose(P[6 + ti][:, j * 128:(j + 1) * 128],
                                                hqt2[ti][:, j * 128:(j + 1) * 128],
                                                ident[:]),
                                    waits=w_,
                                    inc="pe" if j == wdt_prev // 128 - 1 else None)
                            tr_p[ti] = lastt
                            hq_free[ti] = ("pe", lastt)
                        for kc in range(KC):
                            w_ = [("lu", wld_u), up_ev] if kc == 0 else []
                            lastu = S.add("tensor", lambda e, kc=kc, tc=tc, wdt=wdt:
                                          e.matmul(P[1][:, 0:wdt],
                                                   xqf(kc)[:, tc * 128:(tc + 1) * 128],
                                                   fW_u(kc)[:, 0:wdt],
                                                   start=(kc == 0), stop=(kc == KC - 1)),
                                          waits=w_, inc="pe" if kc == KC - 1 else None)
                        lastu_all = lastu
                        lastg_all = lastg
                        vv_h = v_op(lambda e, wdt=wdt: e.tensor_tensor(
                            out=hh[:, 0:wdt], in0=hg[:, 0:wdt], in1=P[1][:, 0:wdt],
                            op=ALU.mult), waits=[("ss", ss_g), ("pe", lastu)])
                        up_ev = ("vv", vv_h)
                        (qd, _) = emit_quant(hh[:, 0:wdt], hqt2[ti][:, 0:wdt], wdt,
                                             src_ready=[("vv", vv_h)],
                                             dst_free=[hq_free[ti]])
                        qd_prev_ti = qd_p[ti]
                        qd_p[ti] = qd
                    g_free = [("pe", lastg_all)]
                    u_free = [("pe", lastu_all)]
                    if pi > 0:
                        emit_downstage(pi - 1, FFN_P[pi - 1][1], FFN_P[pi - 1][0])
                wdt_l, i0_l = FFN_P[NFP - 1][1], FFN_P[NFP - 1][0]
                for ti in range(2):
                    lastt = None
                    for j in range(wdt_l // 128):
                        w_ = [("vv", qd_p[ti])]
                        lastt = S.add("tensor", lambda e, j=j, ti=ti:
                                      e.transpose(P[6 + ti][:, j * 128:(j + 1) * 128],
                                                  hqt2[ti][:, j * 128:(j + 1) * 128],
                                                  ident[:]),
                                      waits=w_,
                                      inc="pe" if j == wdt_l // 128 - 1 else None)
                    tr_p[ti] = lastt
                    hq_free[ti] = ("pe", lastt)
                emit_downstage(NFP - 1, wdt_l, i0_l)
                dd = S.val("vv")
                for ti in range(2):
                    tc = pair * 2 + ti
                    osl = ti
                    v_op(lambda e, osl=osl, ti=ti: e.tensor_copy(
                        out=out_sl[osl][:, 0:512], in_=P[2 + ti * 2][:]),
                        waits=[("pe", S.val("pe"))] + out_free[osl], inc=False)
                    vv_e = v_op(lambda e, osl=osl, ti=ti: e.tensor_copy(
                        out=out_sl[osl][:, 512:1024], in_=P[2 + ti * 2 + 1][:]))
                    gsem = "ga" if osl == 0 else "gb"
                    gda = gd_accum(gsem, out[b, tc * 128:(tc + 1) * 128, :],
                                   out_sl[osl],
                                   waits=[("vv", vv_e), ("gx", resid_gx),
                                          (f"so{osl}", S.val(f"so{osl}"))])
                    out_free[osl] = [(gsem, gda)]
                    dd = vv_e
                dpsum_ev = ("vv", dd)

        for sname in ("gx", "ga", "gb"):
            S.add("gpsimd", lambda e, sname=sname: e.wait_ge(sem[sname], S.cnt[sname]))
        for sname in ("so0", "so1", "gx", "ga", "gb", "st0", "st1", "st2", "st3"):
            S.add("sync", lambda e, sname=sname: e.wait_ge(sem[sname], S.cnt[sname]))

        def run_items(eng_obj, name):
            for waits, fn, inc, inc_by, drain_after in S.items[name]:
                for (s_name, val) in waits:
                    if val:
                        eng_obj.wait_ge(sem[s_name], val)
                inst = fn(eng_obj) if fn is not None else None
                if inc is not None and inst is not None:
                    inst.then_inc(sem[inc], inc_by)
                if drain_after:
                    eng_obj.drain()

        @block.sync
        def _(e):
            run_items(e, "sync")

        @block.gpsimd
        def _(e):
            run_items(e, "gpsimd")

        @block.scalar
        def _(e):
            run_items(e, "scalar")

        @block.vector
        def _(e):
            run_items(e, "vector")

        @block.tensor
        def _(e):
            run_items(e, "tensor")

    return nc


def kernel(L0_attn_x, L0_attn_w_qkv, L0_attn_w_proj, L0_attn_b_proj,
           L0_ffn_w_gate, L0_ffn_w_up, L0_ffn_w_down):
    x = np.ascontiguousarray(np.asarray(L0_attn_x, dtype=np.float32))
    wqkv = np.ascontiguousarray(np.asarray(L0_attn_w_qkv, dtype=np.float32))
    wproj = np.ascontiguousarray(np.asarray(L0_attn_w_proj, dtype=np.float32))
    bproj = np.ascontiguousarray(np.asarray(L0_attn_b_proj, dtype=np.float32))
    wg = np.ascontiguousarray(np.asarray(L0_ffn_w_gate, dtype=np.float32))
    wu = np.ascontiguousarray(np.asarray(L0_ffn_w_up, dtype=np.float32))
    wd = np.ascontiguousarray(np.asarray(L0_ffn_w_down, dtype=np.float32))

    nc = build_core_program()
    in_maps = [{"x": x[2 * c:2 * c + 2], "wqkv": wqkv, "wproj": wproj,
                "bproj": bproj, "wg": wg, "wu": wu, "wd": wd}
               for c in range(NCORES)]
    res = run_bass_kernel_spmd(nc, in_maps, core_ids=list(range(NCORES)))
    return np.concatenate([r["out"] for r in res.results], axis=0)


# version 17
# speedup vs baseline: 1.1914x; 1.0393x over previous
import numpy as np
import concourse.bass as bass
import concourse.mybir as mybir
from concourse.bass_utils import run_bass_kernel_spmd
from contextlib import ExitStack


F32 = mybir.dt.float32
AF = mybir.ActivationFunctionType
ALU = mybir.AluOpType
AX = mybir.AxisListType

NCORES = 8
B_CORE = 2
N = 2048
C = 1024
H = 16
DH = 64
IF = 2816
NTC = N // 128
KC = C // 128
FFN_PASSES = [(i * 256, 256) for i in range(11)]

ENGS = ("sync", "gpsimd", "scalar", "vector", "tensor")


class Sched:
    def __init__(self):
        self.items = {e: [] for e in ENGS}
        self.cnt = {s: 0 for s in
                    ("pl0", "pl1", "xl0", "xl1", "lw", "lu", "lb", "ld2",
                     "st0", "st1", "st2", "st3", "so0", "so1",
                     "gx", "ga", "gb", "pe", "vv", "ss", "gg")}

    def add(self, eng, fn, waits=(), inc=None, inc_by=1, drain_after=False):
        self.items[eng].append((tuple(waits), fn, inc, inc_by, drain_after))
        if inc is not None:
            self.cnt[inc] += inc_by
            return self.cnt[inc]
        return None

    def val(self, sem):
        return self.cnt[sem]


def build_core_program(stage=7, nbatch=B_CORE, skip=(), debug=False):
    nc = bass.Bass("TRN2", target_bir_lowering=False,
                   detect_race_conditions=False)

    x_in = nc.dram_tensor("x", [B_CORE, N, C], F32, kind="ExternalInput").ap()
    wqkv = nc.dram_tensor("wqkv", [3 * C, C], F32, kind="ExternalInput").ap()
    wproj = nc.dram_tensor("wproj", [C, C], F32, kind="ExternalInput").ap()
    bproj = nc.dram_tensor("bproj", [C], F32, kind="ExternalInput").ap()
    wg_in = nc.dram_tensor("wg", [IF, C], F32, kind="ExternalInput").ap()
    wu_in = nc.dram_tensor("wu", [IF, C], F32, kind="ExternalInput").ap()
    wd_in = nc.dram_tensor("wd", [C, IF], F32, kind="ExternalInput").ap()
    out = nc.dram_tensor("out", [B_CORE, N, C], F32, kind="ExternalOutput").ap()
    s_qkvT = nc.dram_tensor("s_qkvT", [C, 3 * C], F32, kind="Internal").ap()
    s_projT = nc.dram_tensor("s_projT", [C, C], F32, kind="Internal").ap()
    s_gateT = nc.dram_tensor("s_gateT", [C, IF], F32, kind="Internal").ap()
    s_upT = nc.dram_tensor("s_upT", [C, IF], F32, kind="Internal").ap()
    s_downT = nc.dram_tensor("s_downT", [IF, C], F32, kind="Internal").ap()
    if debug:
        d_ctx = nc.dram_tensor("d_ctx", [128, 512], F32, kind="ExternalOutput").ap()
        d_rs = nc.dram_tensor("d_rs", [128, 8], F32, kind="ExternalOutput").ap()
        d_G = nc.dram_tensor("d_G", [128, 8192], F32, kind="ExternalOutput").ap()
        d_kfm = nc.dram_tensor("d_kfm", [128, 2048], F32, kind="ExternalOutput").ap()
        d_xnf = nc.dram_tensor("d_xnf", [128, 2048], F32, kind="ExternalOutput").ap()
        d_xq = nc.dram_tensor("d_xq", [128, 1024], F32, kind="ExternalOutput").ap()
        d_stat = nc.dram_tensor("d_stat", [128, 32], F32, kind="ExternalOutput").ap()
        d_qu = nc.dram_tensor("d_qu", [128, 512], F32, kind="ExternalOutput").ap()
        d_qn = nc.dram_tensor("d_qn", [128, 512], F32, kind="ExternalOutput").ap()
        d_qsg = nc.dram_tensor("d_qsg", [128, 512], F32, kind="ExternalOutput").ap()
        d_qsc = nc.dram_tensor("d_qsc", [128, 96], F32, kind="ExternalOutput").ap()
        d_xn0 = nc.dram_tensor("d_xn0", [128, 1024], F32, kind="ExternalOutput").ap()
        d_rn = [nc.dram_tensor(f"d_rn{i}", [128, 1], F32, kind="ExternalOutput").ap()
                for i in range(4)]

    S = Sched()

    with ExitStack() as stk:
        def sbt(name, shape, dt=F32):
            return stk.enter_context(nc.sbuf_tensor(name, shape, dt))

        XNF = sbt("XNF", [128, 16384])
        XQF = sbt("XQF", [128, 16384])
        WBIG = sbt("WBIG", [128, 8192])
        xin = sbt("xin", [128, 2048])
        tA = sbt("tA", [128, 1024])
        tB = sbt("tB", [128, 1024])
        qu = sbt("qu", [128, 384])
        qn = sbt("qn", [128, 384])
        qm = sbt("qm", [128, 384])
        qsg = sbt("qsg", [128, 384])
        hg = sbt("hg", [128, 384])
        hh = sbt("hh", [128, 384])
        hqt = sbt("hqt", [128, 384])
        hqT = sbt("hqT", [128, 384])
        wstr = sbt("wstr", [128, 1024])
        outst = sbt("outst", [128, 2048])
        ctxsb = outst
        stat = sbt("stat", [128, 64])
        rsdP = [sbt("rsd0", [128, 1]), sbt("rsd1", [128, 1])]
        nbP = [sbt("nb0", [128, 1]), sbt("nb1", [128, 1])]
        qsc = sbt("qsc", [128, 128])
        rs_all = sbt("rs_all", [128, 8])
        zeros1 = sbt("zeros1", [128, 1])
        zrow = sbt("zrow", [1, 128])
        ident = sbt("ident", [128, 128])
        ones1 = sbt("ones1", [1, 128])
        brow = sbt("brow", [1, 1024])
        P = [stk.enter_context(nc.psum_tensor(f"P{i}", [128, 512], F32))
             for i in range(8)]
        sem = {s: stk.enter_context(nc.semaphore(s))
               for s in ("pl0", "pl1", "xl0", "xl1", "lw", "lu", "lb", "ld2",
                         "st0", "st1", "st2", "st3", "so0", "so1",
                         "gx", "ga", "gb", "pe", "vv", "ss", "gg")}
        block = stk.enter_context(nc.Block())

        x3 = lambda ap: ap.rearrange("p (b k) -> p b k", k=16)

        def bc16(ap, nb):
            return ap.unsqueeze(2).broadcast_to([128, nb, 16])

        def ld_dma(semn, dst, src, waits=()):
            return S.add("sync", lambda e, d=dst, s=src: e.dma_start(out=d, in_=s),
                         waits=waits, inc=semn, inc_by=16)

        def st_dma(semn, dst, src, waits=()):
            return S.add("sync", lambda e, d=dst, s=src: e.dma_start(out=d, in_=s),
                         waits=waits, inc=semn, inc_by=16)

        def ld_dma_g(semn, dst, src, waits=()):
            return S.add("gpsimd",
                         lambda e, d=dst, s=src: e.dma_start(out=d, in_=s),
                         waits=waits, inc=semn, inc_by=16)

        def gd_accum(semn, dst, src, waits=()):
            return S.add("gpsimd",
                         lambda e, d=dst, s=src: e.dma_start(out=d, in_=s,
                                                             accum_op=ALU.add),
                         waits=waits, inc=semn, inc_by=16)

        def v_op(fn, waits=(), inc=True, drain=False):
            return S.add("vector", fn, waits=waits, inc="vv" if inc else None,
                         drain_after=drain)

        def s_op(fn, waits=(), inc=True):
            return S.add("scalar", fn, waits=waits, inc="ss" if inc else None)

        qsg_busy = [0]

        def emit_quant(src, dst, w, src_ready, dst_free=(), bufs=None):
            nb = w // 16
            bm = qsc[:, 0:nb]
            bm64, rc64, s264 = qsc[:, 0:64], qsc[:, 64:128], qsc[:, 0:64]
            bu, bg, bm2, bsg = bufs if bufs is not None else (qu, qn, qm, qsg)
            sg = bsg[:, 0:w]
            u = bu[:, 0:w]
            gg_ = bg[:, 0:w]
            xv = lambda ap: ap.rearrange("p (b k) -> p b k", k=16)
            ss_sign = s_op(lambda e, s=src, sg=sg: e.sign(out=sg, in_=s),
                           waits=list(src_ready) + [("vv", qsg_busy[0])])
            v_op(lambda e, s=src, bm=bm: e.reduce_max(
                out=bm, in_=xv(s), axis=AX.X, apply_absolute_value=True),
                waits=src_ready, inc=False, drain=True)
            v_op(lambda e: e.reciprocal(out=rc64, in_=bm64), inc=False, drain=True)
            v_op(lambda e: e.tensor_scalar(out=rc64, in0=rc64, scalar1=12.0,
                                           scalar2=2e12, op0=ALU.mult,
                                           op1=ALU.min), inc=False, drain=True)
            v_op(lambda e: e.reciprocal(out=s264, in_=rc64), inc=False, drain=True)
            v_op(lambda e, s=src, u=u: e.scalar_tensor_tensor(
                out=u, in0=s, scalar=-1.0, in1=s, op0=ALU.mult, op1=ALU.max),
                inc=False)
            v_op(lambda e, u=u, nb=nb: e.tensor_tensor(
                out=xv(u), in0=xv(u), in1=bc16(rc64[:, 0:nb], nb), op=ALU.mult),
                inc=False)
            m2 = bm2[:, 0:w]
            v_op(lambda e, u=u, gg_=gg_: e.tensor_scalar(
                out=gg_, in0=u, scalar1=0.5, scalar2=None, op0=ALU.is_gt),
                inc=False)
            for th in (1.5, 2.5, 3.5):
                v_op(lambda e, th=th, u=u, gg_=gg_: e.scalar_tensor_tensor(
                    out=gg_, in0=u, scalar=th, in1=gg_, op0=ALU.is_gt,
                    op1=ALU.add), inc=False)
            v_op(lambda e, u=u, m2=m2: e.tensor_scalar(
                out=m2, in0=u, scalar1=5.0, scalar2=None, op0=ALU.is_gt),
                inc=False)
            v_op(lambda e, u=u, m2=m2: e.scalar_tensor_tensor(
                out=m2, in0=u, scalar=7.0, in1=m2, op0=ALU.is_gt, op1=ALU.add),
                inc=False)
            v_op(lambda e, m2=m2, gg_=gg_: e.scalar_tensor_tensor(
                out=gg_, in0=m2, scalar=2.0, in1=gg_, op0=ALU.mult, op1=ALU.add),
                inc=False)
            v_op(lambda e, u=u, m2=m2: e.tensor_scalar(
                out=m2, in0=u, scalar1=10.0, scalar2=4.0, op0=ALU.is_gt,
                op1=ALU.mult), inc=False)
            v_op(lambda e, m2=m2, gg_=gg_: e.tensor_tensor(
                out=gg_, in0=gg_, in1=m2, op=ALU.add), inc=False)
            v_op(lambda e, sg=sg, nb=nb: e.scalar_tensor_tensor(
                out=xv(sg), in0=xv(sg), scalar=0.0, in1=bc16(s264[:, 0:nb], nb),
                op0=ALU.add, op1=ALU.mult),
                waits=[("ss", ss_sign)], inc=False)
            vv_fin = v_op(lambda e, d=dst, gg_=gg_, sg=sg: e.tensor_tensor(
                out=d, in0=gg_, in1=sg, op=ALU.mult), waits=dst_free)
            qsg_busy[0] = vv_fin
            return vv_fin, ss_sign

        def emit_quant_multi(src, dst, cols, src_ready, dst_free=(),
                             bufs=None, maxw=512):
            off, last = 0, (0, 0)
            while off < cols:
                w = min(maxw, cols - off)
                last = emit_quant(src[:, off:off + w], dst[:, off:off + w], w,
                                  src_ready, dst_free, bufs=bufs)
                off += w
            return last

        gg_ms = S.add("gpsimd", lambda e: e.memset(ident[:], 0.0), inc="gg")
        gg_id = S.add("gpsimd", lambda e: e.affine_select(
            out=ident[:], in_=ident[:], compare_op=ALU.not_equal, fill=1.0,
            base=0, pattern=[[-1, 128]], channel_multiplier=1),
            waits=[("gg", gg_ms)], inc="gg")
        gg_ones = S.add("gpsimd", lambda e: e.memset(ones1[:], 1.0), inc="gg")
        gg_zrow = S.add("gpsimd", lambda e: e.memset(zrow[:], 0.0), inc="gg")
        ld_brow = ld_dma("lb", brow[0:1, :], bproj.rearrange("(a c) -> a c", a=1))
        v_zero = v_op(lambda e: e.memset(zeros1[:], 0.0))
        v_op(lambda e: e.memset(qsc[:], 1.0), drain=True)

        rawsl = [XNF[:, 0:2816], XNF[:, 2816:5632]]
        wqsl = [XNF[:, 5632:8448], XNF[:, 8448:11264]]
        evsl = [XNF[:, 11264:11776], XNF[:, 11776:12288],
                XNF[:, 12288:12800], XNF[:, 12800:13312]]
        prep_bufs = (XNF[:, 13312:14336], XNF[:, 14336:15360],
                     XNF[:, 15360:16384], XQF[:, 0:1024])
        ev_free = [("st0", 0), ("st1", 0), ("st2", 0), ("st3", 0)]
        bank_ev = [0, 0]
        slot_free = [[], []]

        def prep_slab(src_hbm, rows, cols, dst_hbm, quant):
            nonlocal ev_free, bank_ev, slot_free
            nchunk = rows // 128
            ncc = cols // 128
            ei = 0
            for ch in range(nchunk):
                sl = ch % 2
                raw = rawsl[sl][:, 0:cols]
                hc = cols // 2
                ld_r = 0
                for qh in range(2):
                    ld_r = ld_dma_g(
                        f"pl{sl}", raw[:, qh * hc:(qh + 1) * hc],
                        src_hbm[ch * 128:(ch + 1) * 128, qh * hc:(qh + 1) * hc],
                        waits=slot_free[sl] if qh == 0 else [])
                if quant:
                    wqt = wqsl[sl][:, 0:cols]
                    (qdone, _) = emit_quant_multi(
                        raw, wqt, cols, src_ready=[(f"pl{sl}", ld_r)],
                        dst_free=[("pe", S.val("pe"))],
                        bufs=prep_bufs, maxw=1024)
                    src_t, src_rdy = wqt, ("vv", qdone)
                    slot_free[sl] = [("vv", qdone)]
                else:
                    src_t, src_rdy = raw, (f"pl{sl}", ld_r)
                    slot_free[sl] = []
                for g in range((ncc + 3) // 4):
                    bank = P[g % 2]
                    n_in_g = min(4, ncc - g * 4)
                    last_pe = None
                    for j in range(n_in_g):
                        cc = g * 4 + j
                        w_ = [src_rdy]
                        if j == 0:
                            w_.append(("ss", bank_ev[g % 2]))
                            w_.append(("gg", gg_id))
                        last_pe = S.add(
                            "tensor",
                            lambda e, b=bank, j=j, s=src_t, cc=cc:
                            e.transpose(b[:, j * 128:(j + 1) * 128],
                                        s[:, cc * 128:(cc + 1) * 128], ident[:]),
                            waits=w_, inc="pe" if j == n_in_g - 1 else None)
                    es = evsl[ei % 4]
                    ss_ev = s_op(lambda e, b=bank, es=es, n=n_in_g:
                                 e.copy(out=es[:, 0:n * 128], in_=b[:, 0:n * 128]),
                                 waits=[("pe", last_pe), ev_free[ei % 4]])
                    bank_ev[g % 2] = ss_ev
                    last_st = 0
                    for j in range(n_in_g):
                        cc = g * 4 + j
                        last_st = st_dma(
                            f"st{ei % 4}",
                            dst_hbm[cc * 128:(cc + 1) * 128,
                                    ch * 128:(ch + 1) * 128],
                            es[:, j * 128:(j + 1) * 128], waits=[("ss", ss_ev)])
                    ev_free[ei % 4] = (f"st{ei % 4}", last_st)
                    ei += 1
                if not quant:
                    slot_free[sl] = [("pe", S.val("pe"))]
                else:
                    slot_free[sl].append(("pe", S.val("pe")))

        prep_slab(wqkv, 3 * C, C, s_qkvT, quant=False)
        prep_slab(wproj, C, C, s_projT, quant=False)
        prep_slab(wg_in, IF, C, s_gateT, quant=True)
        prep_slab(wd_in, C, IF, s_downT, quant=True)
        prep_pe = S.val("pe")
        prep_sts = [(f"st{i}", S.val(f"st{i}")) for i in range(4)]
        prep_ss, prep_vv = S.val("ss"), S.val("vv")

        LP_PIECES = [(ch, s) for ch in range(IF // 128) for s in range(3)]
        lp_state = {"i": 0,
                    "slot_free": [[("vv", prep_vv)], [("vv", prep_vv)]],
                    "ev_free": [], "p5_free": []}
        lp_slots = [hg, hh]
        lp_ev = outst[:, 512:896]

        lp_pend = {}

        def lp_back(p):
            qd, sl, ch, s, w = lp_pend.pop(p)
            nt = w // 128
            lastt = None
            for j in range(nt):
                w_ = [("vv", qd)] if j == 0 else []
                if j == 0:
                    w_ += lp_state["p5_free"]
                lastt = S.add("tensor", lambda e, j=j, sl=sl:
                              e.transpose(P[5][:, j * 128:(j + 1) * 128],
                                          lp_slots[sl][:, j * 128:(j + 1) * 128],
                                          ident[:]),
                              waits=w_, inc="pe" if j == nt - 1 else None)
            lp_state["slot_free"][sl] = [("pe", lastt)]
            ev = s_op(lambda e, nt=nt: e.copy(
                out=lp_ev[:, 0:nt * 128], in_=P[5][:, 0:nt * 128]),
                waits=[("pe", lastt)] + lp_state["ev_free"])
            lp_state["p5_free"] = [("ss", ev)]
            last_st = 0
            for j in range(nt):
                cc = s * 3 + j
                last_st = st_dma(
                    "st0",
                    s_upT[cc * 128:(cc + 1) * 128, ch * 128:(ch + 1) * 128],
                    lp_ev[:, j * 128:(j + 1) * 128], waits=[("ss", ev)])
            lp_state["ev_free"] = [("st0", last_st)]

        def pump_up(n):
            for _ in range(n):
                if lp_state["i"] >= len(LP_PIECES):
                    return
                p = lp_state["i"]
                lp_state["i"] += 1
                if p - 2 in lp_pend:
                    lp_back(p - 2)
                ch, s = LP_PIECES[p]
                w = 256 if s == 2 else 384
                off = s * 384
                sl = p % 2
                slot = lp_slots[sl][:, 0:w]
                ld_p = ld_dma_g(f"pl{sl}", slot,
                                wu_in[ch * 128:(ch + 1) * 128, off:off + w],
                                waits=lp_state["slot_free"][sl])
                (qd, _) = emit_quant(slot, slot, w,
                                     src_ready=[(f"pl{sl}", ld_p)])
                lp_pend[p] = (qd, sl, ch, s, w)

        def pump_flush():
            for p in sorted(lp_pend):
                lp_back(p)

        xnf = lambda kc: XNF[:, kc * 2048:(kc + 1) * 2048]
        xqtm = lambda tc: XQF[:, tc * 1024:(tc + 1) * 1024]
        xqf = xnf
        v_tm = WBIG[:, 0:4096]
        k_fm = WBIG[:, 4096:6144]
        kT = WBIG[:, 6144:8192]
        G_t = lambda rc: WBIG[:, rc * 1024:(rc + 1) * 1024]
        fW_g = lambda kc: WBIG[:, kc * 384:(kc + 1) * 384]
        fW_u = lambda kc: WBIG[:, 3072 + kc * 384:3072 + (kc + 1) * 384]
        fW_d = lambda sl: WBIG[:, 6144 + sl * 1024:6144 + (sl + 1) * 1024]
        wv_t = lambda kc: WBIG[:, 6144 + kc * 256:6144 + (kc + 1) * 256]
        wk_t = lambda kc: WBIG[:, 6144 + kc * 128:6144 + (kc + 1) * 128]
        FFN_P = [(0, 384), (384, 384), (768, 384), (1152, 384), (1536, 384),
                 (1920, 384), (2304, 384), (2688, 128)]
        NFP = len(FFN_P)

        pro_bufs = (wstr[:, 0:1024], tA[:, 0:1024], tB[:, 0:1024],
                    outst[:, 1024:2048])
        xin_t = [xin[:, 0:1024], xin[:, 1024:2048]]
        out_sl = [outst[:, 0:1024], outst[:, 1024:2048]]
        out_free = [[], []]

        for b in range(nbatch):
            if stage < 2:
                break
            if b == 0:
                xnf_guard = [("pe", prep_pe), ("ss", prep_ss),
                             ("vv", prep_vv)] + prep_sts
            else:
                xnf_guard = [("pe", S.val("pe")), ("ss", S.val("ss")),
                             ("vv", S.val("vv")), ("ga", S.val("ga")),
                             ("gb", S.val("gb"))]
            xin_free = [[], []]
            b_start_pe = S.val("pe")
            ctx_b_guard = [("vv", S.val("vv")), ("ss", S.val("ss"))]
            ev_tc = [0] * NTC
            wv_ready = 0
            for kcw in range(KC):
                w_ = []
                if kcw == 0:
                    w_ = [("pe", b_start_pe)]
                    if b == 0:
                        w_ += prep_sts
                wv_ready = ld_dma(
                    "lw", wv_t(kcw),
                    s_qkvT[kcw * 128:(kcw + 1) * 128, 2048:2048 + 256],
                    waits=w_)
            vps_ev = [("ss", 0), ("ss", 0)]
            for tc in range(NTC):
                sl = tc % 2
                xt = xin_t[sl]
                w0 = list(xnf_guard) if tc < 2 else xin_free[sl]
                ld_x = 0
                for qh in range(2):
                    ld_x = ld_dma(
                        f"xl{sl}", xt[:, qh * 512:(qh + 1) * 512],
                        x_in[b, tc * 128:(tc + 1) * 128, qh * 512:(qh + 1) * 512],
                        waits=w0 if qh == 0 else [])
                p8 = (tc % 2) * 8
                su, mn, vr, t1, ve = (
                    stat[:, p8:p8 + 1], stat[:, p8 + 1:p8 + 2],
                    stat[:, p8 + 2:p8 + 3], stat[:, p8 + 5:p8 + 6],
                    stat[:, p8 + 6:p8 + 7])
                rsd = rsdP[tc % 2][:]
                nb_ = nbP[tc % 2][:]
                s_op(lambda e, xt=xt, su=su, tc=tc: e.activation(
                    out=xqtm(tc), in_=xt, func=AF.Identity, accum_out=su),
                    waits=[(f"xl{sl}", ld_x)])
                ss_sq = s_op(lambda e, xt=xt, t1=t1, tc=tc: e.activation(
                    out=xqtm(tc), in_=xt, func=AF.Square, accum_out=t1),
                    waits=[(f"xl{sl}", ld_x)])
                v_op(lambda e, su=su, mn=mn: e.tensor_scalar(
                    out=mn, in0=su, scalar1=1.0 / C, scalar2=None, op0=ALU.mult),
                    waits=[("ss", ss_sq)], inc=False, drain=True)
                v_op(lambda e, vr=vr, t1=t1: e.tensor_scalar(
                    out=vr, in0=t1, scalar1=1.0 / C, scalar2=None, op0=ALU.mult),
                    waits=[("ss", ss_sq)], inc=False, drain=True)
                v_op(lambda e, mn=mn, t1=t1: e.tensor_scalar(
                    out=t1, in0=mn, scalar1=mn, scalar2=-1.0, op0=ALU.mult,
                    op1=ALU.mult), inc=False, drain=True)
                v_op(lambda e, vr=vr, t1=t1: e.tensor_tensor(
                    out=vr, in0=vr, in1=t1, op=ALU.add), inc=False, drain=True)
                vv_ve = v_op(lambda e, vr=vr, ve=ve: e.tensor_scalar(
                    out=ve, in0=vr, scalar1=1e-5, scalar2=None, op0=ALU.add),
                    drain=True)
                ss_sv = s_op(lambda e, ve=ve, t1=t1: e.activation(
                    out=t1, in_=ve, func=AF.Sqrt),
                    waits=[("vv", vv_ve)])
                v_op(lambda e, rsd=rsd, t1=t1: e.reciprocal(out=rsd, in_=t1),
                     waits=[("ss", ss_sv)], inc=False, drain=True)
                v_op(lambda e, rsd=rsd, t1=t1: e.tensor_tensor(
                    out=t1, in0=rsd, in1=rsd, op=ALU.mult), inc=False, drain=True)
                v_op(lambda e, ve=ve, t1=t1: e.tensor_tensor(
                    out=t1, in0=t1, in1=ve, op=ALU.mult), inc=False, drain=True)
                v_op(lambda e, t1=t1: e.tensor_scalar(
                    out=t1, in0=t1, scalar1=-0.5, scalar2=1.5, op0=ALU.mult,
                    op1=ALU.add), inc=False, drain=True)
                v_op(lambda e, rsd=rsd, t1=t1: e.tensor_tensor(
                    out=rsd, in0=rsd, in1=t1, op=ALU.mult), inc=False, drain=True)
                vv_nb = v_op(lambda e, mn=mn, rsd=rsd, nb_=nb_: e.tensor_scalar(
                    out=nb_, in0=mn, scalar1=rsd, scalar2=-1.0, op0=ALU.mult,
                    op1=ALU.mult), drain=True)
                ss_xn = s_op(lambda e, xt=xt, rsd=rsd, nb_=nb_: e.activation(
                    out=xt, in_=xt, func=AF.Identity, scale=rsd, bias=nb_),
                    waits=[("vv", vv_nb)])
                if debug and b == 0 and tc == 0:
                    (qd0, _) = emit_quant(xt[:, 0:512], xqtm(tc)[:, 0:512], 512,
                                          src_ready=[("ss", ss_xn)],
                                          bufs=pro_bufs)
                    dbg_st = 0
                    for dap, src_ in [(d_qu, qu[:]), (d_qn, qn[:]),
                                      (d_qsg, qsg[:]), (d_qsc, qsc[:, 0:96]),
                                      (d_xn0, xt)]:
                        dbg_st = st_dma("st1", dap, src_, waits=[("vv", qd0)])
                    (qd, qss) = emit_quant(xt[:, 512:1024], xqtm(tc)[:, 512:1024],
                                           512, src_ready=[("ss", ss_xn),
                                                           ("st1", dbg_st)],
                                           bufs=pro_bufs)
                else:
                    (qd, qss) = emit_quant_multi(xt, xqtm(tc), 1024,
                                                 src_ready=[("ss", ss_xn)],
                                                 bufs=pro_bufs, maxw=1024)
                last_tr = None
                for kc in range(KC):
                    w_ = [("ss", ss_xn)]
                    if kc == 0 and tc > 0:
                        w_.append(("ss", ev_tc[tc - 1]))
                    bankt = P[6 + kc // 4]
                    last_tr = S.add("tensor",
                                    lambda e, kc=kc, xt=xt, bankt=bankt:
                                    e.transpose(bankt[:, (kc % 4) * 128:
                                                      (kc % 4 + 1) * 128],
                                                xt[:, kc * 128:(kc + 1) * 128],
                                                ident[:]),
                                    waits=w_, inc="pe" if kc == KC - 1 else None)
                xin_free[sl] = [("pe", last_tr), ("vv", qd), ("ss", qss)]
                ev = 0
                for kc in range(KC):
                    ev = s_op(lambda e, kc=kc, tc=tc: e.copy(
                        out=xnf(kc)[:, tc * 128:(tc + 1) * 128],
                        in_=P[6 + kc // 4][:, (kc % 4) * 128:(kc % 4 + 1) * 128]),
                        waits=[("pe", last_tr)] if kc == 0 else [])
                ev_tc[tc] = ev
                bank = P[tc % 2]
                last = None
                for kc in range(KC):
                    w_ = [("lw", wv_ready)]
                    if kc == 0:
                        w_.append(("ss", ev_tc[tc]))
                        w_.append(vps_ev[tc % 2])
                    last = S.add("tensor", lambda e, bank=bank, kc=kc, tc=tc:
                                 e.matmul(bank[:, 0:256],
                                          xnf(kc)[:, tc * 128:(tc + 1) * 128],
                                          wv_t(kc),
                                          start=(kc == 0), stop=(kc == KC - 1)),
                                 waits=w_, inc="pe" if kc == KC - 1 else None)
                ev_v = s_op(lambda e, bank=bank, tc=tc: e.copy(
                    out=v_tm[:, tc * 256:(tc + 1) * 256], in_=bank[:, 0:256]),
                    waits=[("pe", last)])
                vps_ev[tc % 2] = ("ss", ev_v)
            prologue_ss = S.val("ss")
            prologue_vv = S.val("vv")

            if stage < 3:
                continue
            ctx_ev_prev = list(ctx_b_guard)
            for vg in range(4):
                vz = v_op(lambda e: e.memset(P[4][0:64, 128:512], 0.0),
                          waits=ctx_ev_prev)
                S.add("tensor", lambda e: e.matmul(
                    P[4][0:64, 0:128], ones1[0:1, 0:64], zrow[0:1, :],
                    start=True, stop=False),
                    waits=[("gg", gg_zrow), ("vv", vz)] + ctx_ev_prev,
                    inc="pe")
                if vg > 0:
                    wv_ready = 0
                    for kcw in range(KC):
                        wv_ready = ld_dma(
                            "lw", wv_t(kcw),
                            s_qkvT[kcw * 128:(kcw + 1) * 128,
                                   2048 + vg * 256:2048 + (vg + 1) * 256],
                            waits=[("pe", S.val("pe"))] if kcw == 0 else [])
                    vps_ev = [("ss", S.val("ss")), ("ss", S.val("ss"))]
                    for tc in range(NTC):
                        bank = P[tc % 2]
                        last = None
                        for kc in range(KC):
                            w_ = [("lw", wv_ready)]
                            if kc == 0:
                                w_.append(("ss", ev_tc[tc]))
                                w_.append(vps_ev[tc % 2])
                            last = S.add("tensor",
                                         lambda e, bank=bank, kc=kc, tc=tc:
                                         e.matmul(bank[:, 0:256],
                                                  xnf(kc)[:, tc * 128:(tc + 1) * 128],
                                                  wv_t(kc),
                                                  start=(kc == 0),
                                                  stop=(kc == KC - 1)),
                                         waits=w_,
                                         inc="pe" if kc == KC - 1 else None)
                        ev = s_op(lambda e, bank=bank, tc=tc: e.copy(
                            out=v_tm[:, tc * 256:(tc + 1) * 256],
                            in_=bank[:, 0:256]),
                            waits=[("pe", last)])
                        vps_ev[tc % 2] = ("ss", ev)
                        if b == 0:
                            pump_up(1)
                    v_done_ss = S.val("ss")
                else:
                    v_done_ss = prologue_ss
                k_free_vv = 0
                for ks in range(2):
                    sc_idx = vg * 2 + ks
                    wk_ready = 0
                    for kcw in range(KC):
                        wk_ready = ld_dma(
                            "lw", wk_t(kcw),
                            s_qkvT[kcw * 128:(kcw + 1) * 128,
                                   1024 + sc_idx * 128:1024 + (sc_idx + 1) * 128],
                            waits=[("pe", S.val("pe"))] if kcw == 0 else [])
                    kps_ev = [("ss", S.val("ss")), ("ss", S.val("ss"))]
                    kexp_last = 0
                    for tq in range(4):
                        bank = P[2 + tq % 2]
                        last = None
                        for kc in range(KC):
                            w_ = [("lw", wk_ready)]
                            if kc == 0:
                                w_.append(kps_ev[tq % 2])
                            last = S.add("tensor", lambda e, bank=bank, kc=kc, tq=tq:
                                         e.matmul(bank[:],
                                                  wk_t(kc),
                                                  xnf(kc)[:, tq * 512:(tq + 1) * 512],
                                                  start=(kc == 0),
                                                  stop=(kc == KC - 1)),
                                         waits=w_, inc="pe" if kc == KC - 1 else None)
                        w_e = [("pe", last)]
                        kexp_last = s_op(
                            lambda e, bank=bank, tq=tq, sc_idx=sc_idx:
                            e.activation(
                                out=k_fm[:, tq * 512:(tq + 1) * 512], in_=bank[:],
                                func=AF.Exp,
                                accum_out=stat[:, 32 + sc_idx * 4 + tq:
                                               33 + sc_idx * 4 + tq]),
                            waits=w_e)
                        kps_ev[tq % 2] = ("ss", kexp_last)
                        if b == 0:
                            pump_up(1)
                    kexp_ss = kexp_last
                    v_op(lambda e, sc_idx=sc_idx: e.reduce_sum(
                        out=rs_all[:, sc_idx:sc_idx + 1],
                        in_=stat[:, 32 + sc_idx * 4:36 + sc_idx * 4], axis=AX.X),
                        waits=[("ss", kexp_ss)], inc=False, drain=True)
                    k_free_vv = v_op(lambda e, sc_idx=sc_idx: e.reciprocal(
                        out=rs_all[:, sc_idx:sc_idx + 1],
                        in_=rs_all[:, sc_idx:sc_idx + 1]), drain=True)
                    tr_ev = ("ss", 0)
                    lastt = None
                    for g in range(4):
                        for j in range(4):
                            tcc = g * 4 + j
                            w_ = [("ss", kexp_ss)]
                            if j == 0:
                                w_.append(tr_ev)
                            lastt = S.add("tensor", lambda e, j=j, tcc=tcc:
                                          e.transpose(P[2][:, j * 128:(j + 1) * 128],
                                                      k_fm[:, tcc * 128:(tcc + 1) * 128],
                                                      ident[:]),
                                          waits=w_, inc="pe" if j == 3 else None)
                        evt = s_op(lambda e, g=g: e.copy(
                            out=kT[:, g * 512:(g + 1) * 512], in_=P[2][:]),
                            waits=[("pe", lastt)])
                        tr_ev = ("ss", evt)
                    kT_ss = S.val("ss")
                    for hl in range(2):
                        head_local = ks * 2 + hl
                        last = None
                        for tc in range(NTC):
                            w_ = []
                            if tc == 0:
                                w_ = [("ss", kT_ss), ("ss", v_done_ss),
                                      ("vv", vz)]
                            fin = (head_local == 3 and tc == NTC - 1)
                            last = S.add(
                                "tensor",
                                lambda e, tc=tc, head_local=head_local, fin=fin:
                                e.matmul(P[4][0:64, head_local * 128:
                                              head_local * 128 + 128],
                                         v_tm[:, tc * 256 + head_local * 64:
                                              tc * 256 + (head_local + 1) * 64],
                                         kT[:, tc * 128:(tc + 1) * 128],
                                         start=False, stop=fin),
                                waits=w_, inc="pe" if tc == NTC - 1 else None)
                        ctx_last_pe = last
                ev_c = 0
                for head_local in range(4):
                    hgl = vg * 4 + head_local
                    ev_c = s_op(lambda e, hgl=hgl, head_local=head_local: e.copy(
                        out=ctxsb[(hgl % 2) * 64:(hgl % 2) * 64 + 64,
                                  (hgl // 2) * 64:(hgl // 2) * 64 + 64],
                        in_=P[4][0:64, head_local * 128 + (head_local % 2) * 64:
                                 head_local * 128 + (head_local % 2) * 64 + 64]),
                        waits=[("pe", ctx_last_pe)] if head_local == 0 else [])
                ctx_ev_prev = [("ss", ev_c)]
            if b == 0:
                pump_up(len(LP_PIECES))
                pump_flush()
                out_free[0] = [("st0", S.val("st0"))]
            ctx_ss = S.val("ss")
            phase1_vv = S.val("vv")

            if stage < 4:
                continue
            gf_ev = ("ss", 0)
            g_last_pe = S.val("pe")
            for rc in range(8):
                for hh_ in range(2):
                    h_gl = rc * 2 + hh_
                    wp_ready = ld_dma(
                        "lw", wstr[hh_ * 64:hh_ * 64 + 64, 0:1024],
                        s_projT[h_gl * 64:(h_gl + 1) * 64, :],
                        waits=[("pe", g_last_pe)])
                    for half in range(2):
                        bank = P[4 + half]
                        w_ = [("lw", wp_ready), ("ss", ctx_ss)]
                        if hh_ == 0 and half == 0:
                            w_.append(gf_ev)
                        g_last_pe = S.add(
                            "tensor",
                            lambda e, bank=bank, hh_=hh_, h_gl=h_gl, half=half:
                            e.matmul(bank[hh_ * 64:(hh_ + 1) * 64, :],
                                     ctxsb[(h_gl % 2) * 64:(h_gl % 2) * 64 + 64,
                                           (h_gl // 2) * 64:(h_gl // 2) * 64 + 64],
                                     wstr[hh_ * 64:hh_ * 64 + 64,
                                          half * 512:half * 512 + 512],
                                     start=True, stop=True),
                            waits=w_,
                            inc="pe" if half == 1 else None)
                s_op(lambda e, rc=rc: e.activation(
                    out=G_t(rc)[:, 0:512], in_=P[4][:], func=AF.Identity,
                    scale=rs_all[:, rc:rc + 1], bias=zeros1[:]),
                    waits=[("pe", g_last_pe), ("vv", v_zero),
                           ("vv", phase1_vv)])
                ev2 = s_op(lambda e, rc=rc: e.activation(
                    out=G_t(rc)[:, 512:1024], in_=P[5][:], func=AF.Identity,
                    scale=rs_all[:, rc:rc + 1], bias=zeros1[:]))
                gf_ev = ("ss", ev2)
            g_ready_ss = S.val("ss")
            g_ready_pe = S.val("pe")
            if debug and b == 0:
                for name_ap, src in [(d_ctx, ctxsb[:]), (d_rs, rs_all[:]),
                                     (d_G, WBIG[:]), (d_kfm, k_fm),
                                     (d_xnf, xnf(0)), (d_xq, xqtm(0)),
                                     (d_stat, stat[:]),
                                     (d_rn[0], rsdP[0][:]),
                                     (d_rn[1], rsdP[1][:]),
                                     (d_rn[2], nbP[0][:]),
                                     (d_rn[3], nbP[1][:])]:
                    st_dma("st0", name_ap, src,
                           waits=[("ss", g_ready_ss), ("vv", S.val("vv"))])

            if stage < 5:
                continue
            wq_sl = [wstr[:, 0:512], wstr[:, 512:1024]]
            wq_free = [[("pe", g_ready_pe)], [("pe", g_ready_pe)]]
            xp2_free = [list(xin_free[0]), list(xin_free[1])]
            qpsum_ev = ("ss", g_ready_ss)
            ap_ev = ("ss", g_ready_ss)
            tr_ev2 = ("ss", g_ready_ss)
            qtr_prev = 0
            for pair in range(8):
                qmm_last = None
                for kc in range(KC):
                    for half in range(2):
                        wsl = (kc * 2 + half) % 2
                        ldq = 0
                        for qh in range(2):
                            ldq = ld_dma(
                                "lw", wq_sl[wsl][:, qh * 256:(qh + 1) * 256],
                                s_qkvT[kc * 128:(kc + 1) * 128,
                                       half * 512 + qh * 256:
                                       half * 512 + (qh + 1) * 256],
                                waits=wq_free[wsl] if qh == 0 else [])
                        for ti in range(2):
                            tc = pair * 2 + ti
                            bank = P[ti * 2 + half]
                            w_ = [("lw", ldq)]
                            if kc == 0:
                                w_.append(qpsum_ev)
                            qmm_last = S.add(
                                "tensor",
                                lambda e, bank=bank, kc=kc, tc=tc, half=half,
                                wsl=wsl:
                                e.matmul(bank[:],
                                         xnf(kc)[:, tc * 128:(tc + 1) * 128],
                                         wq_sl[wsl][:],
                                         start=(kc == 0), stop=(kc == KC - 1)),
                                waits=w_,
                                inc="pe" if ti == 1 else None)
                        wq_free[wsl] = [("pe", qmm_last)]
                for ti in range(2):
                    tc = pair * 2 + ti
                    s_op(lambda e, ti=ti: e.activation(
                        out=tA[:, 0:512], in_=P[ti * 2][:], func=AF.Exp),
                        waits=[("pe", qmm_last), ("vv", S.val("vv"))])
                    ss_exp = s_op(lambda e, ti=ti: e.activation(
                        out=tA[:, 512:1024], in_=P[ti * 2 + 1][:], func=AF.Exp))
                    if ti == 1:
                        qpsum_ev = ("ss", ss_exp)
                    v_op(lambda e: e.reduce_sum(
                        out=stat[:, 16:32],
                        in_=tA[:].rearrange("p (h d) -> p h d", d=64), axis=AX.X),
                        waits=[("ss", ss_exp)], inc=False, drain=True)
                    v_op(lambda e: e.reciprocal(out=stat[:, 16:32],
                                                in_=stat[:, 16:32]), inc=False,
                         drain=True)
                    vv_qs = v_op(lambda e: e.tensor_tensor(
                        out=tB[:].rearrange("p (h d) -> p h d", d=64),
                        in0=tA[:].rearrange("p (h d) -> p h d", d=64),
                        in1=stat[:, 16:32].unsqueeze(2).broadcast_to([128, 16, 64]),
                        op=ALU.mult), waits=[("pe", qtr_prev)])
                    for g in range(2):
                        bankt = P[6 + g]
                        lastq = None
                        for j in range(4):
                            kc = g * 4 + j
                            w_ = [("vv", vv_qs)]
                            if j == 0:
                                w_.append(tr_ev2)
                            lastq = S.add("tensor",
                                          lambda e, bankt=bankt, j=j, kc=kc:
                                          e.transpose(bankt[:, j * 128:(j + 1) * 128],
                                                      tB[:, kc * 128:(kc + 1) * 128],
                                                      ident[:]),
                                          waits=w_, inc="pe" if j == 3 else None)
                        ss_t = s_op(lambda e, g=g: e.copy(
                            out=tA[:, g * 512:(g + 1) * 512], in_=P[6 + g][:]),
                            waits=[("pe", lastq)])
                        tr_ev2 = ("ss", ss_t)
                        qtr_prev = lastq
                    qsf_ss = S.val("ss")
                    last_ap = None
                    for half in range(2):
                        bank = P[4 + half]
                        for kc in range(KC):
                            w_ = [("ss", qsf_ss)]
                            if kc == 0 and half == 0:
                                w_.append(ap_ev)
                            last_ap = S.add(
                                "tensor", lambda e, bank=bank, kc=kc, half=half:
                                e.matmul(bank[:], tA[:, kc * 128:(kc + 1) * 128],
                                         G_t(kc)[:, half * 512:(half + 1) * 512],
                                         start=(kc == 0), stop=False),
                                waits=w_, inc=None)
                        if "bias" in skip:
                            last_ap = S.add(
                                "tensor", lambda e, bank=bank, kc=KC - 1, half=half:
                                e.matmul(bank[:], tA[:, (KC - 1) * 128:KC * 128],
                                         G_t(KC - 1)[:, half * 512:(half + 1) * 512],
                                         start=False, stop=True),
                                waits=[], inc="pe")
                        else:
                            last_ap = S.add(
                                "tensor", lambda e, bank=bank, half=half:
                                e.matmul(bank[:], ones1[:],
                                         brow[0:1, half * 512:(half + 1) * 512],
                                         start=False, stop=True),
                                waits=[("lb", ld_brow), ("gg", gg_ones)], inc="pe")
                    osl = ti
                    ld_x2 = 0
                    for qh in range(2):
                        ld_x2 = ld_dma(
                            f"xl{ti}", xin_t[ti][:, qh * 512:(qh + 1) * 512],
                            x_in[b, tc * 128:(tc + 1) * 128,
                                 qh * 512:(qh + 1) * 512],
                            waits=xp2_free[ti] if qh == 0 else [])
                    v_op(lambda e, osl=osl, ti=ti: e.tensor_tensor(
                        out=out_sl[osl][:, 0:512], in0=P[4][:],
                        in1=xin_t[ti][:, 0:512], op=ALU.add),
                        waits=[("pe", last_ap), (f"xl{ti}", ld_x2)] + out_free[osl],
                        inc=False)
                    vv_o = v_op(lambda e, osl=osl, ti=ti: e.tensor_tensor(
                        out=out_sl[osl][:, 512:1024], in0=P[5][:],
                        in1=xin_t[ti][:, 512:1024], op=ALU.add))
                    ap_ev = ("vv", vv_o)
                    xp2_free[ti] = [("vv", vv_o)]
                    st_w = st_dma(f"so{osl}", out[b, tc * 128:(tc + 1) * 128, :],
                                  out_sl[osl], waits=[("vv", vv_o)])
                    out_free[osl] = [(f"so{osl}", st_w)]
            resid_gx = S.val("gx")

            if stage < 6:
                continue
            xq_guard = [("pe", S.val("pe")), ("ss", S.val("ss")),
                        ("vv", S.val("vv"))]
            prev_ev = 0
            last_tr = None
            for tc in range(NTC):
                for kc in range(KC):
                    w_ = [("ss", prologue_ss), ("vv", prologue_vv)]
                    if tc == 0 and kc == 0:
                        w_ += xq_guard
                    if tc % 4 == 0 and kc == 0:
                        w_.append(("ss", prev_ev))
                    last_tr = S.add("tensor", lambda e, kc=kc, tc=tc:
                                    e.transpose(P[kc][:, (tc % 4) * 128:(tc % 4 + 1) * 128],
                                                xqtm(tc)[:, kc * 128:(kc + 1) * 128],
                                                ident[:]),
                                    waits=w_, inc="pe" if kc == KC - 1 else None)
                if tc % 4 == 3:
                    ev = 0
                    for kc in range(KC):
                        ev = s_op(lambda e, kc=kc, tc=tc: e.copy(
                            out=xqf(kc)[:, (tc - 3) * 128:(tc + 1) * 128],
                            in_=P[kc][:, 0:512]),
                            waits=[("pe", last_tr)])
                    prev_ev = ev
            xq_fm_ss = S.val("ss")

            g_free = [("pe", S.val("pe")), ("ss", xq_fm_ss)]
            u_free = [("pe", S.val("pe")), ("st0", S.val("st0"))]
            dpsum_ev = ("vv", S.val("vv"))
            gate_ev = ("ss", S.val("ss"))
            up_ev = ("vv", S.val("vv"))
            hq_free = [("pe", S.val("pe")), ("pe", S.val("pe"))]
            hqT_busy = [[], []]
            ht_ev = [S.val("ss"), S.val("ss")]
            fwd_free = [[("pe", S.val("pe"))], [("pe", S.val("pe"))]]
            hqt2 = [hqt[:, 0:384], tA[:, 0:384]]
            hqT_sl = [hqT[:, 0:384], tB[:, 0:384]]
            for pair in range(8):
                qd_p = [None, None]
                tr_p = [None, None]

                def emit_downstage(dp, wdt_d, i0_d):
                    nonlocal fwd_free, hqT_busy
                    nj = wdt_d // 128
                    ht_last = 0
                    for ti in range(2):
                        ht_ev[ti] = s_op(lambda e, ti=ti, wdt_d=wdt_d: e.copy(
                            out=hqT_sl[ti][:, 0:wdt_d], in_=P[6 + ti][:, 0:wdt_d]),
                            waits=[("pe", tr_p[ti])] + hqT_busy[ti])
                        ht_last = ht_ev[ti]
                    dld = {}
                    for j in range(min(nj, 2)):
                        dld[j] = ld_dma_g("ld2", fW_d(j % 2),
                                          s_downT[i0_d + j * 128:i0_d + (j + 1) * 128, :],
                                          waits=fwd_free[j % 2])
                    if nj >= 2:
                        dld[0] = dld[1]
                    dmm = {}
                    last_d = None
                    for j in range(nj):
                        dsl = j % 2
                        if j >= 2:
                            dld[j] = ld_dma_g("ld2", fW_d(dsl),
                                              s_downT[i0_d + j * 128:i0_d + (j + 1) * 128, :],
                                              waits=[("pe", dmm[j - 2])])
                        for ti in range(2):
                            for half in range(2):
                                bank = P[2 + ti * 2 + half]
                                last_d = S.add(
                                    "tensor",
                                    lambda e, j=j, bank=bank, half=half, dp=dp,
                                    dsl=dsl, nj=nj, ti=ti, wdt_d=wdt_d:
                                    e.matmul(bank[:],
                                             hqT_sl[ti][:, j * 128:(j + 1) * 128],
                                             fW_d(dsl)[:, half * 512:(half + 1) * 512],
                                             start=(dp == 0 and j == 0),
                                             stop=(dp == NFP - 1 and j == nj - 1)),
                                    waits=[("ss", ht_last), ("ld2", dld[j])],
                                    inc="pe" if (ti == 1 and half == 1) else None)
                        dmm[j] = last_d
                        fwd_free[dsl] = [("pe", last_d)]
                    hqT_busy = [[("pe", last_d)], [("pe", last_d)]]
                    return last_d

                for pi, (i0, wdt) in enumerate(FFN_P):
                    wld_g = 0
                    for kc in range(KC):
                        wld_g = ld_dma("lw", fW_g(kc)[:, 0:wdt],
                                       s_gateT[kc * 128:(kc + 1) * 128, i0:i0 + wdt],
                                       waits=g_free if kc == 0 else [])
                    wld_u = 0
                    for kc in range(KC):
                        wld_u = ld_dma("lu", fW_u(kc)[:, 0:wdt],
                                       s_upT[kc * 128:(kc + 1) * 128, i0:i0 + wdt],
                                       waits=u_free if kc == 0 else [])
                    lastu_all = None
                    lastg_all = None
                    for ti in range(2):
                        tc = pair * 2 + ti
                        lastg = lastu = None
                        for kc in range(KC):
                            w_ = [("lw", wld_g), ("ss", xq_fm_ss)]
                            if kc == 0:
                                w_.append(gate_ev)
                                if pi == 0 and ti == 0:
                                    w_.append(dpsum_ev)
                            lastg = S.add("tensor", lambda e, kc=kc, tc=tc, wdt=wdt:
                                          e.matmul(P[0][:, 0:wdt],
                                                   xqf(kc)[:, tc * 128:(tc + 1) * 128],
                                                   fW_g(kc)[:, 0:wdt],
                                                   start=(kc == 0), stop=(kc == KC - 1)),
                                          waits=w_, inc="pe" if kc == KC - 1 else None)
                        ss_g = s_op(lambda e, wdt=wdt: e.activation(
                            out=hg[:, 0:wdt], in_=P[0][:, 0:wdt], func=AF.Silu),
                            waits=[("pe", lastg)])
                        gate_ev = ("ss", ss_g)
                        if pi > 0 and qd_p[ti] is not None:
                            wdt_prev = FFN_P[pi - 1][1]
                            lastt = None
                            for j in range(wdt_prev // 128):
                                w_ = [("vv", qd_p[ti])]
                                if j == 0:
                                    w_.append(("ss", ht_ev[ti]))
                                lastt = S.add(
                                    "tensor", lambda e, j=j, ti=ti:
                                    e.transpose(P[6 + ti][:, j * 128:(j + 1) * 128],
                                                hqt2[ti][:, j * 128:(j + 1) * 128],
                                                ident[:]),
                                    waits=w_,
                                    inc="pe" if j == wdt_prev // 128 - 1 else None)
                            tr_p[ti] = lastt
                            hq_free[ti] = ("pe", lastt)
                        for kc in range(KC):
                            w_ = [("lu", wld_u), up_ev] if kc == 0 else []
                            lastu = S.add("tensor", lambda e, kc=kc, tc=tc, wdt=wdt:
                                          e.matmul(P[1][:, 0:wdt],
                                                   xqf(kc)[:, tc * 128:(tc + 1) * 128],
                                                   fW_u(kc)[:, 0:wdt],
                                                   start=(kc == 0), stop=(kc == KC - 1)),
                                          waits=w_, inc="pe" if kc == KC - 1 else None)
                        lastu_all = lastu
                        lastg_all = lastg
                        vv_h = v_op(lambda e, wdt=wdt: e.tensor_tensor(
                            out=hh[:, 0:wdt], in0=hg[:, 0:wdt], in1=P[1][:, 0:wdt],
                            op=ALU.mult), waits=[("ss", ss_g), ("pe", lastu)])
                        up_ev = ("vv", vv_h)
                        (qd, _) = emit_quant(hh[:, 0:wdt], hqt2[ti][:, 0:wdt], wdt,
                                             src_ready=[("vv", vv_h)],
                                             dst_free=[hq_free[ti]])
                        qd_prev_ti = qd_p[ti]
                        qd_p[ti] = qd
                    g_free = [("pe", lastg_all)]
                    u_free = [("pe", lastu_all)]
                    if pi > 0:
                        emit_downstage(pi - 1, FFN_P[pi - 1][1], FFN_P[pi - 1][0])
                wdt_l, i0_l = FFN_P[NFP - 1][1], FFN_P[NFP - 1][0]
                for ti in range(2):
                    lastt = None
                    for j in range(wdt_l // 128):
                        w_ = [("vv", qd_p[ti])]
                        lastt = S.add("tensor", lambda e, j=j, ti=ti:
                                      e.transpose(P[6 + ti][:, j * 128:(j + 1) * 128],
                                                  hqt2[ti][:, j * 128:(j + 1) * 128],
                                                  ident[:]),
                                      waits=w_,
                                      inc="pe" if j == wdt_l // 128 - 1 else None)
                    tr_p[ti] = lastt
                    hq_free[ti] = ("pe", lastt)
                emit_downstage(NFP - 1, wdt_l, i0_l)
                dd = S.val("vv")
                for ti in range(2):
                    tc = pair * 2 + ti
                    osl = ti
                    v_op(lambda e, osl=osl, ti=ti: e.tensor_copy(
                        out=out_sl[osl][:, 0:512], in_=P[2 + ti * 2][:]),
                        waits=[("pe", S.val("pe"))] + out_free[osl], inc=False)
                    vv_e = v_op(lambda e, osl=osl, ti=ti: e.tensor_copy(
                        out=out_sl[osl][:, 512:1024], in_=P[2 + ti * 2 + 1][:]))
                    gsem = "ga" if osl == 0 else "gb"
                    gda = gd_accum(gsem, out[b, tc * 128:(tc + 1) * 128, :],
                                   out_sl[osl],
                                   waits=[("vv", vv_e), ("gx", resid_gx),
                                          (f"so{osl}", S.val(f"so{osl}"))])
                    out_free[osl] = [(gsem, gda)]
                    dd = vv_e
                dpsum_ev = ("vv", dd)

        for sname in ("gx", "ga", "gb"):
            S.add("gpsimd", lambda e, sname=sname: e.wait_ge(sem[sname], S.cnt[sname]))
        for sname in ("so0", "so1", "gx", "ga", "gb", "st0", "st1", "st2", "st3"):
            S.add("sync", lambda e, sname=sname: e.wait_ge(sem[sname], S.cnt[sname]))

        def run_items(eng_obj, name):
            for waits, fn, inc, inc_by, drain_after in S.items[name]:
                for (s_name, val) in waits:
                    if val:
                        eng_obj.wait_ge(sem[s_name], val)
                inst = fn(eng_obj) if fn is not None else None
                if inc is not None and inst is not None:
                    inst.then_inc(sem[inc], inc_by)
                if drain_after:
                    eng_obj.drain()

        @block.sync
        def _(e):
            run_items(e, "sync")

        @block.gpsimd
        def _(e):
            run_items(e, "gpsimd")

        @block.scalar
        def _(e):
            run_items(e, "scalar")

        @block.vector
        def _(e):
            run_items(e, "vector")

        @block.tensor
        def _(e):
            run_items(e, "tensor")

    return nc


def kernel(L0_attn_x, L0_attn_w_qkv, L0_attn_w_proj, L0_attn_b_proj,
           L0_ffn_w_gate, L0_ffn_w_up, L0_ffn_w_down):
    x = np.ascontiguousarray(np.asarray(L0_attn_x, dtype=np.float32))
    wqkv = np.ascontiguousarray(np.asarray(L0_attn_w_qkv, dtype=np.float32))
    wproj = np.ascontiguousarray(np.asarray(L0_attn_w_proj, dtype=np.float32))
    bproj = np.ascontiguousarray(np.asarray(L0_attn_b_proj, dtype=np.float32))
    wg = np.ascontiguousarray(np.asarray(L0_ffn_w_gate, dtype=np.float32))
    wu = np.ascontiguousarray(np.asarray(L0_ffn_w_up, dtype=np.float32))
    wd = np.ascontiguousarray(np.asarray(L0_ffn_w_down, dtype=np.float32))

    nc = build_core_program()
    in_maps = [{"x": x[2 * c:2 * c + 2], "wqkv": wqkv, "wproj": wproj,
                "bproj": bproj, "wg": wg, "wu": wu, "wd": wd}
               for c in range(NCORES)]
    res = run_bass_kernel_spmd(nc, in_maps, core_ids=list(range(NCORES)))
    return np.concatenate([r["out"] for r in res.results], axis=0)


# version 18
# speedup vs baseline: 1.2097x; 1.0154x over previous
import numpy as np
import concourse.bass as bass
import concourse.mybir as mybir
from concourse.bass_utils import run_bass_kernel_spmd
from contextlib import ExitStack


F32 = mybir.dt.float32
AF = mybir.ActivationFunctionType
ALU = mybir.AluOpType
AX = mybir.AxisListType

NCORES = 8
B_CORE = 2
N = 2048
C = 1024
H = 16
DH = 64
IF = 2816
NTC = N // 128
KC = C // 128
FFN_PASSES = [(i * 256, 256) for i in range(11)]

ENGS = ("sync", "gpsimd", "scalar", "vector", "tensor")


class Sched:
    def __init__(self):
        self.items = {e: [] for e in ENGS}
        self.cnt = {s: 0 for s in
                    ("pl0", "pl1", "xl0", "xl1", "lw", "lu", "lb", "ld2",
                     "st0", "st1", "st2", "st3", "so0", "so1",
                     "gx", "ga", "gb", "pe", "vv", "ss", "gg")}

    def add(self, eng, fn, waits=(), inc=None, inc_by=1, drain_after=False):
        self.items[eng].append((tuple(waits), fn, inc, inc_by, drain_after))
        if inc is not None:
            self.cnt[inc] += inc_by
            return self.cnt[inc]
        return None

    def val(self, sem):
        return self.cnt[sem]


def build_core_program(stage=7, nbatch=B_CORE, skip=(), debug=False):
    nc = bass.Bass("TRN2", target_bir_lowering=False,
                   detect_race_conditions=False)

    x_in = nc.dram_tensor("x", [B_CORE, N, C], F32, kind="ExternalInput").ap()
    wqkv = nc.dram_tensor("wqkv", [3 * C, C], F32, kind="ExternalInput").ap()
    wproj = nc.dram_tensor("wproj", [C, C], F32, kind="ExternalInput").ap()
    bproj = nc.dram_tensor("bproj", [C], F32, kind="ExternalInput").ap()
    wg_in = nc.dram_tensor("wg", [IF, C], F32, kind="ExternalInput").ap()
    wu_in = nc.dram_tensor("wu", [IF, C], F32, kind="ExternalInput").ap()
    wd_in = nc.dram_tensor("wd", [C, IF], F32, kind="ExternalInput").ap()
    out = nc.dram_tensor("out", [B_CORE, N, C], F32, kind="ExternalOutput").ap()
    s_qkvT = nc.dram_tensor("s_qkvT", [C, 3 * C], F32, kind="Internal").ap()
    s_projT = nc.dram_tensor("s_projT", [C, C], F32, kind="Internal").ap()
    s_gateT = nc.dram_tensor("s_gateT", [C, IF], F32, kind="Internal").ap()
    s_upT = nc.dram_tensor("s_upT", [C, IF], F32, kind="Internal").ap()
    s_downT = nc.dram_tensor("s_downT", [IF, C], F32, kind="Internal").ap()
    if debug:
        d_ctx = nc.dram_tensor("d_ctx", [128, 512], F32, kind="ExternalOutput").ap()
        d_rs = nc.dram_tensor("d_rs", [128, 8], F32, kind="ExternalOutput").ap()
        d_G = nc.dram_tensor("d_G", [128, 8192], F32, kind="ExternalOutput").ap()
        d_kfm = nc.dram_tensor("d_kfm", [128, 2048], F32, kind="ExternalOutput").ap()
        d_xnf = nc.dram_tensor("d_xnf", [128, 2048], F32, kind="ExternalOutput").ap()
        d_xq = nc.dram_tensor("d_xq", [128, 1024], F32, kind="ExternalOutput").ap()
        d_stat = nc.dram_tensor("d_stat", [128, 32], F32, kind="ExternalOutput").ap()
        d_qu = nc.dram_tensor("d_qu", [128, 512], F32, kind="ExternalOutput").ap()
        d_qn = nc.dram_tensor("d_qn", [128, 512], F32, kind="ExternalOutput").ap()
        d_qsg = nc.dram_tensor("d_qsg", [128, 512], F32, kind="ExternalOutput").ap()
        d_qsc = nc.dram_tensor("d_qsc", [128, 96], F32, kind="ExternalOutput").ap()
        d_xn0 = nc.dram_tensor("d_xn0", [128, 1024], F32, kind="ExternalOutput").ap()
        d_rn = [nc.dram_tensor(f"d_rn{i}", [128, 1], F32, kind="ExternalOutput").ap()
                for i in range(4)]

    S = Sched()

    with ExitStack() as stk:
        def sbt(name, shape, dt=F32):
            return stk.enter_context(nc.sbuf_tensor(name, shape, dt))

        XNF = sbt("XNF", [128, 16384])
        XQF = sbt("XQF", [128, 16384])
        WBIG = sbt("WBIG", [128, 8192])
        xin = sbt("xin", [128, 2048])
        tA = sbt("tA", [128, 1024])
        tB = sbt("tB", [128, 1024])
        qu = sbt("qu", [128, 384])
        qn = sbt("qn", [128, 384])
        qm = sbt("qm", [128, 384])
        qsg = sbt("qsg", [128, 384])
        hg = sbt("hg", [128, 384])
        hh = sbt("hh", [128, 384])
        hqt = sbt("hqt", [128, 384])
        hqT = sbt("hqT", [128, 384])
        wstr = sbt("wstr", [128, 1024])
        outst = sbt("outst", [128, 2048])
        ctxsb = outst
        stat = sbt("stat", [128, 64])
        rsdP = [sbt("rsd0", [128, 1]), sbt("rsd1", [128, 1])]
        nbP = [sbt("nb0", [128, 1]), sbt("nb1", [128, 1])]
        qsc = sbt("qsc", [128, 128])
        rs_all = sbt("rs_all", [128, 8])
        zeros1 = sbt("zeros1", [128, 1])
        zrow = sbt("zrow", [1, 128])
        ident = sbt("ident", [128, 128])
        ones1 = sbt("ones1", [1, 128])
        brow = sbt("brow", [1, 1024])
        P = [stk.enter_context(nc.psum_tensor(f"P{i}", [128, 512], F32))
             for i in range(8)]
        sem = {s: stk.enter_context(nc.semaphore(s))
               for s in ("pl0", "pl1", "xl0", "xl1", "lw", "lu", "lb", "ld2",
                         "st0", "st1", "st2", "st3", "so0", "so1",
                         "gx", "ga", "gb", "pe", "vv", "ss", "gg")}
        block = stk.enter_context(nc.Block())

        x3 = lambda ap: ap.rearrange("p (b k) -> p b k", k=16)

        def bc16(ap, nb):
            return ap.unsqueeze(2).broadcast_to([128, nb, 16])

        def ld_dma(semn, dst, src, waits=()):
            return S.add("sync", lambda e, d=dst, s=src: e.dma_start(out=d, in_=s),
                         waits=waits, inc=semn, inc_by=16)

        def st_dma(semn, dst, src, waits=()):
            return S.add("sync", lambda e, d=dst, s=src: e.dma_start(out=d, in_=s),
                         waits=waits, inc=semn, inc_by=16)

        def ld_dma_g(semn, dst, src, waits=()):
            return S.add("gpsimd",
                         lambda e, d=dst, s=src: e.dma_start(out=d, in_=s),
                         waits=waits, inc=semn, inc_by=16)

        def gd_accum(semn, dst, src, waits=()):
            return S.add("gpsimd",
                         lambda e, d=dst, s=src: e.dma_start(out=d, in_=s,
                                                             accum_op=ALU.add),
                         waits=waits, inc=semn, inc_by=16)

        def v_op(fn, waits=(), inc=True, drain=False):
            return S.add("vector", fn, waits=waits, inc="vv" if inc else None,
                         drain_after=drain)

        def s_op(fn, waits=(), inc=True):
            return S.add("scalar", fn, waits=waits, inc="ss" if inc else None)

        qsg_busy = [0]

        def emit_quant(src, dst, w, src_ready, dst_free=(), bufs=None):
            nb = w // 16
            bm = qsc[:, 0:nb]
            bm64, rc64, s264 = qsc[:, 0:64], qsc[:, 64:128], qsc[:, 0:64]
            bu, bg, bm2, bsg = bufs if bufs is not None else (qu, qn, qm, qsg)
            sg = bsg[:, 0:w]
            u = bu[:, 0:w]
            gg_ = bg[:, 0:w]
            xv = lambda ap: ap.rearrange("p (b k) -> p b k", k=16)
            ss_sign = s_op(lambda e, s=src, sg=sg: e.sign(out=sg, in_=s),
                           waits=list(src_ready) + [("vv", qsg_busy[0])])
            v_op(lambda e, s=src, bm=bm: e.reduce_max(
                out=bm, in_=xv(s), axis=AX.X, apply_absolute_value=True),
                waits=src_ready, inc=False, drain=True)
            v_op(lambda e: e.reciprocal(out=rc64, in_=bm64), inc=False, drain=True)
            v_op(lambda e: e.tensor_scalar(out=rc64, in0=rc64, scalar1=12.0,
                                           scalar2=2e12, op0=ALU.mult,
                                           op1=ALU.min), inc=False, drain=True)
            v_op(lambda e: e.reciprocal(out=s264, in_=rc64), inc=False, drain=True)
            v_op(lambda e, s=src, u=u: e.scalar_tensor_tensor(
                out=u, in0=s, scalar=-1.0, in1=s, op0=ALU.mult, op1=ALU.max),
                inc=False)
            v_op(lambda e, u=u, nb=nb: e.tensor_tensor(
                out=xv(u), in0=xv(u), in1=bc16(rc64[:, 0:nb], nb), op=ALU.mult),
                inc=False)
            m2 = bm2[:, 0:w]
            v_op(lambda e, u=u, gg_=gg_: e.tensor_scalar(
                out=gg_, in0=u, scalar1=0.5, scalar2=None, op0=ALU.is_gt),
                inc=False)
            for th in (1.5, 2.5, 3.5):
                v_op(lambda e, th=th, u=u, gg_=gg_: e.scalar_tensor_tensor(
                    out=gg_, in0=u, scalar=th, in1=gg_, op0=ALU.is_gt,
                    op1=ALU.add), inc=False)
            v_op(lambda e, u=u, m2=m2: e.tensor_scalar(
                out=m2, in0=u, scalar1=5.0, scalar2=None, op0=ALU.is_gt),
                inc=False)
            v_op(lambda e, u=u, m2=m2: e.scalar_tensor_tensor(
                out=m2, in0=u, scalar=7.0, in1=m2, op0=ALU.is_gt, op1=ALU.add),
                inc=False)
            v_op(lambda e, m2=m2, gg_=gg_: e.scalar_tensor_tensor(
                out=gg_, in0=m2, scalar=2.0, in1=gg_, op0=ALU.mult, op1=ALU.add),
                inc=False)
            v_op(lambda e, u=u, m2=m2: e.tensor_scalar(
                out=m2, in0=u, scalar1=10.0, scalar2=4.0, op0=ALU.is_gt,
                op1=ALU.mult), inc=False)
            v_op(lambda e, m2=m2, gg_=gg_: e.tensor_tensor(
                out=gg_, in0=gg_, in1=m2, op=ALU.add), inc=False)
            v_op(lambda e, sg=sg, nb=nb: e.scalar_tensor_tensor(
                out=xv(sg), in0=xv(sg), scalar=0.0, in1=bc16(s264[:, 0:nb], nb),
                op0=ALU.add, op1=ALU.mult),
                waits=[("ss", ss_sign)], inc=False)
            vv_fin = v_op(lambda e, d=dst, gg_=gg_, sg=sg: e.tensor_tensor(
                out=d, in0=gg_, in1=sg, op=ALU.mult), waits=dst_free)
            qsg_busy[0] = vv_fin
            return vv_fin, ss_sign

        def emit_quant_multi(src, dst, cols, src_ready, dst_free=(),
                             bufs=None, maxw=512):
            off, last = 0, (0, 0)
            while off < cols:
                w = min(maxw, cols - off)
                last = emit_quant(src[:, off:off + w], dst[:, off:off + w], w,
                                  src_ready, dst_free, bufs=bufs)
                off += w
            return last

        gg_ms = S.add("gpsimd", lambda e: e.memset(ident[:], 0.0), inc="gg")
        gg_id = S.add("gpsimd", lambda e: e.affine_select(
            out=ident[:], in_=ident[:], compare_op=ALU.not_equal, fill=1.0,
            base=0, pattern=[[-1, 128]], channel_multiplier=1),
            waits=[("gg", gg_ms)], inc="gg")
        gg_ones = S.add("gpsimd", lambda e: e.memset(ones1[:], 1.0), inc="gg")
        gg_zrow = S.add("gpsimd", lambda e: e.memset(zrow[:], 0.0), inc="gg")
        ld_brow = ld_dma("lb", brow[0:1, :], bproj.rearrange("(a c) -> a c", a=1))
        v_zero = v_op(lambda e: e.memset(zeros1[:], 0.0))
        v_op(lambda e: e.memset(qsc[:], 1.0), drain=True)

        rawsl = [XNF[:, 0:2816], XNF[:, 2816:5632]]
        wqsl = [XNF[:, 5632:8448], XNF[:, 8448:11264]]
        evsl = [XNF[:, 11264:11776], XNF[:, 11776:12288],
                XNF[:, 12288:12800], XNF[:, 12800:13312]]
        prep_bufs = (XNF[:, 13312:14336], XNF[:, 14336:15360],
                     XNF[:, 15360:16384], XQF[:, 0:1024])
        ev_free = [("st0", 0), ("st1", 0), ("st2", 0), ("st3", 0)]
        bank_ev = [0, 0]
        slot_free = [[], []]

        def prep_slab(src_hbm, rows, cols, dst_hbm, quant):
            nonlocal ev_free, bank_ev, slot_free
            nchunk = rows // 128
            ncc = cols // 128
            ei = 0
            for ch in range(nchunk):
                sl = ch % 2
                raw = rawsl[sl][:, 0:cols]
                hc = cols // 2
                ld_r = 0
                for qh in range(2):
                    ld_r = ld_dma_g(
                        f"pl{sl}", raw[:, qh * hc:(qh + 1) * hc],
                        src_hbm[ch * 128:(ch + 1) * 128, qh * hc:(qh + 1) * hc],
                        waits=slot_free[sl] if qh == 0 else [])
                if quant:
                    wqt = wqsl[sl][:, 0:cols]
                    (qdone, _) = emit_quant_multi(
                        raw, wqt, cols, src_ready=[(f"pl{sl}", ld_r)],
                        dst_free=[("pe", S.val("pe"))],
                        bufs=prep_bufs, maxw=1024)
                    src_t, src_rdy = wqt, ("vv", qdone)
                    slot_free[sl] = [("vv", qdone)]
                else:
                    src_t, src_rdy = raw, (f"pl{sl}", ld_r)
                    slot_free[sl] = []
                for g in range((ncc + 3) // 4):
                    bank = P[g % 2]
                    n_in_g = min(4, ncc - g * 4)
                    last_pe = None
                    for j in range(n_in_g):
                        cc = g * 4 + j
                        w_ = [src_rdy]
                        if j == 0:
                            w_.append(("ss", bank_ev[g % 2]))
                            w_.append(("gg", gg_id))
                        last_pe = S.add(
                            "tensor",
                            lambda e, b=bank, j=j, s=src_t, cc=cc:
                            e.transpose(b[:, j * 128:(j + 1) * 128],
                                        s[:, cc * 128:(cc + 1) * 128], ident[:]),
                            waits=w_, inc="pe" if j == n_in_g - 1 else None)
                    es = evsl[ei % 4]
                    ss_ev = s_op(lambda e, b=bank, es=es, n=n_in_g:
                                 e.copy(out=es[:, 0:n * 128], in_=b[:, 0:n * 128]),
                                 waits=[("pe", last_pe), ev_free[ei % 4]])
                    bank_ev[g % 2] = ss_ev
                    last_st = 0
                    for j in range(n_in_g):
                        cc = g * 4 + j
                        last_st = st_dma(
                            f"st{ei % 4}",
                            dst_hbm[cc * 128:(cc + 1) * 128,
                                    ch * 128:(ch + 1) * 128],
                            es[:, j * 128:(j + 1) * 128], waits=[("ss", ss_ev)])
                    ev_free[ei % 4] = (f"st{ei % 4}", last_st)
                    ei += 1
                if not quant:
                    slot_free[sl] = [("pe", S.val("pe"))]
                else:
                    slot_free[sl].append(("pe", S.val("pe")))

        prep_slab(wqkv, 3 * C, C, s_qkvT, quant=False)
        prep_slab(wproj, C, C, s_projT, quant=False)
        prep_slab(wg_in, IF, C, s_gateT, quant=True)
        prep_slab(wu_in, IF, C, s_upT, quant=True)
        prep_slab(wd_in, C, IF, s_downT, quant=True)
        prep_pe = S.val("pe")
        prep_sts = [(f"st{i}", S.val(f"st{i}")) for i in range(4)]
        prep_ss, prep_vv = S.val("ss"), S.val("vv")

        xnf = lambda kc: XNF[:, kc * 2048:(kc + 1) * 2048]
        xqtm = lambda tc: XQF[:, tc * 1024:(tc + 1) * 1024]
        xqf = xnf
        v_tm = WBIG[:, 0:4096]
        k_fm = WBIG[:, 4096:6144]
        kT = WBIG[:, 6144:8192]
        G_t = lambda rc: WBIG[:, rc * 1024:(rc + 1) * 1024]
        fW_g = lambda kc: WBIG[:, kc * 384:(kc + 1) * 384]
        fW_u = lambda kc: WBIG[:, 3072 + kc * 384:3072 + (kc + 1) * 384]
        fW_d = lambda sl: WBIG[:, 6144 + sl * 1024:6144 + (sl + 1) * 1024]
        wv_t = lambda kc: WBIG[:, 6144 + kc * 256:6144 + (kc + 1) * 256]
        wk_t = lambda kc: WBIG[:, 6144 + kc * 128:6144 + (kc + 1) * 128]
        FFN_P = [(0, 384), (384, 384), (768, 384), (1152, 384), (1536, 384),
                 (1920, 384), (2304, 384), (2688, 128)]
        NFP = len(FFN_P)

        pro_bufs = (wstr[:, 0:1024], tA[:, 0:1024], tB[:, 0:1024],
                    outst[:, 1024:2048])
        xin_t = [xin[:, 0:1024], xin[:, 1024:2048]]
        out_sl = [outst[:, 0:1024], outst[:, 1024:2048]]
        out_free = [[], []]

        for b in range(nbatch):
            if stage < 2:
                break
            if b == 0:
                xnf_guard = [("pe", prep_pe), ("ss", prep_ss),
                             ("vv", prep_vv)] + prep_sts
            else:
                xnf_guard = [("pe", S.val("pe")), ("ss", S.val("ss")),
                             ("vv", S.val("vv")), ("ga", S.val("ga")),
                             ("gb", S.val("gb"))]
            xin_free = [[], []]
            b_start_pe = S.val("pe")
            ctx_b_guard = [("vv", S.val("vv")), ("ss", S.val("ss"))]
            ev_tc = [0] * NTC
            wv_ready = 0
            for kcw in range(KC):
                w_ = []
                if kcw == 0:
                    w_ = [("pe", b_start_pe)]
                    if b == 0:
                        w_ += prep_sts
                wv_ready = ld_dma(
                    "lw", wv_t(kcw),
                    s_qkvT[kcw * 128:(kcw + 1) * 128, 2048:2048 + 256],
                    waits=w_)
            vps_ev = [("ss", 0), ("ss", 0)]
            for tc in range(NTC):
                sl = tc % 2
                xt = xin_t[sl]
                w0 = list(xnf_guard) if tc < 2 else xin_free[sl]
                ld_x = 0
                for qh in range(2):
                    ld_x = ld_dma(
                        f"xl{sl}", xt[:, qh * 512:(qh + 1) * 512],
                        x_in[b, tc * 128:(tc + 1) * 128, qh * 512:(qh + 1) * 512],
                        waits=w0 if qh == 0 else [])
                p8 = (tc % 2) * 8
                su, mn, vr, t1, ve = (
                    stat[:, p8:p8 + 1], stat[:, p8 + 1:p8 + 2],
                    stat[:, p8 + 2:p8 + 3], stat[:, p8 + 5:p8 + 6],
                    stat[:, p8 + 6:p8 + 7])
                rsd = rsdP[tc % 2][:]
                nb_ = nbP[tc % 2][:]
                s_op(lambda e, xt=xt, su=su, tc=tc: e.activation(
                    out=xqtm(tc), in_=xt, func=AF.Identity, accum_out=su),
                    waits=[(f"xl{sl}", ld_x)])
                ss_sq = s_op(lambda e, xt=xt, t1=t1, tc=tc: e.activation(
                    out=xqtm(tc), in_=xt, func=AF.Square, accum_out=t1),
                    waits=[(f"xl{sl}", ld_x)])
                v_op(lambda e, su=su, mn=mn: e.tensor_scalar(
                    out=mn, in0=su, scalar1=1.0 / C, scalar2=None, op0=ALU.mult),
                    waits=[("ss", ss_sq)], inc=False, drain=True)
                v_op(lambda e, vr=vr, t1=t1: e.tensor_scalar(
                    out=vr, in0=t1, scalar1=1.0 / C, scalar2=None, op0=ALU.mult),
                    waits=[("ss", ss_sq)], inc=False, drain=True)
                v_op(lambda e, mn=mn, t1=t1: e.tensor_scalar(
                    out=t1, in0=mn, scalar1=mn, scalar2=-1.0, op0=ALU.mult,
                    op1=ALU.mult), inc=False, drain=True)
                v_op(lambda e, vr=vr, t1=t1: e.tensor_tensor(
                    out=vr, in0=vr, in1=t1, op=ALU.add), inc=False, drain=True)
                vv_ve = v_op(lambda e, vr=vr, ve=ve: e.tensor_scalar(
                    out=ve, in0=vr, scalar1=1e-5, scalar2=None, op0=ALU.add),
                    drain=True)
                ss_sv = s_op(lambda e, ve=ve, t1=t1: e.activation(
                    out=t1, in_=ve, func=AF.Sqrt),
                    waits=[("vv", vv_ve)])
                v_op(lambda e, rsd=rsd, t1=t1: e.reciprocal(out=rsd, in_=t1),
                     waits=[("ss", ss_sv)], inc=False, drain=True)
                v_op(lambda e, rsd=rsd, t1=t1: e.tensor_tensor(
                    out=t1, in0=rsd, in1=rsd, op=ALU.mult), inc=False, drain=True)
                v_op(lambda e, ve=ve, t1=t1: e.tensor_tensor(
                    out=t1, in0=t1, in1=ve, op=ALU.mult), inc=False, drain=True)
                v_op(lambda e, t1=t1: e.tensor_scalar(
                    out=t1, in0=t1, scalar1=-0.5, scalar2=1.5, op0=ALU.mult,
                    op1=ALU.add), inc=False, drain=True)
                v_op(lambda e, rsd=rsd, t1=t1: e.tensor_tensor(
                    out=rsd, in0=rsd, in1=t1, op=ALU.mult), inc=False, drain=True)
                vv_nb = v_op(lambda e, mn=mn, rsd=rsd, nb_=nb_: e.tensor_scalar(
                    out=nb_, in0=mn, scalar1=rsd, scalar2=-1.0, op0=ALU.mult,
                    op1=ALU.mult), drain=True)
                ss_xn = s_op(lambda e, xt=xt, rsd=rsd, nb_=nb_: e.activation(
                    out=xt, in_=xt, func=AF.Identity, scale=rsd, bias=nb_),
                    waits=[("vv", vv_nb)])
                if debug and b == 0 and tc == 0:
                    (qd0, _) = emit_quant(xt[:, 0:512], xqtm(tc)[:, 0:512], 512,
                                          src_ready=[("ss", ss_xn)],
                                          bufs=pro_bufs)
                    dbg_st = 0
                    for dap, src_ in [(d_qu, qu[:]), (d_qn, qn[:]),
                                      (d_qsg, qsg[:]), (d_qsc, qsc[:, 0:96]),
                                      (d_xn0, xt)]:
                        dbg_st = st_dma("st1", dap, src_, waits=[("vv", qd0)])
                    (qd, qss) = emit_quant(xt[:, 512:1024], xqtm(tc)[:, 512:1024],
                                           512, src_ready=[("ss", ss_xn),
                                                           ("st1", dbg_st)],
                                           bufs=pro_bufs)
                else:
                    (qd, qss) = emit_quant_multi(xt, xqtm(tc), 1024,
                                                 src_ready=[("ss", ss_xn)],
                                                 bufs=pro_bufs, maxw=1024)
                last_tr = None
                for kc in range(KC):
                    w_ = [("ss", ss_xn)]
                    if kc == 0 and tc > 0:
                        w_.append(("ss", ev_tc[tc - 1]))
                    bankt = P[6 + kc // 4]
                    last_tr = S.add("tensor",
                                    lambda e, kc=kc, xt=xt, bankt=bankt:
                                    e.transpose(bankt[:, (kc % 4) * 128:
                                                      (kc % 4 + 1) * 128],
                                                xt[:, kc * 128:(kc + 1) * 128],
                                                ident[:]),
                                    waits=w_, inc="pe" if kc == KC - 1 else None)
                xin_free[sl] = [("pe", last_tr), ("vv", qd), ("ss", qss)]
                ev = 0
                for kc in range(KC):
                    ev = s_op(lambda e, kc=kc, tc=tc: e.copy(
                        out=xnf(kc)[:, tc * 128:(tc + 1) * 128],
                        in_=P[6 + kc // 4][:, (kc % 4) * 128:(kc % 4 + 1) * 128]),
                        waits=[("pe", last_tr)] if kc == 0 else [])
                ev_tc[tc] = ev
                bank = P[tc % 2]
                last = None
                for kc in range(KC):
                    w_ = [("lw", wv_ready)]
                    if kc == 0:
                        w_.append(("ss", ev_tc[tc]))
                        w_.append(vps_ev[tc % 2])
                    last = S.add("tensor", lambda e, bank=bank, kc=kc, tc=tc:
                                 e.matmul(bank[:, 0:256],
                                          xnf(kc)[:, tc * 128:(tc + 1) * 128],
                                          wv_t(kc),
                                          start=(kc == 0), stop=(kc == KC - 1)),
                                 waits=w_, inc="pe" if kc == KC - 1 else None)
                ev_v = s_op(lambda e, bank=bank, tc=tc: e.copy(
                    out=v_tm[:, tc * 256:(tc + 1) * 256], in_=bank[:, 0:256]),
                    waits=[("pe", last)])
                vps_ev[tc % 2] = ("ss", ev_v)
            prologue_ss = S.val("ss")
            prologue_vv = S.val("vv")

            if stage < 3:
                continue
            ctx_ev_prev = list(ctx_b_guard)
            for vg in range(4):
                vz = v_op(lambda e: e.memset(P[4][0:64, 128:512], 0.0),
                          waits=ctx_ev_prev)
                S.add("tensor", lambda e: e.matmul(
                    P[4][0:64, 0:128], ones1[0:1, 0:64], zrow[0:1, :],
                    start=True, stop=False),
                    waits=[("gg", gg_zrow), ("vv", vz)] + ctx_ev_prev,
                    inc="pe")
                if vg > 0:
                    wv_ready = 0
                    for kcw in range(KC):
                        wv_ready = ld_dma(
                            "lw", wv_t(kcw),
                            s_qkvT[kcw * 128:(kcw + 1) * 128,
                                   2048 + vg * 256:2048 + (vg + 1) * 256],
                            waits=[("pe", S.val("pe"))] if kcw == 0 else [])
                    vps_ev = [("ss", S.val("ss")), ("ss", S.val("ss"))]
                    for tc in range(NTC):
                        bank = P[tc % 2]
                        last = None
                        for kc in range(KC):
                            w_ = [("lw", wv_ready)]
                            if kc == 0:
                                w_.append(("ss", ev_tc[tc]))
                                w_.append(vps_ev[tc % 2])
                            last = S.add("tensor",
                                         lambda e, bank=bank, kc=kc, tc=tc:
                                         e.matmul(bank[:, 0:256],
                                                  xnf(kc)[:, tc * 128:(tc + 1) * 128],
                                                  wv_t(kc),
                                                  start=(kc == 0),
                                                  stop=(kc == KC - 1)),
                                         waits=w_,
                                         inc="pe" if kc == KC - 1 else None)
                        ev = s_op(lambda e, bank=bank, tc=tc: e.copy(
                            out=v_tm[:, tc * 256:(tc + 1) * 256],
                            in_=bank[:, 0:256]),
                            waits=[("pe", last)])
                        vps_ev[tc % 2] = ("ss", ev)
                    v_done_ss = S.val("ss")
                else:
                    v_done_ss = prologue_ss
                k_free_vv = 0
                for ks in range(2):
                    sc_idx = vg * 2 + ks
                    wk_ready = 0
                    for kcw in range(KC):
                        wk_ready = ld_dma(
                            "lw", wk_t(kcw),
                            s_qkvT[kcw * 128:(kcw + 1) * 128,
                                   1024 + sc_idx * 128:1024 + (sc_idx + 1) * 128],
                            waits=[("pe", S.val("pe"))] if kcw == 0 else [])
                    kps_ev = [("ss", S.val("ss")), ("ss", S.val("ss"))]
                    kexp_last = 0
                    for tq in range(4):
                        bank = P[2 + tq % 2]
                        last = None
                        for kc in range(KC):
                            w_ = [("lw", wk_ready)]
                            if kc == 0:
                                w_.append(kps_ev[tq % 2])
                            last = S.add("tensor", lambda e, bank=bank, kc=kc, tq=tq:
                                         e.matmul(bank[:],
                                                  wk_t(kc),
                                                  xnf(kc)[:, tq * 512:(tq + 1) * 512],
                                                  start=(kc == 0),
                                                  stop=(kc == KC - 1)),
                                         waits=w_, inc="pe" if kc == KC - 1 else None)
                        w_e = [("pe", last)]
                        kexp_last = s_op(
                            lambda e, bank=bank, tq=tq, sc_idx=sc_idx:
                            e.activation(
                                out=k_fm[:, tq * 512:(tq + 1) * 512], in_=bank[:],
                                func=AF.Exp,
                                accum_out=stat[:, 32 + sc_idx * 4 + tq:
                                               33 + sc_idx * 4 + tq]),
                            waits=w_e)
                        kps_ev[tq % 2] = ("ss", kexp_last)
                    kexp_ss = kexp_last
                    v_op(lambda e, sc_idx=sc_idx: e.reduce_sum(
                        out=rs_all[:, sc_idx:sc_idx + 1],
                        in_=stat[:, 32 + sc_idx * 4:36 + sc_idx * 4], axis=AX.X),
                        waits=[("ss", kexp_ss)], inc=False, drain=True)
                    k_free_vv = v_op(lambda e, sc_idx=sc_idx: e.reciprocal(
                        out=rs_all[:, sc_idx:sc_idx + 1],
                        in_=rs_all[:, sc_idx:sc_idx + 1]), drain=True)
                    tr_ev = ("ss", 0)
                    lastt = None
                    for g in range(4):
                        for j in range(4):
                            tcc = g * 4 + j
                            w_ = [("ss", kexp_ss)]
                            if j == 0:
                                w_.append(tr_ev)
                            lastt = S.add("tensor", lambda e, j=j, tcc=tcc:
                                          e.transpose(P[2][:, j * 128:(j + 1) * 128],
                                                      k_fm[:, tcc * 128:(tcc + 1) * 128],
                                                      ident[:]),
                                          waits=w_, inc="pe" if j == 3 else None)
                        evt = s_op(lambda e, g=g: e.copy(
                            out=kT[:, g * 512:(g + 1) * 512], in_=P[2][:]),
                            waits=[("pe", lastt)])
                        tr_ev = ("ss", evt)
                    kT_ss = S.val("ss")
                    for hl in range(2):
                        head_local = ks * 2 + hl
                        last = None
                        for tc in range(NTC):
                            w_ = []
                            if tc == 0:
                                w_ = [("ss", kT_ss), ("ss", v_done_ss),
                                      ("vv", vz)]
                            fin = (head_local == 3 and tc == NTC - 1)
                            last = S.add(
                                "tensor",
                                lambda e, tc=tc, head_local=head_local, fin=fin:
                                e.matmul(P[4][0:64, head_local * 128:
                                              head_local * 128 + 128],
                                         v_tm[:, tc * 256 + head_local * 64:
                                              tc * 256 + (head_local + 1) * 64],
                                         kT[:, tc * 128:(tc + 1) * 128],
                                         start=False, stop=fin),
                                waits=w_, inc="pe" if tc == NTC - 1 else None)
                        ctx_last_pe = last
                ev_c = 0
                for head_local in range(4):
                    hgl = vg * 4 + head_local
                    ev_c = s_op(lambda e, hgl=hgl, head_local=head_local: e.copy(
                        out=ctxsb[(hgl % 2) * 64:(hgl % 2) * 64 + 64,
                                  (hgl // 2) * 64:(hgl // 2) * 64 + 64],
                        in_=P[4][0:64, head_local * 128 + (head_local % 2) * 64:
                                 head_local * 128 + (head_local % 2) * 64 + 64]),
                        waits=[("pe", ctx_last_pe)] if head_local == 0 else [])
                ctx_ev_prev = [("ss", ev_c)]
            ctx_ss = S.val("ss")
            phase1_vv = S.val("vv")

            if stage < 4:
                continue
            gf_ev = ("ss", 0)
            g_last_pe = S.val("pe")
            for rc in range(8):
                for hh_ in range(2):
                    h_gl = rc * 2 + hh_
                    wp_ready = ld_dma(
                        "lw", wstr[hh_ * 64:hh_ * 64 + 64, 0:1024],
                        s_projT[h_gl * 64:(h_gl + 1) * 64, :],
                        waits=[("pe", g_last_pe)])
                    for half in range(2):
                        bank = P[4 + half]
                        w_ = [("lw", wp_ready), ("ss", ctx_ss)]
                        if hh_ == 0 and half == 0:
                            w_.append(gf_ev)
                        g_last_pe = S.add(
                            "tensor",
                            lambda e, bank=bank, hh_=hh_, h_gl=h_gl, half=half:
                            e.matmul(bank[hh_ * 64:(hh_ + 1) * 64, :],
                                     ctxsb[(h_gl % 2) * 64:(h_gl % 2) * 64 + 64,
                                           (h_gl // 2) * 64:(h_gl // 2) * 64 + 64],
                                     wstr[hh_ * 64:hh_ * 64 + 64,
                                          half * 512:half * 512 + 512],
                                     start=True, stop=True),
                            waits=w_,
                            inc="pe" if half == 1 else None)
                s_op(lambda e, rc=rc: e.activation(
                    out=G_t(rc)[:, 0:512], in_=P[4][:], func=AF.Identity,
                    scale=rs_all[:, rc:rc + 1], bias=zeros1[:]),
                    waits=[("pe", g_last_pe), ("vv", v_zero),
                           ("vv", phase1_vv)])
                ev2 = s_op(lambda e, rc=rc: e.activation(
                    out=G_t(rc)[:, 512:1024], in_=P[5][:], func=AF.Identity,
                    scale=rs_all[:, rc:rc + 1], bias=zeros1[:]))
                gf_ev = ("ss", ev2)
            g_ready_ss = S.val("ss")
            g_ready_pe = S.val("pe")
            if debug and b == 0:
                for name_ap, src in [(d_ctx, ctxsb[:]), (d_rs, rs_all[:]),
                                     (d_G, WBIG[:]), (d_kfm, k_fm),
                                     (d_xnf, xnf(0)), (d_xq, xqtm(0)),
                                     (d_stat, stat[:]),
                                     (d_rn[0], rsdP[0][:]),
                                     (d_rn[1], rsdP[1][:]),
                                     (d_rn[2], nbP[0][:]),
                                     (d_rn[3], nbP[1][:])]:
                    st_dma("st0", name_ap, src,
                           waits=[("ss", g_ready_ss), ("vv", S.val("vv"))])

            if stage < 5:
                continue
            wq_sl = [wstr[:, 0:512], wstr[:, 512:1024]]
            wq4 = [wstr[:, q * 256:(q + 1) * 256] for q in range(4)]
            wq_sems = ("lw", "lu", "lb", "st3")
            wq_free = [[("pe", g_ready_pe)] for _ in range(4)]
            wq_free[2] = [("pe", g_ready_pe), ("lb", ld_brow)]
            wq_free[3] = [("pe", g_ready_pe), prep_sts[3]]
            xp2_free = [list(xin_free[0]), list(xin_free[1])]
            qpsum_ev = ("ss", g_ready_ss)
            ap_ev = ("ss", g_ready_ss)
            tr_ev2 = ("ss", g_ready_ss)
            qtr_prev = 0
            for pair in range(8):
                qmm_last = None
                for kc in range(KC):
                    for half in range(2):
                        for q in range(2):
                            wsl = (kc * 4 + half * 2 + q) % 4
                            sname = wq_sems[wsl]
                            ldq = ld_dma(
                                sname, wq4[wsl],
                                s_qkvT[kc * 128:(kc + 1) * 128,
                                       half * 512 + q * 256:
                                       half * 512 + (q + 1) * 256],
                                waits=wq_free[wsl])
                            for ti in range(2):
                                tc = pair * 2 + ti
                                bank = P[ti * 2 + half]
                                w_ = [(sname, ldq)]
                                if kc == 0 and q == 0:
                                    w_.append(qpsum_ev)
                                qmm_last = S.add(
                                    "tensor",
                                    lambda e, bank=bank, kc=kc, tc=tc, q=q,
                                    wsl=wsl:
                                    e.matmul(bank[:, q * 256:(q + 1) * 256],
                                             xnf(kc)[:, tc * 128:(tc + 1) * 128],
                                             wq4[wsl][:],
                                             start=(kc == 0 and q == 0),
                                             stop=(kc == KC - 1 and q == 1)),
                                    waits=w_,
                                    inc="pe" if ti == 1 else None)
                            wq_free[wsl] = [("pe", qmm_last)]
                for ti in range(2):
                    tc = pair * 2 + ti
                    s_op(lambda e, ti=ti: e.activation(
                        out=tA[:, 0:512], in_=P[ti * 2][:], func=AF.Exp),
                        waits=[("pe", qmm_last), ("vv", S.val("vv"))])
                    ss_exp = s_op(lambda e, ti=ti: e.activation(
                        out=tA[:, 512:1024], in_=P[ti * 2 + 1][:], func=AF.Exp))
                    if ti == 1:
                        qpsum_ev = ("ss", ss_exp)
                    v_op(lambda e: e.reduce_sum(
                        out=stat[:, 16:32],
                        in_=tA[:].rearrange("p (h d) -> p h d", d=64), axis=AX.X),
                        waits=[("ss", ss_exp)], inc=False, drain=True)
                    v_op(lambda e: e.reciprocal(out=stat[:, 16:32],
                                                in_=stat[:, 16:32]), inc=False,
                         drain=True)
                    vv_qs = v_op(lambda e: e.tensor_tensor(
                        out=tB[:].rearrange("p (h d) -> p h d", d=64),
                        in0=tA[:].rearrange("p (h d) -> p h d", d=64),
                        in1=stat[:, 16:32].unsqueeze(2).broadcast_to([128, 16, 64]),
                        op=ALU.mult), waits=[("pe", qtr_prev)])
                    for g in range(2):
                        bankt = P[6 + g]
                        lastq = None
                        for j in range(4):
                            kc = g * 4 + j
                            w_ = [("vv", vv_qs)]
                            if j == 0:
                                w_.append(tr_ev2)
                            lastq = S.add("tensor",
                                          lambda e, bankt=bankt, j=j, kc=kc:
                                          e.transpose(bankt[:, j * 128:(j + 1) * 128],
                                                      tB[:, kc * 128:(kc + 1) * 128],
                                                      ident[:]),
                                          waits=w_, inc="pe" if j == 3 else None)
                        ss_t = s_op(lambda e, g=g: e.copy(
                            out=tA[:, g * 512:(g + 1) * 512], in_=P[6 + g][:]),
                            waits=[("pe", lastq)])
                        tr_ev2 = ("ss", ss_t)
                        qtr_prev = lastq
                    qsf_ss = S.val("ss")
                    last_ap = None
                    for half in range(2):
                        bank = P[4 + half]
                        for kc in range(KC):
                            w_ = [("ss", qsf_ss)]
                            if kc == 0 and half == 0:
                                w_.append(ap_ev)
                            last_ap = S.add(
                                "tensor", lambda e, bank=bank, kc=kc, half=half:
                                e.matmul(bank[:], tA[:, kc * 128:(kc + 1) * 128],
                                         G_t(kc)[:, half * 512:(half + 1) * 512],
                                         start=(kc == 0), stop=False),
                                waits=w_, inc=None)
                        if "bias" in skip:
                            last_ap = S.add(
                                "tensor", lambda e, bank=bank, kc=KC - 1, half=half:
                                e.matmul(bank[:], tA[:, (KC - 1) * 128:KC * 128],
                                         G_t(KC - 1)[:, half * 512:(half + 1) * 512],
                                         start=False, stop=True),
                                waits=[], inc="pe")
                        else:
                            last_ap = S.add(
                                "tensor", lambda e, bank=bank, half=half:
                                e.matmul(bank[:], ones1[:],
                                         brow[0:1, half * 512:(half + 1) * 512],
                                         start=False, stop=True),
                                waits=[("lb", ld_brow), ("gg", gg_ones)], inc="pe")
                    osl = ti
                    ld_x2 = 0
                    for qh in range(2):
                        ld_x2 = ld_dma(
                            f"xl{ti}", xin_t[ti][:, qh * 512:(qh + 1) * 512],
                            x_in[b, tc * 128:(tc + 1) * 128,
                                 qh * 512:(qh + 1) * 512],
                            waits=xp2_free[ti] if qh == 0 else [])
                    v_op(lambda e, osl=osl, ti=ti: e.tensor_tensor(
                        out=out_sl[osl][:, 0:512], in0=P[4][:],
                        in1=xin_t[ti][:, 0:512], op=ALU.add),
                        waits=[("pe", last_ap), (f"xl{ti}", ld_x2)] + out_free[osl],
                        inc=False)
                    vv_o = v_op(lambda e, osl=osl, ti=ti: e.tensor_tensor(
                        out=out_sl[osl][:, 512:1024], in0=P[5][:],
                        in1=xin_t[ti][:, 512:1024], op=ALU.add))
                    ap_ev = ("vv", vv_o)
                    xp2_free[ti] = [("vv", vv_o)]
                    st_w = st_dma(f"so{osl}", out[b, tc * 128:(tc + 1) * 128, :],
                                  out_sl[osl], waits=[("vv", vv_o)])
                    out_free[osl] = [(f"so{osl}", st_w)]
            resid_gx = S.val("gx")

            if stage < 6:
                continue
            xq_guard = [("pe", S.val("pe")), ("ss", S.val("ss")),
                        ("vv", S.val("vv"))]
            prev_ev = 0
            last_tr = None
            for tc in range(NTC):
                for kc in range(KC):
                    w_ = [("ss", prologue_ss), ("vv", prologue_vv)]
                    if tc == 0 and kc == 0:
                        w_ += xq_guard
                    if tc % 4 == 0 and kc == 0:
                        w_.append(("ss", prev_ev))
                    last_tr = S.add("tensor", lambda e, kc=kc, tc=tc:
                                    e.transpose(P[kc][:, (tc % 4) * 128:(tc % 4 + 1) * 128],
                                                xqtm(tc)[:, kc * 128:(kc + 1) * 128],
                                                ident[:]),
                                    waits=w_, inc="pe" if kc == KC - 1 else None)
                if tc % 4 == 3:
                    ev = 0
                    for kc in range(KC):
                        ev = s_op(lambda e, kc=kc, tc=tc: e.copy(
                            out=xqf(kc)[:, (tc - 3) * 128:(tc + 1) * 128],
                            in_=P[kc][:, 0:512]),
                            waits=[("pe", last_tr)])
                    prev_ev = ev
            xq_fm_ss = S.val("ss")

            g_free = [("pe", S.val("pe")), ("ss", xq_fm_ss)]
            u_free = [("pe", S.val("pe"))]
            dpsum_ev = ("vv", S.val("vv"))
            gate_ev = ("ss", S.val("ss"))
            up_ev = ("vv", S.val("vv"))
            hq_free = [("pe", S.val("pe")), ("pe", S.val("pe"))]
            hqT_busy = [[], []]
            ht_ev = [S.val("ss"), S.val("ss")]
            fwd_free = [[("pe", S.val("pe"))], [("pe", S.val("pe"))]]
            hqt2 = [hqt[:, 0:384], tA[:, 0:384]]
            hqT_sl = [hqT[:, 0:384], tB[:, 0:384]]
            for pair in range(8):
                qd_p = [None, None]
                tr_p = [None, None]

                def emit_downstage(dp, wdt_d, i0_d):
                    nonlocal fwd_free, hqT_busy
                    nj = wdt_d // 128
                    ht_last = 0
                    for ti in range(2):
                        ht_ev[ti] = s_op(lambda e, ti=ti, wdt_d=wdt_d: e.copy(
                            out=hqT_sl[ti][:, 0:wdt_d], in_=P[6 + ti][:, 0:wdt_d]),
                            waits=[("pe", tr_p[ti])] + hqT_busy[ti])
                        ht_last = ht_ev[ti]
                    dld = {}
                    for j in range(min(nj, 2)):
                        dld[j] = ld_dma_g("ld2", fW_d(j % 2),
                                          s_downT[i0_d + j * 128:i0_d + (j + 1) * 128, :],
                                          waits=fwd_free[j % 2])
                    if nj >= 2:
                        dld[0] = dld[1]
                    dmm = {}
                    last_d = None
                    for j in range(nj):
                        dsl = j % 2
                        if j >= 2:
                            dld[j] = ld_dma_g("ld2", fW_d(dsl),
                                              s_downT[i0_d + j * 128:i0_d + (j + 1) * 128, :],
                                              waits=[("pe", dmm[j - 2])])
                        for ti in range(2):
                            for half in range(2):
                                bank = P[2 + ti * 2 + half]
                                last_d = S.add(
                                    "tensor",
                                    lambda e, j=j, bank=bank, half=half, dp=dp,
                                    dsl=dsl, nj=nj, ti=ti, wdt_d=wdt_d:
                                    e.matmul(bank[:],
                                             hqT_sl[ti][:, j * 128:(j + 1) * 128],
                                             fW_d(dsl)[:, half * 512:(half + 1) * 512],
                                             start=(dp == 0 and j == 0),
                                             stop=(dp == NFP - 1 and j == nj - 1)),
                                    waits=[("ss", ht_last), ("ld2", dld[j])],
                                    inc="pe" if (ti == 1 and half == 1) else None)
                        dmm[j] = last_d
                        fwd_free[dsl] = [("pe", last_d)]
                    hqT_busy = [[("pe", last_d)], [("pe", last_d)]]
                    return last_d

                for pi, (i0, wdt) in enumerate(FFN_P):
                    wld_g = 0
                    for kc in range(KC):
                        wld_g = ld_dma("lw", fW_g(kc)[:, 0:wdt],
                                       s_gateT[kc * 128:(kc + 1) * 128, i0:i0 + wdt],
                                       waits=g_free if kc == 0 else [])
                    wld_u = 0
                    for kc in range(KC):
                        wld_u = ld_dma("lu", fW_u(kc)[:, 0:wdt],
                                       s_upT[kc * 128:(kc + 1) * 128, i0:i0 + wdt],
                                       waits=u_free if kc == 0 else [])
                    lastu_all = None
                    lastg_all = None
                    for ti in range(2):
                        tc = pair * 2 + ti
                        lastg = lastu = None
                        for kc in range(KC):
                            w_ = [("lw", wld_g), ("ss", xq_fm_ss)]
                            if kc == 0:
                                w_.append(gate_ev)
                                if pi == 0 and ti == 0:
                                    w_.append(dpsum_ev)
                            lastg = S.add("tensor", lambda e, kc=kc, tc=tc, wdt=wdt:
                                          e.matmul(P[0][:, 0:wdt],
                                                   xqf(kc)[:, tc * 128:(tc + 1) * 128],
                                                   fW_g(kc)[:, 0:wdt],
                                                   start=(kc == 0), stop=(kc == KC - 1)),
                                          waits=w_, inc="pe" if kc == KC - 1 else None)
                        ss_g = s_op(lambda e, wdt=wdt: e.activation(
                            out=hg[:, 0:wdt], in_=P[0][:, 0:wdt], func=AF.Silu),
                            waits=[("pe", lastg)])
                        gate_ev = ("ss", ss_g)
                        if pi > 0 and qd_p[ti] is not None:
                            wdt_prev = FFN_P[pi - 1][1]
                            lastt = None
                            for j in range(wdt_prev // 128):
                                w_ = [("vv", qd_p[ti])]
                                if j == 0:
                                    w_.append(("ss", ht_ev[ti]))
                                lastt = S.add(
                                    "tensor", lambda e, j=j, ti=ti:
                                    e.transpose(P[6 + ti][:, j * 128:(j + 1) * 128],
                                                hqt2[ti][:, j * 128:(j + 1) * 128],
                                                ident[:]),
                                    waits=w_,
                                    inc="pe" if j == wdt_prev // 128 - 1 else None)
                            tr_p[ti] = lastt
                            hq_free[ti] = ("pe", lastt)
                        for kc in range(KC):
                            w_ = [("lu", wld_u), up_ev] if kc == 0 else []
                            lastu = S.add("tensor", lambda e, kc=kc, tc=tc, wdt=wdt:
                                          e.matmul(P[1][:, 0:wdt],
                                                   xqf(kc)[:, tc * 128:(tc + 1) * 128],
                                                   fW_u(kc)[:, 0:wdt],
                                                   start=(kc == 0), stop=(kc == KC - 1)),
                                          waits=w_, inc="pe" if kc == KC - 1 else None)
                        lastu_all = lastu
                        lastg_all = lastg
                        vv_h = v_op(lambda e, wdt=wdt: e.tensor_tensor(
                            out=hh[:, 0:wdt], in0=hg[:, 0:wdt], in1=P[1][:, 0:wdt],
                            op=ALU.mult), waits=[("ss", ss_g), ("pe", lastu)])
                        up_ev = ("vv", vv_h)
                        (qd, _) = emit_quant(hh[:, 0:wdt], hqt2[ti][:, 0:wdt], wdt,
                                             src_ready=[("vv", vv_h)],
                                             dst_free=[hq_free[ti]])
                        qd_prev_ti = qd_p[ti]
                        qd_p[ti] = qd
                    g_free = [("pe", lastg_all)]
                    u_free = [("pe", lastu_all)]
                    if pi > 0:
                        emit_downstage(pi - 1, FFN_P[pi - 1][1], FFN_P[pi - 1][0])
                wdt_l, i0_l = FFN_P[NFP - 1][1], FFN_P[NFP - 1][0]
                for ti in range(2):
                    lastt = None
                    for j in range(wdt_l // 128):
                        w_ = [("vv", qd_p[ti])]
                        lastt = S.add("tensor", lambda e, j=j, ti=ti:
                                      e.transpose(P[6 + ti][:, j * 128:(j + 1) * 128],
                                                  hqt2[ti][:, j * 128:(j + 1) * 128],
                                                  ident[:]),
                                      waits=w_,
                                      inc="pe" if j == wdt_l // 128 - 1 else None)
                    tr_p[ti] = lastt
                    hq_free[ti] = ("pe", lastt)
                emit_downstage(NFP - 1, wdt_l, i0_l)
                dd = S.val("vv")
                for ti in range(2):
                    tc = pair * 2 + ti
                    osl = ti
                    v_op(lambda e, osl=osl, ti=ti: e.tensor_copy(
                        out=out_sl[osl][:, 0:512], in_=P[2 + ti * 2][:]),
                        waits=[("pe", S.val("pe"))] + out_free[osl], inc=False)
                    vv_e = v_op(lambda e, osl=osl, ti=ti: e.tensor_copy(
                        out=out_sl[osl][:, 512:1024], in_=P[2 + ti * 2 + 1][:]))
                    gsem = "ga" if osl == 0 else "gb"
                    gda = gd_accum(gsem, out[b, tc * 128:(tc + 1) * 128, :],
                                   out_sl[osl],
                                   waits=[("vv", vv_e), ("gx", resid_gx),
                                          (f"so{osl}", S.val(f"so{osl}"))])
                    out_free[osl] = [(gsem, gda)]
                    dd = vv_e
                dpsum_ev = ("vv", dd)

        for sname in ("gx", "ga", "gb"):
            S.add("gpsimd", lambda e, sname=sname: e.wait_ge(sem[sname], S.cnt[sname]))
        for sname in ("so0", "so1", "gx", "ga", "gb", "st0", "st1", "st2", "st3"):
            S.add("sync", lambda e, sname=sname: e.wait_ge(sem[sname], S.cnt[sname]))

        def run_items(eng_obj, name):
            for waits, fn, inc, inc_by, drain_after in S.items[name]:
                for (s_name, val) in waits:
                    if val:
                        eng_obj.wait_ge(sem[s_name], val)
                inst = fn(eng_obj) if fn is not None else None
                if inc is not None and inst is not None:
                    inst.then_inc(sem[inc], inc_by)
                if drain_after:
                    eng_obj.drain()

        @block.sync
        def _(e):
            run_items(e, "sync")

        @block.gpsimd
        def _(e):
            run_items(e, "gpsimd")

        @block.scalar
        def _(e):
            run_items(e, "scalar")

        @block.vector
        def _(e):
            run_items(e, "vector")

        @block.tensor
        def _(e):
            run_items(e, "tensor")

    return nc


def kernel(L0_attn_x, L0_attn_w_qkv, L0_attn_w_proj, L0_attn_b_proj,
           L0_ffn_w_gate, L0_ffn_w_up, L0_ffn_w_down):
    x = np.ascontiguousarray(np.asarray(L0_attn_x, dtype=np.float32))
    wqkv = np.ascontiguousarray(np.asarray(L0_attn_w_qkv, dtype=np.float32))
    wproj = np.ascontiguousarray(np.asarray(L0_attn_w_proj, dtype=np.float32))
    bproj = np.ascontiguousarray(np.asarray(L0_attn_b_proj, dtype=np.float32))
    wg = np.ascontiguousarray(np.asarray(L0_ffn_w_gate, dtype=np.float32))
    wu = np.ascontiguousarray(np.asarray(L0_ffn_w_up, dtype=np.float32))
    wd = np.ascontiguousarray(np.asarray(L0_ffn_w_down, dtype=np.float32))

    nc = build_core_program()
    in_maps = [{"x": x[2 * c:2 * c + 2], "wqkv": wqkv, "wproj": wproj,
                "bproj": bproj, "wg": wg, "wu": wu, "wd": wd}
               for c in range(NCORES)]
    res = run_bass_kernel_spmd(nc, in_maps, core_ids=list(range(NCORES)))
    return np.concatenate([r["out"] for r in res.results], axis=0)
